# version 07
# speedup vs baseline: 1.2805x; 1.0290x over previous
import math
from contextlib import ExitStack

import numpy as np
import ml_dtypes
import concourse.bass as bass
import concourse.mybir as mybir
from concourse.bass_utils import run_bass_kernel_spmd

F32 = mybir.dt.float32
BF16 = mybir.dt.bfloat16
AF = mybir.ActivationFunctionType
ALU = mybir.AluOpType
BF = ml_dtypes.bfloat16

N_CORES = 8
DEBUG_L0 = False
D = 1024
T = 4096
LN_EPS = 1e-5

ENGINES = ("sync", "gpsimd", "tensor", "vector", "scalar")


class Emit:

    MAXQ = 4

    def __init__(self, nc, block, sems):
        self.nc = nc
        self.block = block
        self.sem = sems
        self.q = {e: [] for e in ENGINES}
        self.cnt = {s: 0 for s in sems}
        self.issuer_drain = {}
        self.consumer_wait = {}

    def wait(self, eng, sem, val=None):
        if val is None:
            val = self.cnt[sem]
        s = self.sem[sem]
        self.q[eng].append(lambda h, s=s, val=val: h.wait_ge(s, val))
        self.consumer_wait[sem] = max(self.consumer_wait.get(sem, 0), val)
        return val

    def op(self, eng, meth, *args, sem=None, inc=1, **kw):
        s = self.sem[sem] if sem else None

        def f(h, meth=meth, args=args, kw=kw, s=s, inc=inc):
            r = getattr(h, meth)(*args, **kw)
            if s is not None:
                r.then_inc(s, inc)
        self.q[eng].append(f)
        if sem:
            self.cnt[sem] += inc
            return self.cnt[sem]
        return None

    def dma(self, eng, out, in_, sem="LD"):
        base = self.issuer_drain.get(sem, 0)
        if (self.consumer_wait.get(sem, 0) > base or
                self.cnt[sem] + 16 - base > 16 * self.MAXQ):
            v = self.wait(eng, sem)
            self.issuer_drain[sem] = v
        return self.op(eng, "dma_start", sem=sem, inc=16, out=out, in_=in_)

    def flush(self):
        for eng in ENGINES:
            lst = self.q[eng]
            if not lst:
                continue
            self.q[eng] = []

            def body(h, lst=lst):
                for f in lst:
                    f(h)
            getattr(self.block, eng)(body)

    def barrier(self, engines=ENGINES):
        for e in engines:
            if e in ("vector", "gpsimd"):
                self.op(e, "engine_nop", sem="BAR", inc=1)
            else:
                self.op(e, "nop", sem="BAR", inc=1)
        tgt = self.cnt["BAR"]
        for e in engines:
            self.wait(e, "BAR", tgt)


def std_sems(nc, ctx):
    sems = {}
    for s in ("LD", "LD2", "ST", "TE", "VE", "SE", "GE", "BAR"):
        sems[s] = ctx.enter_context(nc.semaphore(s))
    return sems


def register_fconsts(E, ctx, nc, values):
    values = [float(v) for v in values]
    if not values:
        return
    cst = ctx.enter_context(nc.sbuf_tensor("c_fconsts", [1, len(values)], F32))
    for i, v in enumerate(values):
        E.op("vector", "memset", cst[0:1, i:i + 1], v)
        nc.const_aps.register(F32, v, cst[0:1, i:i + 1])
    E.op("vector", "engine_nop", sem="VE")


def emit_consts(E, ctx, nc):
    ones_col = ctx.enter_context(nc.sbuf_tensor("c_ones_col", [128, 1], F32))
    ones_colb = ctx.enter_context(nc.sbuf_tensor("c_ones_colb", [128, 1], BF16))
    ones_row = ctx.enter_context(nc.sbuf_tensor("c_ones_row", [1, 128], F32))
    ones_rowb = ctx.enter_context(nc.sbuf_tensor("c_ones_rowb", [1, 128], BF16))
    ident = ctx.enter_context(nc.sbuf_tensor("c_ident", [128, 128], BF16))
    identf = ctx.enter_context(nc.sbuf_tensor("c_identf", [128, 128], F32))
    idsc = ctx.enter_context(nc.sbuf_tensor("c_idsc", [128, 128], F32))

    E.op("gpsimd", "iota", idsc[:], [[1, 128]], sem="GE", base=0,
         channel_multiplier=-1, allow_small_or_imprecise_dtypes=True)
    E.wait("vector", "GE")
    E.op("vector", "memset", ones_col[:], 1.0)
    E.op("vector", "memset", ones_colb[:], 1.0)
    E.op("vector", "memset", ones_row[:], 1.0)
    E.op("vector", "memset", ones_rowb[:], 1.0)
    E.op("vector", "tensor_scalar", identf[:], idsc[:], 0.0, 1.0,
         op0=ALU.is_equal, op1=ALU.mult, sem="VE")
    E.wait("vector", "VE")
    E.op("vector", "tensor_copy", ident[:], identf[:], sem="VE")
    return dict(ones_col=ones_col, ones_colb=ones_colb, ones_row=ones_row,
                ones_rowb=ones_rowb, ident=ident, identf=identf)


def emit_ln_fm(E, C, x_tiles, sq_tiles, psA, psB, psb, strow, dim, eps=LN_EPS,
               ones=None):
    nd = len(x_tiles)
    W = x_tiles[0].shape[-1]
    if ones is None:
        ones = C["ones_col"]
    for d in range(nd):
        E.op("scalar", "activation", sq_tiles[d], x_tiles[d], AF.Square,
             sem="SE" if d == nd - 1 else None)
    se = E.cnt["SE"]
    E.wait("tensor", "SE")
    E.wait("tensor", "VE")
    for d in range(nd):
        E.op("tensor", "matmul", psA[:], ones[:, 0:1], x_tiles[d],
             start=(d == 0), stop=(d == nd - 1))
    E.wait("tensor", "SE", se)
    for d in range(nd):
        E.op("tensor", "matmul", psB[:], C["ones_col"][:, 0:1], sq_tiles[d],
             start=(d == 0), stop=(d == nd - 1),
             sem="TE" if d == nd - 1 else None)
    te = E.cnt["TE"]
    E.wait("scalar", "TE", te)
    E.op("scalar", "mul", strow[:, :, 0], psA[:], 1.0 / dim)
    E.op("scalar", "mul", strow[:, :, 1], psB[:], 1.0 / dim, sem="SE")
    E.wait("scalar", "SE")
    E.op("scalar", "activation", strow[:, :, 2], strow[:, :, 0], AF.Square,
         sem="SE")
    E.wait("vector", "SE")
    E.op("vector", "tensor_sub", strow[:, :, 3], strow[:, :, 1], strow[:, :, 2],
         sem="VE")
    E.wait("scalar", "VE")
    E.op("scalar", "activation", strow[:, :, 3], strow[:, :, 3], AF.Ln,
         bias=eps, sem="SE")
    E.wait("scalar", "SE")
    E.op("scalar", "activation", strow[:, :, 3], strow[:, :, 3], AF.Exp,
         scale=-0.5, sem="SE")
    E.wait("tensor", "SE")
    E.op("tensor", "matmul", psb[:, 0:W], C["ones_row"][0:1, :], strow[:, :, 0],
         start=True, stop=True)
    E.op("tensor", "matmul", psb[:, W:2 * W], C["ones_row"][0:1, :],
         strow[:, :, 3], start=True, stop=True, sem="TE")
    return E.cnt["TE"]



def build_l0():
    nc = bass.Bass("TRN2", target_bir_lowering=False)
    xfm = nc.dram_tensor("xfm", [D, T], F32, kind="ExternalInput").ap()
    w_in = nc.dram_tensor("w_in", [D, 4224], F32, kind="ExternalInput").ap()
    b_in_row = nc.dram_tensor("b_in_row", [1, 4224], F32, kind="ExternalInput").ap()
    b_in_t = nc.dram_tensor("b_in_t", [33, 128, 1], F32, kind="ExternalInput").ap()
    ln_g_t = nc.dram_tensor("ln_g_t", [8, 128, 1], F32, kind="ExternalInput").ap()
    ln_b_t = nc.dram_tensor("ln_b_t", [8, 128, 1], F32, kind="ExternalInput").ap()
    qkvec = nc.dram_tensor("qkvec", [4, 128, 1], F32, kind="ExternalInput").ap()
    w_out = nc.dram_tensor("w_out", [2048, D], F32, kind="ExternalInput").ap()
    b_out_t = nc.dram_tensor("b_out_t", [8, 128, 1], F32, kind="ExternalInput").ap()
    router_wT = nc.dram_tensor("router_wT", [D, 16], F32, kind="ExternalInput").ap()
    router_b_t = nc.dram_tensor("router_b_t", [1, 128, 1], F32, kind="ExternalInput").ap()
    gate_p = nc.dram_tensor("gate_p", [16, D, 512], F32, kind="ExternalInput").ap()
    up_p = nc.dram_tensor("up_p", [16, D, 512], F32, kind="ExternalInput").ap()
    down_p = nc.dram_tensor("down_p", [16, 512, D], F32, kind="ExternalInput").ap()
    shT_gate = nc.dram_tensor("shT_gate", [D, 1024], F32, kind="ExternalInput").ap()
    shT_up = nc.dram_tensor("shT_up", [D, 1024], F32, kind="ExternalInput").ap()
    shT_down = nc.dram_tensor("shT_down", [1024, D], F32, kind="ExternalInput").ap()

    x1fm = nc.dram_tensor("x1fm", [D, 2048], F32, kind="ExternalOutput").ap()

    if DEBUG_L0:
        dbg_xn = nc.dram_tensor("dbg_xn", [D, 2048], BF16, kind="ExternalOutput").ap()
        dbg_v = nc.dram_tensor("dbg_v", [T, 2048], BF16, kind="ExternalOutput").ap()
        dbg_u = nc.dram_tensor("dbg_u", [2048, 2048], BF16, kind="ExternalOutput").ap()
        dbg_att = nc.dram_tensor("dbg_att", [D, 2048], F32, kind="ExternalOutput").ap()
        dbg_pvu = nc.dram_tensor("dbg_pvu", [2048, 2048], BF16, kind="ExternalOutput").ap()
        dbg_acc = nc.dram_tensor("dbg_acc", [D, 2048], F32, kind="ExternalOutput").ap()
        dbg_cfm = nc.dram_tensor("dbg_cfm", [16, 2048], F32, kind="ExternalOutput").ap()
        dbg_ltm = nc.dram_tensor("dbg_ltm", [128, 16, 16], F32, kind="ExternalOutput").ap()
        dbg_t8 = nc.dram_tensor("dbg_t8", [128, 16, 8], F32, kind="ExternalOutput").ap()
        dbg_lgf = nc.dram_tensor("dbg_lgf", [16, 2048], F32, kind="ExternalOutput").ap()
        dbg_sml = nc.dram_tensor("dbg_sml", [128, 16, 5], F32, kind="ExternalOutput").ap()
        dbg_ctm = nc.dram_tensor("dbg_ctm", [128, 16, 16], F32, kind="ExternalOutput").ap()
    v_dram = nc.dram_tensor("v_dram", [T, 2048], BF16, kind="Internal").ap()
    u_dram = nc.dram_tensor("u_dram", [2048, 2048], BF16, kind="Internal").ap()
    xn_dram = nc.dram_tensor("xn_dram", [D, 2048], BF16, kind="Internal").ap()
    xn32_dram = nc.dram_tensor("xn32_dram", [D, 2048], F32, kind="Internal").ap()
    pvu_dram = nc.dram_tensor("pvu_dram", [2048, 2048], BF16, kind="Internal").ap()
    att_dram = nc.dram_tensor("att_dram", [D, 2048], F32, kind="Internal").ap()

    with ExitStack() as ctx:
        sems = std_sems(nc, ctx)
        block = ctx.enter_context(nc.Block())
        E = Emit(nc, block, sems)
        C = emit_consts(E, ctx, nc)
        register_fconsts(E, ctx, nc,
                         [LN_EPS, -0.5, 1.0 / D, 1.0 / T])

        lng = ctx.enter_context(nc.sbuf_tensor("lng", [128, 8], F32))
        lnb = ctx.enter_context(nc.sbuf_tensor("lnb", [128, 8], F32))
        bint = ctx.enter_context(nc.sbuf_tensor("bint", [128, 33], F32))
        qkv4 = ctx.enter_context(nc.sbuf_tensor("qkv4", [128, 4], F32))
        boutt = ctx.enter_context(nc.sbuf_tensor("boutt", [128, 8], F32))
        rbt = ctx.enter_context(nc.sbuf_tensor("rbt", [128, 1], F32))
        qf = ctx.enter_context(nc.sbuf_tensor("qf", [128, 2048], BF16))
        kf = ctx.enter_context(nc.sbuf_tensor("kf", [128, T], BF16))

        for d in range(8):
            E.dma("sync", lng[:, d:d + 1], ln_g_t[d])
            E.dma("sync", lnb[:, d:d + 1], ln_b_t[d])
            E.dma("sync", boutt[:, d:d + 1], b_out_t[d])
        for t in range(33):
            E.dma("sync", bint[:, t:t + 1], b_in_t[t])
        for i in range(4):
            E.dma("sync", qkv4[:, i:i + 1], qkvec[i])
        E.dma("sync", rbt[:], router_b_t[0])
        E.barrier()
        E.flush()

        with (
            nc.sbuf_tensor("xln", [128, 8, T], BF16) as xln,
        ):
            with (
                nc.sbuf_tensor("xt", [128, 8, 512], F32) as xt,
                nc.sbuf_tensor("sq", [128, 8, 512], F32) as sq,
                nc.sbuf_tensor("xnf", [128, 8, 512], F32) as xnf,
                nc.sbuf_tensor("xnb", [128, 8, 512], BF16) as xnb,
                nc.sbuf_tensor("strow", [1, 512, 4], F32) as strow,
                nc.psum_tensor("psA", [1, 512], F32) as psA,
                nc.psum_tensor("psB", [1, 512], F32) as psB,
                nc.psum_tensor("psb", [128, 1024], F32) as psb,
            ):
                for j in range(T // 512):
                    own = j < 4
                    for d in range(8):
                        E.dma("sync", xt[:, d, :],
                              xfm[d * 128:(d + 1) * 128, j * 512:(j + 1) * 512])
                    E.wait("scalar", "LD")
                    E.wait("tensor", "LD")
                    xts = [xt[:, d, :] for d in range(8)]
                    sqs = [sq[:, d, :] for d in range(8)]
                    te = emit_ln_fm(E, C, xts, sqs, psA, psB, psb, strow, D)
                    E.wait("vector", "TE", te)
                    for d in range(8):
                        E.op("vector", "tensor_sub", xnf[:, d, :], xt[:, d, :],
                             psb[:, 0:512])
                        E.op("vector", "tensor_mul", xnf[:, d, :], xnf[:, d, :],
                             psb[:, 512:1024],
                             sem="VE" if d == 7 else None)
                    ve = E.cnt["VE"]
                    if own:
                        for d in range(8):
                            E.op("vector", "tensor_copy", xnb[:, d, :],
                                 xnf[:, d, :], sem="VE" if d == 7 else None)
                        E.wait("sync", "VE")
                        for d in range(8):
                            E.dma("sync", xn_dram[d * 128:(d + 1) * 128,
                                                  j * 512:(j + 1) * 512],
                                  xnb[:, d, :], sem="ST")
                            E.dma("sync", xn32_dram[d * 128:(d + 1) * 128,
                                                    j * 512:(j + 1) * 512],
                                  xnf[:, d, :], sem="ST")
                    E.wait("scalar", "VE", ve)
                    E.wait("tensor", "VE", ve)
                    xnts = [xnf[:, d, :] for d in range(8)]
                    te = emit_ln_fm(E, C, xnts, sqs, psA, psB, psb, strow, D)
                    E.wait("vector", "TE", te)
                    for d in range(8):
                        E.op("vector", "tensor_sub", sq[:, d, :], xnf[:, d, :],
                             psb[:, 0:512])
                        E.op("vector", "tensor_mul", sq[:, d, :], sq[:, d, :],
                             psb[:, 512:1024])
                        E.op("vector", "tensor_scalar",
                             xln[:, d, j * 512:(j + 1) * 512],
                             sq[:, d, :], lng[:, d:d + 1], lnb[:, d:d + 1],
                             op0=ALU.mult, op1=ALU.add,
                             sem="VE" if d == 7 else None)
                    E.wait("sync", "VE")
                    E.wait("sync", "SE")
                E.wait("sync", "ST")
                E.barrier()
                E.flush()

            with (
                nc.sbuf_tensor("wsb", [128, 8, 4224], BF16) as wsb,
                nc.sbuf_tensor("binrow", [1, 4224], F32) as binrow,
                nc.sbuf_tensor("zf", [128, T], BF16) as zf,
                nc.sbuf_tensor("ev1", [128, 512], F32) as ev1,
                nc.sbuf_tensor("ub", [128, 512], BF16) as ub,
                nc.sbuf_tensor("bvb", [128, 2048], F32) as bvb,
                nc.psum_tensor("psm", [128, 2048], F32) as psm,
                nc.psum_tensor("psv", [128, 2048], F32) as psv,
            ):
                E.dma("sync", binrow[:], b_in_row[:])
                for d in range(8):
                    E.dma("gpsimd", wsb[:, d, :], w_in[d * 128:(d + 1) * 128, :],
                          sem="LD2")
                E.wait("tensor", "LD2")

                for f in range(16):
                    for jc in range(4):
                        sl = psm[:, (f % 4) * 512:(f % 4) * 512 + 512]
                        for d in range(8):
                            E.op("tensor", "matmul", sl,
                                 wsb[:, d, f * 128:(f + 1) * 128],
                                 xln[:, d, jc * 512:(jc + 1) * 512],
                                 start=(d == 0), stop=(d == 7),
                                 sem="TE" if d == 7 else None)
                        E.wait("scalar", "TE")
                        E.op("scalar", "activation", ub[:], sl, AF.Silu,
                             bias=bint[:, f:f + 1], sem="SE")
                        E.wait("sync", "SE")
                        E.dma("sync", u_dram[f * 128:(f + 1) * 128,
                                             jc * 512:(jc + 1) * 512],
                              ub[:], sem="ST")
                        E.wait("tensor", "SE")
                        E.wait("scalar", "ST")
                for jc in range(8):
                    sl = psm[:, (jc % 4) * 512:(jc % 4) * 512 + 512]
                    for d in range(8):
                        E.op("tensor", "matmul", sl,
                             wsb[:, d, 4096:4224],
                             xln[:, d, jc * 512:(jc + 1) * 512],
                             start=(d == 0), stop=(d == 7),
                             sem="TE" if d == 7 else None)
                    E.wait("scalar", "TE")
                    E.op("scalar", "activation", zf[:, jc * 512:(jc + 1) * 512],
                         sl, AF.Silu, bias=bint[:, 32:33], sem="SE")
                    E.wait("tensor", "SE")
                E.wait("vector", "SE")
                E.op("vector", "tensor_scalar", qf[:], zf[:, 0:2048],
                     qkv4[:, 0:1], qkv4[:, 1:2], op0=ALU.mult, op1=ALU.add)
                E.op("vector", "tensor_scalar", kf[:], zf[:],
                     qkv4[:, 2:3], qkv4[:, 3:4], op0=ALU.mult, op1=ALU.add,
                     sem="VE")

                for ec in range(4):
                    E.op("tensor", "matmul", psv[:, ec * 512:(ec + 1) * 512],
                         C["ones_row"][0:1, :],
                         binrow[0:1, 2048 + ec * 512:2048 + (ec + 1) * 512],
                         start=True, stop=True,
                         sem="TE" if ec == 3 else None)
                E.wait("vector", "TE")
                E.op("vector", "tensor_copy", bvb[:], psv[:], sem="VE")
                E.wait("tensor", "VE")

                for tt in range(32):
                    for ec in range(4):
                        sl = psv[:, ec * 512:(ec + 1) * 512]
                        for d in range(8):
                            E.op("tensor", "matmul", sl,
                                 xln[:, d, tt * 128:(tt + 1) * 128],
                                 wsb[:, d, 2048 + ec * 512:2048 + (ec + 1) * 512],
                                 start=(d == 0), stop=(d == 7),
                                 sem="TE" if d == 7 else None)
                        E.wait("vector", "TE")
                        E.op("vector", "tensor_add", ev1[:], sl,
                             bvb[:, ec * 512:(ec + 1) * 512], sem="VE")
                        E.wait("scalar", "VE")
                        E.op("scalar", "activation", ub[:], ev1[:], AF.Silu,
                             sem="SE")
                        E.wait("sync", "SE")
                        E.dma("sync", v_dram[tt * 128:(tt + 1) * 128,
                                             ec * 512:(ec + 1) * 512],
                              ub[:], sem="ST")
                        E.wait("tensor", "VE")
                        E.wait("vector", "SE")
                        E.wait("scalar", "ST")
                E.wait("sync", "ST")
                E.barrier()
                E.flush()

        with (
            nc.sbuf_tensor("vsb", [128, 32, 2048], BF16) as vsb,
            nc.sbuf_tensor("pt", [128, 32, 256], BF16) as pt,
            nc.sbuf_tensor("sc1", [128, 256], F32) as sc1,
            nc.sbuf_tensor("pvu", [128, 16, 256], BF16) as pvu,
            nc.sbuf_tensor("ut", [128, 256], BF16) as ut,
            nc.psum_tensor("pss", [128, 512], F32) as pss,
            nc.psum_tensor("psp", [128, 8 * 256], F32) as psp,
        ):
            for tt in range(32):
                E.dma("sync", vsb[:, tt, :], v_dram[tt * 128:(tt + 1) * 128, :])
            E.wait("tensor", "LD")

            for qc in range(8):
                q_ap = qf[:, qc * 256:(qc + 1) * 256]
                for kt in range(32):
                    sl = pss[:, (kt % 2) * 256:(kt % 2) * 256 + 256]
                    E.op("tensor", "matmul", sl,
                         kf[:, kt * 128:(kt + 1) * 128], q_ap,
                         start=True, stop=True, sem="TE")
                    E.wait("vector", "TE")
                    E.op("vector", "tensor_scalar", sc1[:], sl, 0.0, None,
                         op0=ALU.max, sem="VE")
                    E.wait("scalar", "VE")
                    E.op("scalar", "activation", pt[:, kt, :], sc1[:],
                         AF.Square, scale=1.0 / T, sem="SE")
                    E.wait("tensor", "VE")
                    E.wait("vector", "SE")
                E.wait("tensor", "SE")
                for eg in range(2):
                    for kt in range(32):
                        for et in range(8):
                            e_abs = eg * 8 + et
                            E.op("tensor", "matmul",
                                 psp[:, et * 256:(et + 1) * 256],
                                 vsb[:, kt, e_abs * 128:(e_abs + 1) * 128],
                                 pt[:, kt, :],
                                 start=(kt == 0), stop=(kt == 31),
                                 sem="TE" if (kt == 31 and et == 7) else None)
                    te = E.cnt["TE"]
                    for et in range(8):
                        e_abs = eg * 8 + et
                        E.dma("sync", ut[:],
                              u_dram[e_abs * 128:(e_abs + 1) * 128,
                                     qc * 256:(qc + 1) * 256])
                        E.wait("vector", "LD")
                        if et == 0:
                            E.wait("vector", "TE", te)
                        E.op("vector", "tensor_mul", pvu[:, e_abs, :],
                             psp[:, et * 256:(et + 1) * 256], ut[:], sem="VE")
                        E.wait("sync", "VE")
                    E.wait("tensor", "VE")
                E.wait("sync", "VE")
                for et in range(16):
                    E.dma("sync", pvu_dram[et * 128:(et + 1) * 128,
                                           qc * 256:(qc + 1) * 256],
                          pvu[:, et, :], sem="ST")
                E.wait("tensor", "ST")
                E.wait("vector", "ST")
                E.wait("scalar", "ST")
            E.barrier()
            E.flush()

        with (
            nc.sbuf_tensor("wob", [128, 16, 1024], BF16) as wob,
            nc.sbuf_tensor("pvt", [128, 16, 512], BF16) as pvt,
            nc.sbuf_tensor("xnt", [128, 512], BF16) as xnt,
            nc.sbuf_tensor("attf", [128, 512], F32) as attf,
            nc.psum_tensor("pso", [128, 8 * 512], F32) as pso,
        ):
            for et in range(16):
                E.dma("gpsimd", wob[:, et, :], w_out[et * 128:(et + 1) * 128, :],
                      sem="LD2")
            E.wait("tensor", "LD2")
            for jc in range(4):
                for et in range(16):
                    E.dma("sync", pvt[:, et, :],
                          pvu_dram[et * 128:(et + 1) * 128,
                                   jc * 512:(jc + 1) * 512])
                E.wait("tensor", "LD")
                for dt in range(8):
                    for et in range(16):
                        E.op("tensor", "matmul", pso[:, dt * 512:(dt + 1) * 512],
                             wob[:, et, dt * 128:(dt + 1) * 128], pvt[:, et, :],
                             start=(et == 0), stop=(et == 15),
                             sem="TE" if (et == 15 and dt == 7) else None)
                te = E.cnt["TE"]
                for dt in range(8):
                    E.dma("sync", xnt[:], xn_dram[dt * 128:(dt + 1) * 128,
                                                  jc * 512:(jc + 1) * 512])
                    E.wait("vector", "LD")
                    if dt == 0:
                        E.wait("vector", "TE", te)
                    E.op("vector", "tensor_scalar", attf[:],
                         pso[:, dt * 512:(dt + 1) * 512], boutt[:, dt:dt + 1],
                         None, op0=ALU.add)
                    E.op("vector", "tensor_add", attf[:], attf[:], xnt[:],
                         sem="VE")
                    E.wait("sync", "VE")
                    E.dma("sync", att_dram[dt * 128:(dt + 1) * 128,
                                           jc * 512:(jc + 1) * 512], attf[:],
                          sem="ST")
                    E.wait("vector", "ST")
                E.wait("tensor", "VE")
                E.wait("sync", "TE")
            E.wait("sync", "ST")
            E.barrier()
            E.flush()

        with (
            nc.sbuf_tensor("xnr", [128, 8, 2048], BF16) as xnr,
            nc.sbuf_tensor("acc", [128, 8, 2048], F32) as acc,
            nc.sbuf_tensor("cfm", [16, 2048], F32) as cfm,
            nc.sbuf_tensor("selc", [16, 16 * 128], F32) as selc,
        ):
            for d in range(8):
                E.dma("sync", xnr[:, d, :], xn_dram[d * 128:(d + 1) * 128, :])
            E.op("gpsimd", "iota", selc[:], [[1, 16], [0, 128]], sem="GE",
                 base=0, channel_multiplier=-1,
                 allow_small_or_imprecise_dtypes=True)
            E.wait("vector", "GE")
            E.op("vector", "tensor_scalar", selc[:], selc[:], 0.0, 1.0,
                 op0=ALU.is_equal, op1=ALU.mult, sem="VE")
            sel_ve = E.cnt["VE"]

            with (
                nc.sbuf_tensor("rwt", [128, 8, 16], F32) as rwt,
                nc.sbuf_tensor("xn32", [128, 8, 512], F32) as xn32,
                nc.sbuf_tensor("lgf", [16, 2048], F32) as lgf,
                nc.sbuf_tensor("ltm", [128, 16, 16], F32) as ltm,
                nc.sbuf_tensor("top8", [128, 16, 8], F32) as top8,
                nc.sbuf_tensor("sml", [128, 16, 5], F32) as sml,
                nc.sbuf_tensor("ctm", [128, 16, 16], F32) as ctm,
                nc.psum_tensor("ps_l", [16, 512], F32) as ps_l,
                nc.psum_tensor("ps_t", [128, 16], F32) as ps_t,
                nc.psum_tensor("ps_t2", [16, 128], F32) as ps_t2,
            ):
                for d in range(8):
                    E.dma("sync", rwt[:, d, :],
                          router_wT[d * 128:(d + 1) * 128, :])
                E.wait("tensor", "LD")
                for jc in range(4):
                    for d in range(8):
                        E.dma("sync", xn32[:, d, :],
                              xn32_dram[d * 128:(d + 1) * 128,
                                        jc * 512:(jc + 1) * 512])
                    E.wait("tensor", "LD")
                    for d in range(8):
                        E.op("tensor", "matmul", ps_l[:, :],
                             rwt[:, d, :], xn32[:, d, :],
                             start=(d == 0), stop=(d == 7),
                             sem="TE" if d == 7 else None)
                    E.wait("vector", "TE")
                    E.wait("sync", "TE")
                    E.op("vector", "tensor_scalar",
                         lgf[0:16, jc * 512:(jc + 1) * 512],
                         ps_l[:, :], rbt[0:16, 0:1], None, op0=ALU.add,
                         sem="VE")
                    E.wait("tensor", "VE")
                for g in range(16):
                    E.op("tensor", "transpose", ps_t[:, :],
                         lgf[0:16, g * 128:(g + 1) * 128],
                         C["identf"][0:16, 0:16], sem="TE")
                    E.wait("vector", "TE")
                    E.op("vector", "tensor_copy", ltm[:, g, :], ps_t[:, :],
                         sem="VE")
                    E.wait("tensor", "VE")
                for g in range(16):
                    E.op("vector", "max", top8[:, g, :], ltm[:, g, :],
                         sem="VE" if g == 15 else None)
                E.wait("vector", "VE")
                for g in range(16):
                    E.op("vector", "tensor_sub", sml[:, g, 0:1],
                         top8[:, g, 1:2], top8[:, g, 0:1],
                         sem="VE" if g == 15 else None)
                E.wait("scalar", "VE")
                for g in range(16):
                    E.op("scalar", "activation", sml[:, g, 1:2], sml[:, g, 0:1],
                         AF.Exp, sem="SE" if g == 15 else None)
                E.wait("vector", "SE")
                for g in range(16):
                    E.op("vector", "tensor_scalar", sml[:, g, 2:3],
                         sml[:, g, 1:2], 1.0, None, op0=ALU.add,
                         sem="VE" if g == 15 else None)
                E.wait("vector", "VE")
                for g in range(16):
                    E.op("vector", "reciprocal", sml[:, g, 3:4], sml[:, g, 2:3],
                         sem="VE" if g == 15 else None)
                E.wait("vector", "VE")
                for g in range(16):
                    E.op("vector", "tensor_mul", sml[:, g, 4:5],
                         sml[:, g, 1:2], sml[:, g, 3:4],
                         sem="VE" if g == 15 else None)
                E.wait("vector", "VE")
                for g in range(16):
                    E.op("vector", "tensor_scalar", ctm[:, g, :], ltm[:, g, :],
                         top8[:, g, 0:1], sml[:, g, 3:4],
                         op0=ALU.is_equal, op1=ALU.mult)
                    E.op("vector", "tensor_scalar", ltm[:, g, :], ltm[:, g, :],
                         top8[:, g, 1:2], sml[:, g, 4:5],
                         op0=ALU.is_equal, op1=ALU.mult,
                         sem="VE" if g == 15 else None)
                E.wait("vector", "VE")
                for g in range(16):
                    E.op("vector", "tensor_add", ctm[:, g, :], ctm[:, g, :],
                         ltm[:, g, :], sem="VE" if g == 15 else None)
                E.wait("tensor", "VE")
                for g in range(16):
                    E.op("tensor", "transpose", ps_t2[:, :],
                         ctm[:, g, :], C["identf"][:], sem="TE")
                    E.wait("vector", "TE")
                    E.op("vector", "tensor_copy", cfm[0:16, g * 128:(g + 1) * 128],
                         ps_t2[0:16, :], sem="VE")
                    E.wait("tensor", "VE")
                if DEBUG_L0:
                    E.wait("sync", "VE")
                    E.dma("sync", dbg_ltm[:, :, :], ltm[:, :, :], sem="ST")
                    E.dma("sync", dbg_t8[:, :, :], top8[:, :, :], sem="ST")
                    E.dma("sync", dbg_lgf[:], lgf[0:16, :], sem="ST")
                    E.dma("sync", dbg_sml[:, :, :], sml[:, :, :], sem="ST")
                    E.dma("sync", dbg_ctm[:, :, :], ctm[:, :, :], sem="ST")
                    E.wait("sync", "ST")
                E.barrier()
                E.flush()

            with (
                nc.sbuf_tensor("ew", [128, 2, 8, 1024], BF16) as ew,
                nc.sbuf_tensor("dw", [128, 2, 4, 1024], BF16) as dw,
                nc.sbuf_tensor("gsc", [128, 4, 512], F32) as gsc,
                nc.sbuf_tensor("hb", [128, 4, 512], BF16) as hb,
                nc.sbuf_tensor("cbe", [128, 4, 512], BF16) as cbe,
                nc.psum_tensor("ps_g", [128, 4 * 512], F32) as ps_g,
                nc.psum_tensor("ps_u", [128, 4 * 512], F32) as ps_u,
            ):
                te_done = {}
                ve_add = {}
                for e in range(16):
                    buf = e % 2
                    if e >= 2:
                        E.wait("gpsimd", "TE", te_done[e - 2])
                    for d in range(8):
                        E.dma("gpsimd", ew[:, buf, d, 0:512],
                              gate_p[e, d * 128:(d + 1) * 128, :], sem="LD2")
                        E.dma("gpsimd", ew[:, buf, d, 512:1024],
                              up_p[e, d * 128:(d + 1) * 128, :], sem="LD2")
                    for ht in range(4):
                        E.dma("gpsimd", dw[:, buf, ht, :],
                              down_p[e, ht * 128:(ht + 1) * 128, :], sem="LD2")
                    ld2_e = E.cnt["LD2"]

                    ve_cbe = {}
                    if e == 0:
                        E.wait("tensor", "VE", sel_ve)
                    for jc in range(4):
                        if e > 0:
                            E.wait("tensor", "VE", ve_add[jc])
                        E.op("tensor", "matmul",
                             ps_g[:, jc * 512:(jc + 1) * 512],
                             selc[0:16, e * 128:(e + 1) * 128],
                             cfm[0:16, jc * 512:(jc + 1) * 512],
                             start=True, stop=True, sem="TE")
                    E.wait("vector", "TE")
                    for jc in range(4):
                        E.op("vector", "tensor_copy", cbe[:, jc, :],
                             ps_g[:, jc * 512:(jc + 1) * 512], sem="VE")
                        ve_cbe[jc] = E.cnt["VE"]

                    E.wait("tensor", "LD2", ld2_e)
                    for jc in range(4):
                        xn_ap = [xnr[:, d, jc * 512:(jc + 1) * 512]
                                 for d in range(8)]
                        te_g = {}
                        te_u = {}
                        for ht in range(4):
                            E.wait("tensor", "VE",
                                   ve_cbe[ht] if jc == 0 else ve_add[ht])
                            for d in range(8):
                                E.op("tensor", "matmul",
                                     ps_g[:, ht * 512:(ht + 1) * 512],
                                     ew[:, buf, d, ht * 128:(ht + 1) * 128],
                                     xn_ap[d],
                                     start=(d == 0), stop=(d == 7),
                                     sem="TE" if d == 7 else None)
                            te_g[ht] = E.cnt["TE"]
                        for ht in range(4):
                            if e > 0 or jc > 0:
                                E.wait("tensor", "VE", ve_add[4 + ht])
                            for d in range(8):
                                E.op("tensor", "matmul",
                                     ps_u[:, ht * 512:(ht + 1) * 512],
                                     ew[:, buf, d,
                                        512 + ht * 128:512 + (ht + 1) * 128],
                                     xn_ap[d],
                                     start=(d == 0), stop=(d == 7),
                                     sem="TE" if d == 7 else None)
                            te_u[ht] = E.cnt["TE"]
                        se_h = {}
                        for ht in range(4):
                            E.wait("scalar", "TE", te_g[ht])
                            E.op("scalar", "activation", gsc[:, ht, :],
                                 ps_g[:, ht * 512:(ht + 1) * 512], AF.Silu,
                                 sem="SE")
                            se_h[ht] = E.cnt["SE"]
                        for ht in range(4):
                            E.wait("vector", "SE", se_h[ht])
                            E.wait("vector", "TE", te_u[ht])
                            E.op("vector", "tensor_mul", hb[:, ht, :],
                                 gsc[:, ht, :],
                                 ps_u[:, ht * 512:(ht + 1) * 512],
                                 sem="VE" if ht == 3 else None)
                        E.wait("vector", "VE")
                        for ht in range(4):
                            E.op("vector", "tensor_mul", hb[:, ht, :],
                                 hb[:, ht, :], cbe[:, jc, :],
                                 sem="VE" if ht == 3 else None)
                        ve_hb = E.cnt["VE"]
                        E.wait("tensor", "SE")
                        E.wait("tensor", "VE", ve_hb)
                        te_d = {}
                        for dt in range(8):
                            psd = (ps_g[:, dt * 512:(dt + 1) * 512] if dt < 4
                                   else ps_u[:, (dt - 4) * 512:(dt - 3) * 512])
                            for ht in range(4):
                                E.op("tensor", "matmul", psd,
                                     dw[:, buf, ht, dt * 128:(dt + 1) * 128],
                                     hb[:, ht, :],
                                     start=(ht == 0), stop=(ht == 3),
                                     sem="TE" if ht == 3 else None)
                            te_d[dt] = E.cnt["TE"]
                        te_done[e] = E.cnt["TE"]
                        for dt in range(8):
                            psd = (ps_g[:, dt * 512:(dt + 1) * 512] if dt < 4
                                   else ps_u[:, (dt - 4) * 512:(dt - 3) * 512])
                            a_ap = acc[:, dt, jc * 512:(jc + 1) * 512]
                            E.wait("vector", "TE", te_d[dt])
                            if e == 0:
                                E.op("vector", "tensor_copy", a_ap, psd,
                                     sem="VE")
                            else:
                                E.op("vector", "tensor_add", a_ap, a_ap, psd,
                                     sem="VE")
                            ve_add[dt] = E.cnt["VE"]
                E.wait("tensor", "VE")
                E.barrier()
                E.flush()

            with (
                nc.sbuf_tensor("sg", [128, 8, 1024], BF16) as sg,
                nc.sbuf_tensor("su", [128, 8, 1024], BF16) as su,
                nc.sbuf_tensor("sd", [128, 8, 1024], BF16) as sd,
                nc.sbuf_tensor("gs2", [128, 8, 512], BF16) as gs2,
                nc.sbuf_tensor("hb2", [128, 8, 512], BF16) as hb2,
                nc.psum_tensor("ps_a", [128, 4 * 512], F32) as ps_a,
                nc.psum_tensor("ps_b2", [128, 4 * 512], F32) as ps_b2,
            ):
                def psx(i):
                    return (ps_a[:, i * 512:(i + 1) * 512] if i < 4
                            else ps_b2[:, (i - 4) * 512:(i - 3) * 512])

                for d in range(8):
                    E.dma("gpsimd", sg[:, d, :], shT_gate[d * 128:(d + 1) * 128, :],
                          sem="LD2")
                    E.dma("gpsimd", su[:, d, :], shT_up[d * 128:(d + 1) * 128, :],
                          sem="LD2")
                    E.dma("gpsimd", sd[:, d, :], shT_down[d * 128:(d + 1) * 128, :],
                          sem="LD2")
                E.wait("tensor", "LD2")
                for jc in range(4):
                    xn_ap = [xnr[:, d, jc * 512:(jc + 1) * 512] for d in range(8)]
                    for ht in range(8):
                        for d in range(8):
                            E.op("tensor", "matmul", psx(ht),
                                 sg[:, d, ht * 128:(ht + 1) * 128], xn_ap[d],
                                 start=(d == 0), stop=(d == 7),
                                 sem="TE" if (ht == 7 and d == 7) else None)
                    E.wait("scalar", "TE")
                    for ht in range(8):
                        E.op("scalar", "activation", gs2[:, ht, :], psx(ht),
                             AF.Silu, sem="SE" if ht == 7 else None)
                    E.wait("tensor", "SE")
                    for ht in range(8):
                        for d in range(8):
                            E.op("tensor", "matmul", psx(ht),
                                 su[:, d, ht * 128:(ht + 1) * 128], xn_ap[d],
                                 start=(d == 0), stop=(d == 7),
                                 sem="TE" if (ht == 7 and d == 7) else None)
                    E.wait("vector", "TE")
                    for ht in range(8):
                        E.op("vector", "tensor_mul", hb2[:, ht, :], gs2[:, ht, :],
                             psx(ht), sem="VE" if ht == 7 else None)
                    E.wait("tensor", "VE")
                    for dt in range(8):
                        for ht in range(8):
                            E.op("tensor", "matmul", psx(dt),
                                 sd[:, ht, dt * 128:(dt + 1) * 128], hb2[:, ht, :],
                                 start=(ht == 0), stop=(ht == 7),
                                 sem="TE" if (dt == 7 and ht == 7) else None)
                    E.wait("vector", "TE")
                    for dt in range(8):
                        a_ap = acc[:, dt, jc * 512:(jc + 1) * 512]
                        E.op("vector", "tensor_add", a_ap, a_ap, psx(dt),
                             sem="VE" if dt == 7 else None)
                    E.wait("tensor", "VE")
                E.barrier()
                E.flush()

            with (
                nc.sbuf_tensor("xf2", [128, 512], F32) as xf2,
                nc.sbuf_tensor("af2", [128, 512], F32) as af2,
                nc.sbuf_tensor("of2", [128, 512], F32) as of2,
            ):
                for jc in range(4):
                    for dt in range(8):
                        E.dma("sync", xf2[:],
                              xfm[dt * 128:(dt + 1) * 128,
                                  jc * 512:(jc + 1) * 512])
                        E.dma("sync", af2[:],
                              att_dram[dt * 128:(dt + 1) * 128,
                                       jc * 512:(jc + 1) * 512])
                        E.wait("vector", "LD")
                        E.op("vector", "tensor_add", of2[:], xf2[:], af2[:])
                        E.op("vector", "tensor_add", of2[:], of2[:],
                             acc[:, dt, jc * 512:(jc + 1) * 512], sem="VE")
                        E.wait("sync", "VE")
                        E.dma("sync", x1fm[dt * 128:(dt + 1) * 128,
                                           jc * 512:(jc + 1) * 512], of2[:],
                              sem="ST")
                        E.wait("vector", "ST")
                if DEBUG_L0:
                    E.dma("sync", dbg_cfm[:], cfm[0:16, :], sem="ST")
                    E.dma("sync", dbg_xn[:], xn_dram[:], sem="ST")
                    E.dma("sync", dbg_v[:], v_dram[:], sem="ST")
                    E.dma("sync", dbg_u[:], u_dram[:], sem="ST")
                    E.dma("sync", dbg_att[:], att_dram[:], sem="ST")
                    E.dma("sync", dbg_pvu[:], pvu_dram[:], sem="ST")
                    for dt in range(8):
                        for jc in range(4):
                            E.dma("sync",
                                  dbg_acc[dt * 128:(dt + 1) * 128,
                                          jc * 512:(jc + 1) * 512],
                                  acc[:, dt, jc * 512:(jc + 1) * 512], sem="ST")
                E.wait("sync", "ST")
                E.barrier()
                E.flush()

    return nc




def build_l1(upto=99):
    nc = bass.Bass("TRN2", target_bir_lowering=False)
    xfull = nc.dram_tensor("xfull", [D, T], F32, kind="ExternalInput").ap()
    xq = nc.dram_tensor("xq", [D, 2048], F32, kind="ExternalInput").ap()
    wq = nc.dram_tensor("wq", [D, D], F32, kind="ExternalInput").ap()
    wdn = nc.dram_tensor("wdn", [D, 512], F32, kind="ExternalInput").ap()
    wku = nc.dram_tensor("wku", [512, D], F32, kind="ExternalInput").ap()
    wvu = nc.dram_tensor("wvu", [512, D], F32, kind="ExternalInput").ap()
    wo = nc.dram_tensor("wo", [D, D], F32, kind="ExternalInput").ap()
    w1 = nc.dram_tensor("w1", [D, 4096], F32, kind="ExternalInput").ap()
    b1_t = nc.dram_tensor("b1_t", [32, 128, 1], F32, kind="ExternalInput").ap()
    w2 = nc.dram_tensor("w2", [4096, D], F32, kind="ExternalInput").ap()
    b2_t = nc.dram_tensor("b2_t", [8, 128, 1], F32, kind="ExternalInput").ap()
    cosq = nc.dram_tensor("cosq", [2048, 512], F32, kind="ExternalInput").ap()
    sinq = nc.dram_tensor("sinq", [2048, 512], F32, kind="ExternalInput").ap()
    cosk = nc.dram_tensor("cosk", [T, 512], F32, kind="ExternalInput").ap()
    sink = nc.dram_tensor("sink", [T, 512], F32, kind="ExternalInput").ap()
    mbcol = nc.dram_tensor("mbcol", [128, 32, 4], F32, kind="ExternalInput").ap()
    mbq = nc.dram_tensor("mbq", [4, 6, 2, 128], F32, kind="ExternalInput").ap()

    x2fm = nc.dram_tensor("x2fm", [D, 2048], F32, kind="ExternalOutput").ap()

    att_dram = nc.dram_tensor("l1att", [D, 2048], F32, kind="Internal").ap()
    xqn_dram = nc.dram_tensor("l1xqn", [D, 2048], BF16, kind="Internal").ap()
    lat_dram = nc.dram_tensor("l1lat", [512, T], BF16, kind="Internal").ap()
    k_dram = nc.dram_tensor("l1k", [8, 128, T], BF16, kind="Internal").ap()
    q_dram = nc.dram_tensor("l1q", [8, 128, 2048], BF16, kind="Internal").ap()
    v_dram = nc.dram_tensor("l1v", [T, 16, 65], BF16, kind="Internal").ap()

    with ExitStack() as ctx:
        sems = std_sems(nc, ctx)
        block = ctx.enter_context(nc.Block())
        E = Emit(nc, block, sems)
        C = emit_consts(E, ctx, nc)
        register_fconsts(E, ctx, nc, [LN_EPS, -0.5, 1.0 / D, 0.125, -20.0])

        b2t = ctx.enter_context(nc.sbuf_tensor("b2t", [128, 8], F32))
        mbc = ctx.enter_context(nc.sbuf_tensor("mbc", [128, 32, 4], F32))
        mbqs = ctx.enter_context(nc.sbuf_tensor("mbqs", [2, 4, 6, 128], F32))
        ind2 = ctx.enter_context(nc.sbuf_tensor("ind2", [2, 512], F32))
        ind2b = ctx.enter_context(nc.sbuf_tensor("ind2b", [2, 512], F32))

        for d in range(8):
            E.dma("sync", b2t[:, d:d + 1], b2_t[d])
        E.dma("sync", mbc[:, :, :], mbcol[:, :, :])
        for p in range(4):
            for t in range(6):
                E.dma("sync", mbqs[:, p, t, :], mbq[p, t, :, :])
        E.op("gpsimd", "iota", ind2[0:2, :], [[1, 512]], base=0,
             channel_multiplier=0, allow_small_or_imprecise_dtypes=True)
        E.op("gpsimd", "iota", ind2b[0:2, :], [[0, 512]], base=0,
             channel_multiplier=1, allow_small_or_imprecise_dtypes=True,
             sem="GE")
        E.wait("vector", "GE")
        E.op("vector", "tensor_scalar", ind2[0:2, :], ind2[0:2, :], 255.5,
             None, op0=ALU.is_gt, sem="VE")
        E.wait("vector", "VE")
        E.op("vector", "tensor_tensor", ind2[0:2, :], ind2[0:2, :],
             ind2b[0:2, :], op=ALU.is_equal, sem="VE")
        E.barrier()
        E.flush()

        with (
            nc.sbuf_tensor("xt", [128, 8, 256], F32) as xt,
            nc.sbuf_tensor("sq", [128, 8, 256], F32) as sq,
            nc.sbuf_tensor("xnb", [128, 8, 256], BF16) as xnb,
            nc.sbuf_tensor("wds", [128, 8, 512], BF16) as wds,
            nc.sbuf_tensor("latc", [128, 4, 256], BF16) as latc,
            nc.sbuf_tensor("strow", [1, 256, 4], F32) as strow,
            nc.psum_tensor("psA", [1, 256], F32) as psA,
            nc.psum_tensor("psB", [1, 256], F32) as psB,
            nc.psum_tensor("psb", [128, 512], F32) as psb,
            nc.psum_tensor("psm", [128, 1024], F32) as psm,
        ):
            for d in range(8):
                E.dma("gpsimd", wds[:, d, :], wdn[d * 128:(d + 1) * 128, :],
                      sem="LD2")
            E.wait("tensor", "LD2")
            for src_ap, nch, is_own in ((xfull, 16, False), (xq, 8, True)):
                for j in range(nch):
                    for d in range(8):
                        E.dma("sync", xt[:, d, :],
                              src_ap[d * 128:(d + 1) * 128,
                                     j * 256:(j + 1) * 256])
                    E.wait("scalar", "LD")
                    E.wait("tensor", "LD")
                    xts = [xt[:, d, :] for d in range(8)]
                    sqs = [sq[:, d, :] for d in range(8)]
                    te = emit_ln_fm(E, C, xts, sqs, psA, psB, psb, strow, D)
                    E.wait("vector", "TE", te)
                    for d in range(8):
                        E.op("vector", "tensor_sub", sq[:, d, :], xt[:, d, :],
                             psb[:, 0:256], sem="VE" if d == 7 else None)
                    E.wait("vector", "VE")
                    for d in range(8):
                        E.op("vector", "tensor_mul", sq[:, d, :], sq[:, d, :],
                             psb[:, 256:512], sem="VE" if d == 7 else None)
                    E.wait("vector", "VE")
                    for d in range(8):
                        E.op("vector", "tensor_copy", xnb[:, d, :], sq[:, d, :],
                             sem="VE" if d == 7 else None)
                    ve = E.cnt["VE"]
                    if is_own:
                        E.wait("sync", "VE", ve)
                        for d in range(8):
                            E.dma("sync", xqn_dram[d * 128:(d + 1) * 128,
                                                   j * 256:(j + 1) * 256],
                                  xnb[:, d, :], sem="ST")
                        E.wait("scalar", "ST")
                    else:
                        E.wait("tensor", "VE", ve)
                        for lt in range(4):
                            sl = psm[:, lt * 256:(lt + 1) * 256]
                            for d in range(8):
                                E.op("tensor", "matmul", sl,
                                     wds[:, d, lt * 128:(lt + 1) * 128],
                                     xnb[:, d, :],
                                     start=(d == 0), stop=(d == 7),
                                     sem="TE" if (lt == 3 and d == 7) else None)
                        E.wait("vector", "TE")
                        for lt in range(4):
                            E.op("vector", "tensor_copy", latc[:, lt, :],
                                 psm[:, lt * 256:(lt + 1) * 256],
                                 sem="VE" if lt == 3 else None)
                        E.wait("sync", "VE")
                        for lt in range(4):
                            E.dma("sync", lat_dram[lt * 128:(lt + 1) * 128,
                                                   j * 256:(j + 1) * 256],
                                  latc[:, lt, :], sem="ST")
                        E.wait("tensor", "VE")
                        E.wait("vector", "ST")
                    E.wait("sync", "VE")
                    E.wait("scalar", "VE")
            E.wait("sync", "ST")
            E.barrier()
            E.flush()
        if upto <= 1:
            return nc

        with (
            nc.sbuf_tensor("wus", [128, 4, 2048], BF16) as wus,
            nc.sbuf_tensor("wqs", [128, 8, 1024], BF16) as wqs,
            nc.sbuf_tensor("latt", [128, 4, 128], BF16) as latt,
            nc.sbuf_tensor("xqt", [128, 8, 128], BF16) as xqt,
            nc.sbuf_tensor("rtm", [128, 1024], F32) as rtm,
            nc.sbuf_tensor("rsc", [128, 512], F32) as rsc,
            nc.sbuf_tensor("cst", [128, 1024], F32) as cst,
            nc.sbuf_tensor("rtb", [128, 1024], BF16) as rtb,
            nc.sbuf_tensor("ktb", [128, 128], BF16) as ktb,
            nc.sbuf_tensor("vtb", [128, 16, 65], BF16) as vtb,
            nc.psum_tensor("psk", [128, 1024], F32) as psk,
            nc.psum_tensor("psv", [128, 1024], F32) as psv,
            nc.psum_tensor("pstp", [128, 128], BF16) as pstp,
        ):
            for lt in range(4):
                E.dma("gpsimd", wus[:, lt, 0:1024],
                      wku[lt * 128:(lt + 1) * 128, :], sem="LD2")
                E.dma("gpsimd", wus[:, lt, 1024:2048],
                      wvu[lt * 128:(lt + 1) * 128, :], sem="LD2")
            for d in range(8):
                E.dma("gpsimd", wqs[:, d, :], wq[d * 128:(d + 1) * 128, :],
                      sem="LD2")
            E.op("vector", "memset", vtb[:, :, 64:65], 1.0, sem="VE")
            E.wait("tensor", "LD2")

            def rope_block(src_psum, cos_ap, sin_ap, dst_dram, tt):
                E.dma("sync", cst[:, 0:512], cos_ap)
                E.dma("sync", cst[:, 512:1024], sin_ap)
                E.wait("vector", "LD")
                kE = src_psum[:, 0:1024:2]
                kO = src_psum[:, 1:1024:2]
                E.op("vector", "tensor_mul", rtm[:, 0:1024:2], kE,
                     cst[:, 0:512])
                E.op("vector", "tensor_mul", rsc[:], kO, cst[:, 512:1024],
                     sem="VE")
                E.wait("vector", "VE")
                E.op("vector", "tensor_sub", rtm[:, 0:1024:2],
                     rtm[:, 0:1024:2], rsc[:])
                E.op("vector", "tensor_mul", rtm[:, 1:1024:2], kE,
                     cst[:, 512:1024], sem="VE")
                E.wait("vector", "VE")
                E.op("vector", "tensor_mul", rsc[:], kO, cst[:, 0:512],
                     sem="VE")
                E.wait("vector", "VE")
                E.op("vector", "tensor_add", rtm[:, 1:1024:2],
                     rtm[:, 1:1024:2], rsc[:], sem="VE")
                E.wait("vector", "VE")
                E.op("vector", "tensor_copy", rtb[:], rtm[:], sem="VE")
                ve = E.cnt["VE"]
                E.wait("tensor", "VE", ve)
                for dtile in range(8):
                    E.op("tensor", "transpose", pstp[:, :],
                         rtb[:, dtile * 128:(dtile + 1) * 128],
                         C["ident"][:], sem="TE")
                    E.wait("vector", "TE")
                    E.op("vector", "tensor_copy", ktb[:, :], pstp[:, :],
                         sem="VE")
                    E.wait("sync", "VE")
                    E.dma("sync", dst_dram[dtile, :, tt * 128:(tt + 1) * 128],
                          ktb[:, :], sem="ST")
                    E.wait("tensor", "VE")
                    E.wait("vector", "ST")
                E.wait("sync", "VE")

            for tt in range(32):
                for lt in range(4):
                    E.dma("sync", latt[:, lt, :],
                          lat_dram[lt * 128:(lt + 1) * 128,
                                   tt * 128:(tt + 1) * 128])
                E.wait("tensor", "LD")
                for ec in range(2):
                    for lt in range(4):
                        E.op("tensor", "matmul", psk[:, ec * 512:(ec + 1) * 512],
                             latt[:, lt, :],
                             wus[:, lt, ec * 512:(ec + 1) * 512],
                             start=(lt == 0), stop=(lt == 3))
                for ec in range(2):
                    for lt in range(4):
                        E.op("tensor", "matmul", psv[:, ec * 512:(ec + 1) * 512],
                             latt[:, lt, :],
                             wus[:, lt, 1024 + ec * 512:1024 + (ec + 1) * 512],
                             start=(lt == 0), stop=(lt == 3),
                             sem="TE" if (ec == 1 and lt == 3) else None)
                te = E.cnt["TE"]
                E.wait("vector", "TE", te)
                E.op("vector", "tensor_copy", vtb[:, :, 0:64], psv[:, :],
                     sem="VE")
                E.wait("sync", "VE")
                E.dma("sync", v_dram[tt * 128:(tt + 1) * 128, :, :],
                      vtb[:, :, :], sem="ST")
                E.wait("vector", "ST")
                rope_block(psk, cosk[tt * 128:(tt + 1) * 128, :],
                           sink[tt * 128:(tt + 1) * 128, :], k_dram, tt)
                E.wait("sync", "TE")
            for tt in range(16):
                for d in range(8):
                    E.dma("sync", xqt[:, d, :],
                          xqn_dram[d * 128:(d + 1) * 128,
                                   tt * 128:(tt + 1) * 128])
                E.wait("tensor", "LD")
                for ec in range(2):
                    for d in range(8):
                        E.op("tensor", "matmul", psk[:, ec * 512:(ec + 1) * 512],
                             xqt[:, d, :],
                             wqs[:, d, ec * 512:(ec + 1) * 512],
                             start=(d == 0), stop=(d == 7),
                             sem="TE" if (ec == 1 and d == 7) else None)
                te = E.cnt["TE"]
                E.wait("vector", "TE", te)
                rope_block(psk, cosq[tt * 128:(tt + 1) * 128, :],
                           sinq[tt * 128:(tt + 1) * 128, :], q_dram, tt)
                E.wait("sync", "TE")
            E.wait("sync", "ST")
            E.barrier()
            E.flush()
        if upto <= 3:
            return nc

        with (
          nc.sbuf_tensor("oatt", [64, 16, 2048], BF16) as oatt,
        ):
          with (
            nc.sbuf_tensor("kq", [128, T], BF16) as kq,
            nc.sbuf_tensor("qq", [128, 2048], BF16) as qq,
            nc.sbuf_tensor("vq", [128, 32, 130], BF16) as vq,
            nc.sbuf_tensor("ptile", [128, 2, 512], BF16) as ptile,
            nc.sbuf_tensor("lrec", [65, 512], F32) as lrec,
            nc.sbuf_tensor("ones64", [65, 128], F32) as ones64,
            nc.sbuf_tensor("otmp", [64, 512], F32) as otmp,
            nc.psum_tensor("pss", [128, 1024], F32) as pss,
            nc.psum_tensor("pso", [65, 2, 512], F32) as pso,
            nc.psum_tensor("psl", [128, 512], F32) as psl,
          ):
              E.op("vector", "memset", ones64[64:65, :], 1.0, sem="VE")
              E.wait("tensor", "VE")
              last_se = [None]
              for qtile in range(8):
                  E.dma("sync", kq[:, :], k_dram[qtile, :, :])
                  E.dma("sync", qq[:, :], q_dram[qtile, :, :])
                  for kvt in range(32):
                      E.dma("sync", vq[:, kvt, :],
                            v_dram[kvt * 128:(kvt + 1) * 128,
                                   2 * qtile:2 * qtile + 2, :]
                            .rearrange("p h e -> p (h e)"))
                  E.wait("tensor", "LD")
                  E.wait("scalar", "LD")
                  for p in range(4):
                      nkvt = 8 * p + 8
                      for half in range(2):
                          hh = 2 * qtile + half
                          qpart = 64 * half

                          def s_mm(kvt, sem_exp_prev):
                              sslot = kvt % 2
                              s_ap = pss[:, sslot * 512:(sslot + 1) * 512]
                              if sem_exp_prev is not None:
                                  E.wait("tensor", "SE", sem_exp_prev)
                              if kvt >= 8 * p + 2:
                                  E.op("tensor", "matmul", s_ap,
                                       kq[qpart:qpart + 64,
                                          kvt * 128:(kvt + 1) * 128],
                                       qq[qpart:qpart + 64,
                                          p * 512:(p + 1) * 512],
                                       start=True, stop=False)
                                  E.op("tensor", "matmul", s_ap,
                                       mbqs[0:2, p, kvt - 8 * p - 2, :],
                                       ind2[0:2, :], start=False, stop=True,
                                       sem="TE")
                              else:
                                  E.op("tensor", "matmul", s_ap,
                                       kq[qpart:qpart + 64,
                                          kvt * 128:(kvt + 1) * 128],
                                       qq[qpart:qpart + 64,
                                          p * 512:(p + 1) * 512],
                                       start=True, stop=True, sem="TE")
                              return E.cnt["TE"]

                          te_of = {}
                          se_of = {}
                          te_of[0] = s_mm(0, last_se[0])
                          for kvt in range(nkvt):
                              sslot = kvt % 2
                              s_ap = pss[:, sslot * 512:(sslot + 1) * 512]
                              E.wait("scalar", "TE", te_of[kvt])
                              E.op("scalar", "activation", ptile[:, sslot, :],
                                   s_ap, AF.Exp, scale=0.125,
                                   bias=mbc[:, kvt, p:p + 1], sem="SE")
                              se_of[kvt] = E.cnt["SE"]
                              if kvt + 1 < nkvt:
                                  te_of[kvt + 1] = s_mm(
                                      kvt + 1, se_of.get(kvt - 1))
                              E.wait("tensor", "SE", se_of[kvt])
                              E.op("tensor", "matmul", pso[:, half, :],
                                   vq[:, kvt, half * 65:half * 65 + 65],
                                   ptile[:, sslot, :],
                                   start=(kvt == 0), stop=(kvt == nkvt - 1),
                                   sem="TE" if kvt == nkvt - 1 else None)
                          last_se[0] = max(se_of[nkvt - 1], se_of[nkvt - 2])
                          te = E.cnt["TE"]
                          E.wait("vector", "TE", te)
                          E.op("vector", "reciprocal", lrec[64:65, :],
                               pso[64:65, half, :], sem="VE")
                          E.wait("vector", "VE")
                          E.op("vector", "tensor_copy", otmp[:, :],
                               pso[0:64, half, :], sem="VE")
                          E.wait("tensor", "VE")
                          E.op("tensor", "matmul", psl[:, :],
                               ones64[64:65, :], lrec[64:65, :],
                               start=True, stop=True, sem="TE")
                          E.wait("vector", "TE")
                          E.op("vector", "tensor_mul",
                               oatt[0:64, hh, p * 512:(p + 1) * 512],
                               otmp[:, :], psl[0:64, :], sem="VE")
                          E.wait("tensor", "VE")
                  E.wait("sync", "VE")
              E.barrier()
              E.flush()

          with (
              nc.sbuf_tensor("wos2", [64, 16, 1024], BF16) as wos2,
              nc.sbuf_tensor("aev", [128, 512], F32) as aev,
              nc.psum_tensor("psw", [128, 4 * 512], F32) as psw,
          ):
              for h2 in range(16):
                  E.dma("gpsimd", wos2[0:64, h2, :],
                        wo[64 * h2:64 * h2 + 64, :], sem="LD2")
              E.wait("tensor", "LD2")
              for dt in range(8):
                  for jc in range(4):
                      sl = psw[:, (jc % 4) * 512:(jc % 4) * 512 + 512]
                      for h2 in range(16):
                          E.op("tensor", "matmul", sl,
                               wos2[0:64, h2, dt * 128:(dt + 1) * 128],
                               oatt[0:64, h2, jc * 512:(jc + 1) * 512],
                               start=(h2 == 0), stop=(h2 == 15),
                               sem="TE" if h2 == 15 else None)
                      E.wait("vector", "TE")
                      E.op("vector", "tensor_copy", aev[:], sl, sem="VE")
                      E.wait("sync", "VE")
                      E.dma("sync", att_dram[dt * 128:(dt + 1) * 128,
                                             jc * 512:(jc + 1) * 512],
                            aev[:], sem="ST")
                      E.wait("vector", "ST")
                      E.wait("tensor", "VE")
              E.wait("sync", "ST")
              E.barrier()
              E.flush()

        with (
            nc.sbuf_tensor("w1s", [128, 8, 4096], BF16) as w1s,
            nc.sbuf_tensor("w2s", [128, 32, 1024], BF16) as w2s,
            nc.sbuf_tensor("b1s", [128, 32], F32) as b1s,
            nc.sbuf_tensor("xqc", [128, 8, 512], BF16) as xqc,
            nc.sbuf_tensor("hcb", [128, 32, 512], BF16) as hcb,
            nc.sbuf_tensor("xf3", [128, 512], F32) as xf3,
            nc.sbuf_tensor("af3", [128, 512], F32) as af3,
            nc.sbuf_tensor("of3", [128, 512], F32) as of3,
            nc.psum_tensor("psf", [128, 4 * 512], F32) as psf,
            nc.psum_tensor("psf2", [128, 4 * 512], F32) as psf2,
        ):
            for d in range(8):
                E.dma("gpsimd", w1s[:, d, :], w1[d * 128:(d + 1) * 128, :],
                      sem="LD2")
            for ht in range(32):
                E.dma("gpsimd", w2s[:, ht, :], w2[ht * 128:(ht + 1) * 128, :],
                      sem="LD2")
            for ht in range(32):
                E.dma("sync", b1s[:, ht:ht + 1], b1_t[ht])
            E.wait("tensor", "LD2")
            E.wait("tensor", "LD")
            E.wait("scalar", "LD")
            for jc in range(4):
                for d in range(8):
                    E.dma("sync", xqc[:, d, :],
                          xqn_dram[d * 128:(d + 1) * 128,
                                   jc * 512:(jc + 1) * 512])
                E.wait("tensor", "LD")
                for hg in range(4):
                    for hi in range(8):
                        ht = hg * 8 + hi
                        ps = (psf[:, hi * 512:(hi + 1) * 512] if hi < 4 else
                              psf2[:, (hi - 4) * 512:(hi - 3) * 512])
                        for d in range(8):
                            E.op("tensor", "matmul", ps,
                                 w1s[:, d, ht * 128:(ht + 1) * 128],
                                 xqc[:, d, :],
                                 start=(d == 0), stop=(d == 7),
                                 sem="TE" if (hi == 7 and d == 7) else None)
                    E.wait("scalar", "TE")
                    for hi in range(8):
                        ht = hg * 8 + hi
                        ps = (psf[:, hi * 512:(hi + 1) * 512] if hi < 4 else
                              psf2[:, (hi - 4) * 512:(hi - 3) * 512])
                        E.op("scalar", "activation", hcb[:, ht, :], ps,
                             AF.Relu, bias=b1s[:, ht:ht + 1],
                             sem="SE" if hi == 7 else None)
                    E.wait("tensor", "SE")
                for dt in range(8):
                    ps = (psf[:, (dt % 4) * 512:(dt % 4) * 512 + 512] if dt < 4
                          else psf2[:, (dt - 4) * 512:(dt - 3) * 512])
                    for ht in range(32):
                        E.op("tensor", "matmul", ps,
                             w2s[:, ht, dt * 128:(dt + 1) * 128], hcb[:, ht, :],
                             start=(ht == 0), stop=(ht == 31),
                             sem="TE" if (dt == 7 and ht == 31) else None)
                te = E.cnt["TE"]
                for dt in range(8):
                    ps = (psf[:, (dt % 4) * 512:(dt % 4) * 512 + 512] if dt < 4
                          else psf2[:, (dt - 4) * 512:(dt - 3) * 512])
                    E.dma("sync", xf3[:], xq[dt * 128:(dt + 1) * 128,
                                             jc * 512:(jc + 1) * 512])
                    E.dma("sync", af3[:], att_dram[dt * 128:(dt + 1) * 128,
                                                   jc * 512:(jc + 1) * 512])
                    E.wait("vector", "LD")
                    if dt == 0:
                        E.wait("vector", "TE", te)
                    E.op("vector", "tensor_scalar", of3[:], ps,
                         b2t[:, dt:dt + 1], None, op0=ALU.add, sem="VE")
                    E.wait("vector", "VE")
                    E.op("vector", "tensor_add", of3[:], of3[:], xf3[:],
                         sem="VE")
                    E.wait("vector", "VE")
                    E.op("vector", "tensor_add", of3[:], of3[:], af3[:],
                         sem="VE")
                    E.wait("sync", "VE")
                    E.dma("sync", x2fm[dt * 128:(dt + 1) * 128,
                                       jc * 512:(jc + 1) * 512], of3[:],
                          sem="ST")
                    E.wait("vector", "ST")
                E.wait("tensor", "VE")
                E.wait("sync", "TE")
            E.wait("sync", "ST")
            E.barrier()
            E.flush()

    return nc


_L1_CACHE = {}
ROPE_THETA = 10000.0


def _rope_tables_np(TT, Dh):
    inv = 1.0 / ROPE_THETA ** (np.arange(0, Dh, 2, dtype=np.float32) / Dh)
    ang = np.outer(np.arange(TT, dtype=np.float32), inv)
    return np.cos(ang).astype(np.float32), np.sin(ang).astype(np.float32)


def run_l1(inputs, x1):
    if "nc" not in _L1_CACHE:
        _L1_CACHE["nc"] = build_l1()
    nc = _L1_CACHE["nc"]
    w = inputs
    cos1, sin1 = _rope_tables_np(T, 64)
    cosk_full = np.tile(cos1, (1, 16))
    sink_full = np.tile(sin1, (1, 16))
    shared = dict(
        wq=np.ascontiguousarray(w["L1_attn_wq"]),
        wdn=np.ascontiguousarray(w["L1_attn_w_down"]),
        wku=np.ascontiguousarray(w["L1_attn_wk_up"]),
        wvu=np.ascontiguousarray(w["L1_attn_wv_up"]),
        wo=np.ascontiguousarray(w["L1_attn_wo"]),
        w1=np.ascontiguousarray(w["L1_ffn_w1"]),
        b1_t=tile_bias(w["L1_ffn_b1"]),
        w2=np.ascontiguousarray(w["L1_ffn_w2"]),
        b2_t=tile_bias(w["L1_ffn_b2"]),
        cosk=cosk_full, sink=sink_full,
    )
    in_maps = []
    for c in range(N_CORES):
        b, h = c // 2, c % 2
        pages = np.arange(h, 16, 2)
        own_rows = (pages[:, None] * 256 + np.arange(256)[None, :]).reshape(-1)
        m = dict(shared)
        m["xfull"] = np.ascontiguousarray(x1[b].T)
        m["xq"] = np.ascontiguousarray(x1[b][own_rows].T)
        m["cosq"] = np.ascontiguousarray(cosk_full[own_rows])
        m["sinq"] = np.ascontiguousarray(sink_full[own_rows])
        mb = np.full((128, 32, 4), -20.0, np.float32)
        mq = np.zeros((4, 6, 2, 128), np.float32)
        for p in range(4):
            pg0 = 4 * p + h
            pg1 = 4 * p + 2 + h
            for kvt in range(8 * p + 8):
                kv_lo = kvt * 128
                kv_page = kv_lo // 256
                full0 = kv_page > pg0
                full1 = kv_page > pg1
                if full0 and full1:
                    mb[:, kvt, p] = -1e9
                elif full0 or full1:
                    t = kvt - (8 * p + 2)
                    assert 0 <= t < 6, (p, kvt, h)
                    if full0:
                        mq[p, t, 0, :] = -1e9
                    if full1:
                        mq[p, t, 1, :] = -1e9
        m["mbcol"] = mb
        m["mbq"] = mq
        in_maps.append(m)
    res = run_bass_kernel_spmd(nc, in_maps, core_ids=list(range(N_CORES)))
    _L1_CACHE["res"] = res.results
    x2 = np.empty_like(x1)
    for c in range(N_CORES):
        b, h = c // 2, c % 2
        pages = np.arange(h, 16, 2)
        own_rows = (pages[:, None] * 256 + np.arange(256)[None, :]).reshape(-1)
        x2[b][own_rows] = res.results[c]["x2fm"].T
    return x2




def build_l2():
    nc = bass.Bass("TRN2", target_bir_lowering=False)
    NR = 2048
    xfm = nc.dram_tensor("xfm", [D, NR], F32, kind="ExternalInput").ap()
    ln1g = nc.dram_tensor("ln1g", [8, 128, 1], F32, kind="ExternalInput").ap()
    ln1b = nc.dram_tensor("ln1b", [8, 128, 1], F32, kind="ExternalInput").ap()
    ln2g = nc.dram_tensor("ln2g", [8, 128, 1], F32, kind="ExternalInput").ap()
    ln2b = nc.dram_tensor("ln2b", [8, 128, 1], F32, kind="ExternalInput").ap()
    tmk = nc.dram_tensor("tmk", [8, 128, 1], F32, kind="ExternalInput").ap()
    tmv = nc.dram_tensor("tmv", [8, 128, 1], F32, kind="ExternalInput").ap()
    tmr = nc.dram_tensor("tmr", [8, 128, 1], F32, kind="ExternalInput").ap()
    tdec = nc.dram_tensor("tdec", [8, 128, 1], F32, kind="ExternalInput").ap()
    tfir = nc.dram_tensor("tfir", [8, 128, 1], F32, kind="ExternalInput").ap()
    cmk = nc.dram_tensor("cmk", [8, 128, 1], F32, kind="ExternalInput").ap()
    cmr = nc.dram_tensor("cmr", [8, 128, 1], F32, kind="ExternalInput").ap()
    wkT = nc.dram_tensor("wkT", [D, D], F32, kind="ExternalInput").ap()
    wvT = nc.dram_tensor("wvT", [D, D], F32, kind="ExternalInput").ap()
    wrT = nc.dram_tensor("wrT", [D, D], F32, kind="ExternalInput").ap()
    woT = nc.dram_tensor("woT", [D, D], F32, kind="ExternalInput").ap()
    wckT = nc.dram_tensor("wckT", [D, 4096], F32, kind="ExternalInput").ap()
    wcvT = nc.dram_tensor("wcvT", [4096, D], F32, kind="ExternalInput").ap()
    wcrT = nc.dram_tensor("wcrT", [D, D], F32, kind="ExternalInput").ap()
    gwT = nc.dram_tensor("gwT", [D, 8], F32, kind="ExternalInput").ap()
    ebrow = nc.dram_tensor("ebrow", [1, 8], F32, kind="ExternalInput").ap()
    few1 = nc.dram_tensor("few1", [8, D, 2816], F32, kind="ExternalInput").ap()
    few2 = nc.dram_tensor("few2", [8, 2816, D], F32, kind="ExternalInput").ap()
    few3 = nc.dram_tensor("few3", [8, D, 2816], F32, kind="ExternalInput").ap()

    x3fm = nc.dram_tensor("x3fm", [D, NR], F32, kind="ExternalOutput").ap()

    xn_dram = nc.dram_tensor("l2xn", [D, NR], BF16, kind="Internal").ap()
    xn32_dram = nc.dram_tensor("l2xn32", [D, NR], F32, kind="Internal").ap()
    att_dram = nc.dram_tensor("l2att", [D, NR], F32, kind="Internal").ap()

    with ExitStack() as ctx:
        sems = std_sems(nc, ctx)
        block = ctx.enter_context(nc.Block())
        E = Emit(nc, block, sems)
        C = emit_consts(E, ctx, nc)
        register_fconsts(E, ctx, nc, [LN_EPS, -0.5, 1.0 / D])

        P8 = {}
        for nm, ap_ in (("ln1g", ln1g), ("ln1b", ln1b), ("ln2g", ln2g),
                        ("ln2b", ln2b), ("tmk", tmk), ("tmv", tmv),
                        ("tmr", tmr), ("tdec", tdec), ("tfir", tfir),
                        ("cmk", cmk), ("cmr", cmr)):
            t = ctx.enter_context(nc.sbuf_tensor("p_" + nm, [128, 8], F32))
            for d in range(8):
                E.dma("sync", t[:, d:d + 1], ap_[d])
            P8[nm] = t
        drv = ctx.enter_context(nc.sbuf_tensor("p_drv", [128, 8, 4], F32))
        E.wait("scalar", "LD")
        E.op("scalar", "activation", drv[:, :, 0], P8["tdec"][:, :], AF.Exp,
             sem="SE")
        E.wait("vector", "SE")
        E.op("vector", "tensor_scalar", drv[:, :, 0], drv[:, :, 0], -1.0, None,
             op0=ALU.mult, sem="VE")
        E.wait("vector", "LD")
        E.op("vector", "tensor_scalar", drv[:, :, 1], P8["tmk"][:, :], -1.0,
             1.0, op0=ALU.mult, op1=ALU.add)
        E.op("vector", "tensor_scalar", drv[:, :, 2], P8["tmv"][:, :], -1.0,
             1.0, op0=ALU.mult, op1=ALU.add)
        E.op("vector", "tensor_scalar", drv[:, :, 3], P8["tmr"][:, :], -1.0,
             1.0, op0=ALU.mult, op1=ALU.add, sem="VE")
        E.barrier()
        E.flush()

        with (
            nc.sbuf_tensor("xt", [128, 8, 512], F32) as xt,
            nc.sbuf_tensor("sq", [128, 8, 512], F32) as sq,
            nc.sbuf_tensor("xnb", [128, 8, 512], BF16) as xnb,
            nc.sbuf_tensor("strow", [1, 512, 4], F32) as strow,
            nc.psum_tensor("psA", [1, 512], F32) as psA,
            nc.psum_tensor("psB", [1, 512], F32) as psB,
            nc.psum_tensor("psb", [128, 1024], F32) as psb,
        ):
            for j in range(4):
                for d in range(8):
                    E.dma("sync", xt[:, d, :],
                          xfm[d * 128:(d + 1) * 128, j * 512:(j + 1) * 512])
                E.wait("scalar", "LD")
                E.wait("tensor", "LD")
                xts = [xt[:, d, :] for d in range(8)]
                sqs = [sq[:, d, :] for d in range(8)]
                te = emit_ln_fm(E, C, xts, sqs, psA, psB, psb, strow, D)
                E.wait("vector", "TE", te)
                for d in range(8):
                    E.op("vector", "tensor_sub", sq[:, d, :], xt[:, d, :],
                         psb[:, 0:512], sem="VE" if d == 7 else None)
                E.wait("vector", "VE")
                for d in range(8):
                    E.op("vector", "tensor_mul", sq[:, d, :], sq[:, d, :],
                         psb[:, 512:1024], sem="VE" if d == 7 else None)
                E.wait("vector", "VE")
                for d in range(8):
                    E.op("vector", "tensor_copy", xnb[:, d, :], sq[:, d, :],
                         sem="VE" if d == 7 else None)
                E.wait("sync", "VE")
                for d in range(8):
                    E.dma("sync", xn_dram[d * 128:(d + 1) * 128,
                                          j * 512:(j + 1) * 512],
                          xnb[:, d, :], sem="ST")
                    E.dma("sync", xn32_dram[d * 128:(d + 1) * 128,
                                            j * 512:(j + 1) * 512],
                          sq[:, d, :], sem="ST")
                E.wait("scalar", "ST")
                E.wait("vector", "ST")
                E.wait("tensor", "VE")
            E.wait("sync", "ST")
            E.barrier()
            E.flush()

        kvr_ctx = ExitStack()
        kf = kvr_ctx.enter_context(nc.sbuf_tensor("l2k", [128, 8, NR], BF16))
        vf = kvr_ctx.enter_context(nc.sbuf_tensor("l2v", [128, 8, NR], BF16))
        rf = kvr_ctx.enter_context(nc.sbuf_tensor("l2r", [128, 8, NR], BF16))
        with (
            nc.sbuf_tensor("xx", [128, 8, NR], BF16) as xx,
        ):
          with (
            nc.sbuf_tensor("xt2", [128, 8, 512], F32) as xt,
            nc.sbuf_tensor("sq2", [128, 8, 512], F32) as sq,
            nc.sbuf_tensor("strow2", [1, 512, 4], F32) as strow,
            nc.psum_tensor("psA2", [1, 512], F32) as psA,
            nc.psum_tensor("psB2", [1, 512], F32) as psB,
            nc.psum_tensor("psb2", [128, 1024], F32) as psb,
          ):
            for j in range(4):
                for d in range(8):
                    E.dma("sync", xt[:, d, :],
                          xn32_dram[d * 128:(d + 1) * 128,
                                    j * 512:(j + 1) * 512])
                E.wait("scalar", "LD")
                E.wait("tensor", "LD")
                xts = [xt[:, d, :] for d in range(8)]
                sqs = [sq[:, d, :] for d in range(8)]
                te = emit_ln_fm(E, C, xts, sqs, psA, psB, psb, strow, D)
                E.wait("vector", "TE", te)
                for d in range(8):
                    E.op("vector", "tensor_sub", sq[:, d, :], xt[:, d, :],
                         psb[:, 0:512], sem="VE" if d == 7 else None)
                E.wait("vector", "VE")
                for d in range(8):
                    E.op("vector", "tensor_mul", sq[:, d, :], sq[:, d, :],
                         psb[:, 512:1024], sem="VE" if d == 7 else None)
                E.wait("vector", "VE")
                for d in range(8):
                    E.op("vector", "tensor_scalar",
                         xx[:, d, j * 512:(j + 1) * 512], sq[:, d, :],
                         P8["ln1g"][:, d:d + 1], P8["ln1b"][:, d:d + 1],
                         op0=ALU.mult, op1=ALU.add,
                         sem="VE" if d == 7 else None)
                E.wait("tensor", "VE")
                E.wait("scalar", "VE")
                E.wait("sync", "VE")
            E.barrier()
            E.flush()
          with (
            nc.sbuf_tensor("mixb", [128, 8, NR], BF16) as mixb,
            nc.sbuf_tensor("wt2", [128, 8, 1024], BF16) as wt2,
            nc.psum_tensor("psm2", [128, 2048], F32) as psm,
          ):
            for mixsel, dst, act in (("tmk", kf, None), ("tmv", vf, None),
                                     ("tmr", rf, AF.Sigmoid)):
                omc = {"tmk": 1, "tmv": 2, "tmr": 3}[mixsel]
                wsrc = {"tmk": wkT, "tmv": wvT, "tmr": wrT}[mixsel]
                for d in range(8):
                    E.op("vector", "tensor_scalar", mixb[:, d, 0:512],
                         xx[:, d, 0:512], P8[mixsel][:, d:d + 1], None,
                         op0=ALU.mult, sem="VE" if d == 7 else None)
                E.wait("vector", "VE")
                for d in range(8):
                    E.op("vector", "tensor_scalar", mixb[:, d, 512:NR],
                         xx[:, d, 512:NR], P8[mixsel][:, d:d + 1], None,
                         op0=ALU.mult, sem="VE" if d == 7 else None)
                E.wait("vector", "VE")
                for d in range(8):
                    E.op("vector", "scalar_tensor_tensor", mixb[:, d, 512:NR],
                         xx[:, d, 0:NR - 512], drv[:, d:d + 1, omc],
                         mixb[:, d, 512:NR], op0=ALU.mult, op1=ALU.add,
                         sem="VE" if d == 7 else None)
                ve = E.cnt["VE"]
                E.wait("gpsimd", "TE")
                for d in range(8):
                    E.dma("gpsimd", wt2[:, d, :],
                          wsrc[d * 128:(d + 1) * 128, :], sem="LD2")
                E.wait("tensor", "LD2")
                E.wait("tensor", "VE", ve)
                evkey = "VE" if act is None else "SE"
                ev_done = {}
                it = 0
                for ot in range(8):
                    for jc in range(4):
                        sl = psm[:, (it % 4) * 512:(it % 4) * 512 + 512]
                        if it >= 4:
                            E.wait("tensor", evkey, ev_done[it - 4])
                        for d in range(8):
                            E.op("tensor", "matmul", sl,
                                 wt2[:, d, ot * 128:(ot + 1) * 128],
                                 mixb[:, d, jc * 512:(jc + 1) * 512],
                                 start=(d == 0), stop=(d == 7),
                                 sem="TE" if d == 7 else None)
                        if act is None:
                            E.wait("vector", "TE")
                            E.op("vector", "tensor_copy",
                                 dst[:, ot, jc * 512:(jc + 1) * 512], sl,
                                 sem="VE")
                            ev_done[it] = E.cnt["VE"]
                        else:
                            E.wait("scalar", "TE")
                            E.op("scalar", "activation",
                                 dst[:, ot, jc * 512:(jc + 1) * 512], sl,
                                 act, sem="SE")
                            ev_done[it] = E.cnt["SE"]
                        it += 1
                E.wait("tensor", evkey)
            E.barrier()
            E.flush()

        with (
            nc.sbuf_tensor("aa", [128, 8, 512], F32) as aa,
            nc.sbuf_tensor("bb", [128, 8, 512], F32) as bb,
            nc.sbuf_tensor("pp", [128, 8, 512], F32) as pp,
            nc.sbuf_tensor("s1", [128, 8, 512], F32) as s1,
            nc.sbuf_tensor("s2", [128, 8, 512], F32) as s2,
            nc.sbuf_tensor("s3", [128, 8, 512], F32) as s3,
        ):
            def vsem(meth, *a, **kw):
                E.op("vector", meth, *a, sem="VE", **kw)
                E.wait("vector", "VE")
                E.wait("scalar", "VE")

            E.op("vector", "memset", aa[:, :, :], 0.0)
            E.op("vector", "memset", bb[:, :, :], 0.0)
            E.op("vector", "memset", pp[:, :, :], -1e38, sem="VE")
            E.wait("vector", "VE")
            E.wait("scalar", "VE")
            for t in range(4):
                ks = kf[:, :, t * 512:(t + 1) * 512]
                vs = vf[:, :, t * 512:(t + 1) * 512]
                rs = rf[:, :, t * 512:(t + 1) * 512]
                A = aa[:, :, :]
                Bb = bb[:, :, :]
                P = pp[:, :, :]
                S1 = s1[:, :, :]
                S2 = s2[:, :, :]
                S3 = s3[:, :, :]
                for d in range(8):
                    E.op("vector", "tensor_scalar", s1[:, d, :], ks[:, d, :],
                         P8["tfir"][:, d:d + 1], None, op0=ALU.add,
                         sem="VE" if d == 7 else None)
                E.wait("vector", "VE")
                E.wait("scalar", "VE")
                vsem("tensor_tensor", S2, P, S1, op=ALU.max)
                E.op("vector", "tensor_sub", S1, S1, S2)
                E.op("vector", "tensor_sub", S3, P, S2, sem="VE")
                E.wait("scalar", "VE")
                E.wait("vector", "VE")
                E.op("scalar", "activation", S1, S1, AF.Exp)
                E.op("scalar", "activation", S3, S3, AF.Exp, sem="SE")
                E.wait("vector", "SE")
                vsem("tensor_mul", S2, S3, Bb)
                vsem("tensor_add", S2, S2, S1)
                vsem("reciprocal", S2, S2)
                E.op("vector", "tensor_mul", S3, S3, A)
                E.op("vector", "tensor_mul", S1, S1, vs, sem="VE")
                E.wait("vector", "VE")
                vsem("tensor_add", S3, S3, S1)
                vsem("tensor_mul", S3, S3, S2)
                vsem("tensor_mul", rs, rs, S3)
                for d in range(8):
                    E.op("vector", "tensor_scalar", s1[:, d, :], pp[:, d, :],
                         drv[:, d:d + 1, 0], None, op0=ALU.add,
                         sem="VE" if d == 7 else None)
                E.wait("vector", "VE")
                E.wait("scalar", "VE")
                vsem("tensor_tensor", S2, S1, ks, op=ALU.max)
                E.op("vector", "tensor_sub", S1, S1, S2)
                E.op("vector", "tensor_tensor", S3, ks, S2,
                     op=ALU.subtract, sem="VE")
                E.wait("scalar", "VE")
                E.wait("vector", "VE")
                E.op("scalar", "activation", S1, S1, AF.Exp)
                E.op("scalar", "activation", S3, S3, AF.Exp, sem="SE")
                E.wait("vector", "SE")
                E.op("vector", "tensor_mul", A, A, S1)
                E.op("vector", "tensor_mul", Bb, Bb, S1, sem="VE")
                E.wait("vector", "VE")
                vsem("tensor_mul", S1, S3, vs)
                E.op("vector", "tensor_add", A, A, S1)
                E.op("vector", "tensor_add", Bb, Bb, S3, sem="VE")
                E.wait("vector", "VE")
                E.wait("scalar", "VE")
                vsem("tensor_copy", P, S2)
            E.barrier()
            E.flush()

        xi_ctx = ExitStack()
        xi = xi_ctx.enter_context(
            nc.sbuf_tensor("l2xi", [128, 8, NR], BF16, side="right"))
        with (
            nc.sbuf_tensor("wt3", [128, 8, 1024], BF16) as wt3,
            nc.sbuf_tensor("xnt2", [128, 4, 512], BF16) as xnt2,
            nc.psum_tensor("psm4", [128, 2048], F32) as psm,
        ):
            for d in range(8):
                E.dma("gpsimd", wt3[:, d, :], woT[d * 128:(d + 1) * 128, :],
                      sem="LD2")
            E.wait("tensor", "LD2")
            ev4 = {}
            ld4 = {}
            it = 0
            for ot in range(8):
                for jc in range(4):
                    sl = psm[:, (it % 4) * 512:(it % 4) * 512 + 512]
                    xb = xnt2[:, it % 4, :]
                    deng = "sync" if it % 2 == 0 else "gpsimd"
                    dsem = "LD" if it % 2 == 0 else "LD2"
                    if it >= 4:
                        E.wait(deng, "VE", ev4[it - 4])
                    E.dma(deng, xb,
                          xn_dram[ot * 128:(ot + 1) * 128,
                                  jc * 512:(jc + 1) * 512], sem=dsem)
                    ld4[it] = (dsem, E.cnt[dsem])
                    if it >= 4:
                        E.wait("tensor", "VE", ev4[it - 4])
                    for d in range(8):
                        E.op("tensor", "matmul", sl,
                             wt3[:, d, ot * 128:(ot + 1) * 128],
                             rf[:, d, jc * 512:(jc + 1) * 512],
                             start=(d == 0), stop=(d == 7),
                             sem="TE" if d == 7 else None)
                    E.wait("vector", ld4[it][0], ld4[it][1])
                    E.wait("vector", "TE")
                    E.op("vector", "tensor_add",
                         xi[:, ot, jc * 512:(jc + 1) * 512], sl, xb,
                         sem="VE")
                    ev4[it] = E.cnt["VE"]
                    it += 1
            E.wait("tensor", "VE")
            E.wait("sync", "VE")
            E.wait("gpsimd", "VE")
            E.barrier()
            E.flush()
        kvr_ctx.close()

        with (
            nc.sbuf_tensor("yy", [128, 8, NR], BF16) as yy,
        ):
            with (
                nc.sbuf_tensor("sq5", [128, 8, 512], F32) as sq,
                nc.sbuf_tensor("strow5", [1, 512, 4], F32) as strow,
                nc.psum_tensor("psA5", [1, 512], F32) as psA,
                nc.psum_tensor("psB5", [1, 512], F32) as psB,
                nc.psum_tensor("psb5", [128, 1024], F32) as psb,
            ):
                for j in range(4):
                    xts = [xi[:, d, j * 512:(j + 1) * 512] for d in range(8)]
                    sqs = [sq[:, d, :] for d in range(8)]
                    E.wait("scalar", "VE")
                    E.wait("tensor", "VE")
                    te = emit_ln_fm(E, C, xts, sqs, psA, psB, psb, strow, D,
                                    ones=C["ones_colb"])
                    E.wait("vector", "TE", te)
                    for d in range(8):
                        E.op("vector", "tensor_sub", sq[:, d, :], xts[d],
                             psb[:, 0:512], sem="VE" if d == 7 else None)
                    E.wait("vector", "VE")
                    for d in range(8):
                        E.op("vector", "tensor_mul", sq[:, d, :], sq[:, d, :],
                             psb[:, 512:1024], sem="VE" if d == 7 else None)
                    E.wait("vector", "VE")
                    for d in range(8):
                        E.op("vector", "tensor_scalar",
                             yy[:, d, j * 512:(j + 1) * 512], sq[:, d, :],
                             P8["ln2g"][:, d:d + 1], P8["ln2b"][:, d:d + 1],
                             op0=ALU.mult, op1=ALU.add,
                             sem="VE" if d == 7 else None)
                    E.wait("tensor", "VE")
                    E.wait("scalar", "VE")
                E.barrier()
                E.flush()

            with (
                nc.sbuf_tensor("mixc", [128, 8, NR], BF16) as mixc,
                nc.sbuf_tensor("rrall", [128, 8, NR], BF16) as rrall,
                nc.sbuf_tensor("omk", [128, 8, 2], F32) as omk,
            ):
                E.op("vector", "tensor_scalar", omk[:, :, 0], P8["cmk"][:, :],
                     -1.0, 1.0, op0=ALU.mult, op1=ALU.add)
                E.op("vector", "tensor_scalar", omk[:, :, 1], P8["cmr"][:, :],
                     -1.0, 1.0, op0=ALU.mult, op1=ALU.add, sem="VE")
                E.wait("vector", "VE")
                with (
                    nc.sbuf_tensor("mixr2", [128, 8, NR], BF16) as mixr2,
                    nc.sbuf_tensor("wcr_s", [128, 8, 1024], BF16) as wcr_s,
                    nc.psum_tensor("ps_r", [128, 2048], F32) as ps_r,
                ):
                    for d in range(8):
                        E.op("vector", "tensor_scalar", mixr2[:, d, 0:NR],
                             yy[:, d, 0:NR], P8["cmr"][:, d:d + 1], None,
                             op0=ALU.mult, sem="VE" if d == 7 else None)
                    E.wait("vector", "VE")
                    for d in range(8):
                        E.op("vector", "scalar_tensor_tensor",
                             mixr2[:, d, 512:NR], yy[:, d, 0:NR - 512],
                             omk[:, d:d + 1, 1], mixr2[:, d, 512:NR],
                             op0=ALU.mult, op1=ALU.add,
                             sem="VE" if d == 7 else None)
                    E.wait("vector", "VE")
                    E.wait("tensor", "VE")
                    for d in range(8):
                        E.dma("gpsimd", wcr_s[:, d, :],
                              wcrT[d * 128:(d + 1) * 128, :], sem="LD2")
                    E.wait("tensor", "LD2")
                    se_done = {}
                    it = 0
                    for jc in range(4):
                        for ot in range(8):
                            sl = ps_r[:, (it % 4) * 512:(it % 4) * 512 + 512]
                            if it >= 4:
                                E.wait("tensor", "SE", se_done[it - 4])
                            for d in range(8):
                                E.op("tensor", "matmul", sl,
                                     wcr_s[:, d, ot * 128:(ot + 1) * 128],
                                     mixr2[:, d, jc * 512:(jc + 1) * 512],
                                     start=(d == 0), stop=(d == 7),
                                     sem="TE" if d == 7 else None)
                            E.wait("scalar", "TE")
                            E.op("scalar", "activation",
                                 rrall[:, ot, jc * 512:(jc + 1) * 512], sl,
                                 AF.Sigmoid, sem="SE")
                            se_done[it] = E.cnt["SE"]
                            it += 1
                    E.wait("tensor", "SE")
                    E.barrier()
                    E.flush()
                for d in range(8):
                    E.op("vector", "tensor_scalar", mixc[:, d, 0:NR],
                         yy[:, d, 0:NR], P8["cmk"][:, d:d + 1], None,
                         op0=ALU.mult, sem="VE" if d == 7 else None)
                E.wait("vector", "VE")
                for d in range(8):
                    E.op("vector", "scalar_tensor_tensor", mixc[:, d, 512:NR],
                         yy[:, d, 0:NR - 512], omk[:, d:d + 1, 0],
                         mixc[:, d, 512:NR], op0=ALU.mult, op1=ALU.add,
                         sem="VE" if d == 7 else None)
                E.wait("vector", "VE")
                E.wait("tensor", "VE")
                with (
                    nc.sbuf_tensor("hk", [128, 8, 512], BF16) as hk,
                    nc.sbuf_tensor("wck_s", [128, 8, 1024], BF16) as wck_s,
                    nc.sbuf_tensor("wcv_s", [128, 8, 1024], BF16) as wcv_s,
                    nc.sbuf_tensor("sq8", [128, 4, 512], F32) as sq8,
                    nc.sbuf_tensor("aev2", [128, 4, 512], F32) as aev2,
                    nc.sbuf_tensor("vvac", [128, 8, 512], F32) as vvac,
                    nc.psum_tensor("ps_k", [128, 2048], F32) as ps_k,
                    nc.psum_tensor("ps_v2", [128, 2048], F32) as ps_v2,
                ):
                    for jc in range(4):
                        for hg in range(4):
                            E.wait("gpsimd", "TE")
                            for d in range(8):
                                E.dma("gpsimd", wck_s[:, d, :],
                                      wckT[d * 128:(d + 1) * 128,
                                           hg * 1024:(hg + 1) * 1024],
                                      sem="LD2")
                            for ht in range(8):
                                E.dma("gpsimd", wcv_s[:, ht, :],
                                      wcvT[hg * 1024 + ht * 128:
                                           hg * 1024 + (ht + 1) * 128, :],
                                      sem="LD2")
                            E.wait("tensor", "LD2")
                            for ht in range(8):
                                sl = (ps_k[:, ht * 512:(ht + 1) * 512]
                                      if ht < 4 else
                                      ps_v2[:, (ht - 4) * 512:(ht - 3) * 512])
                                for d in range(8):
                                    E.op("tensor", "matmul", sl,
                                         wck_s[:, d, ht * 128:(ht + 1) * 128],
                                         mixc[:, d, jc * 512:(jc + 1) * 512],
                                         start=(d == 0), stop=(d == 7),
                                         sem="TE" if (ht == 7 and d == 7)
                                         else None)
                            E.wait("vector", "TE")
                            for hs in range(2):
                                if hs == 1:
                                    E.wait("vector", "SE")
                                for hq in range(4):
                                    ht = hs * 4 + hq
                                    sl = (ps_k[:, ht * 512:(ht + 1) * 512]
                                          if ht < 4 else
                                          ps_v2[:, (ht - 4) * 512:
                                                (ht - 3) * 512])
                                    E.op("vector", "tensor_scalar",
                                         sq8[:, hq, :], sl, 0.0, None,
                                         op0=ALU.max,
                                         sem="VE" if hq == 3 else None)
                                E.wait("scalar", "VE")
                                for hq in range(4):
                                    ht = hs * 4 + hq
                                    E.op("scalar", "activation",
                                         hk[:, ht, :], sq8[:, hq, :],
                                         AF.Square,
                                         sem="SE" if hq == 3 else None)
                            E.wait("tensor", "SE")
                            E.wait("vector", "SE")
                            dslots = [ps_k[:, 0:512], ps_k[:, 512:1024],
                                      ps_v2[:, 0:512], ps_v2[:, 512:1024]]
                            dv = {}
                            for dt in range(8):
                                sl = dslots[dt % 4]
                                if dt >= 4:
                                    E.wait("tensor", "VE", dv[dt - 4])
                                for ht in range(8):
                                    E.op("tensor", "matmul", sl,
                                         wcv_s[:, ht, dt * 128:(dt + 1) * 128],
                                         hk[:, ht, :],
                                         start=(ht == 0), stop=(ht == 7),
                                         sem="TE" if ht == 7 else None)
                                E.wait("vector", "TE")
                                if hg == 0:
                                    E.op("vector", "tensor_copy",
                                         vvac[:, dt, :], sl, sem="VE")
                                else:
                                    E.op("vector", "tensor_add",
                                         vvac[:, dt, :], vvac[:, dt, :], sl,
                                         sem="VE")
                                dv[dt] = E.cnt["VE"]
                            E.wait("tensor", "VE")
                        E.wait("vector", "VE")
                        for ds in range(2):
                            if ds == 1:
                                E.wait("vector", "ST")
                            for dq in range(4):
                                dt = ds * 4 + dq
                                E.op("vector", "tensor_mul", aev2[:, dq, :],
                                     vvac[:, dt, :],
                                     rrall[:, dt, jc * 512:(jc + 1) * 512],
                                     sem="VE" if dq == 3 else None)
                            E.wait("vector", "VE")
                            for dq in range(4):
                                dt = ds * 4 + dq
                                E.op("vector", "tensor_add", aev2[:, dq, :],
                                     aev2[:, dq, :],
                                     xi[:, dt, jc * 512:(jc + 1) * 512],
                                     sem="VE" if dq == 3 else None)
                            E.wait("sync", "VE")
                            for dq in range(4):
                                dt = ds * 4 + dq
                                E.dma("sync",
                                      att_dram[dt * 128:(dt + 1) * 128,
                                               jc * 512:(jc + 1) * 512],
                                      aev2[:, dq, :], sem="ST")
                        E.wait("vector", "ST")
                    E.wait("sync", "ST")
                    E.barrier()
                    E.flush()
        xi_ctx.close()

        acc = ctx.enter_context(nc.sbuf_tensor("l2acc", [128, 8, NR], F32))
        cfm = ctx.enter_context(nc.sbuf_tensor("l2cfm", [8, NR], F32))
        selc8 = ctx.enter_context(nc.sbuf_tensor("l2sel", [8, 8 * 128], F32))
        with (
            nc.sbuf_tensor("gws", [128, 8, 8], F32) as gws,
            nc.sbuf_tensor("xn32c", [128, 8, 512], F32) as xn32c,
            nc.sbuf_tensor("lgf", [8, NR], F32) as lgf,
            nc.sbuf_tensor("ltm", [128, 16, 8], F32) as ltm,
            nc.sbuf_tensor("sct", [128, 16, 8], F32) as sct,
            nc.sbuf_tensor("btm", [128, 16, 8], F32) as btm,
            nc.sbuf_tensor("eqt", [128, 16, 8], F32) as eqt,
            nc.sbuf_tensor("top8", [128, 16, 8], F32) as top8,
            nc.sbuf_tensor("smx", [128, 16, 2], F32) as smx,
            nc.sbuf_tensor("ebb", [128, 8], F32) as ebb,
            nc.psum_tensor("ps_l", [8, 512], F32) as ps_l,
            nc.psum_tensor("ps_t", [128, 8], F32) as ps_t,
            nc.psum_tensor("ps_t2", [8, 128], F32) as ps_t2,
            nc.psum_tensor("ps_e", [128, 8], F32) as ps_e,
        ):
            E.op("gpsimd", "iota", selc8[:], [[1, 8], [0, 128]], sem="GE",
                 base=0, channel_multiplier=-1,
                 allow_small_or_imprecise_dtypes=True)
            E.wait("vector", "GE")
            E.op("vector", "tensor_scalar", selc8[:], selc8[:], 0.0, 1.0,
                 op0=ALU.is_equal, op1=ALU.mult, sem="VE")
            for d in range(8):
                E.dma("sync", gws[:, d, :], gwT[d * 128:(d + 1) * 128, :])
            E.dma("sync", ebb[0:1, :], ebrow[0:1, :])
            E.wait("tensor", "LD")
            E.op("tensor", "matmul", ps_e[:, :], C["ones_row"][0:1, :],
                 ebb[0:1, :], start=True, stop=True, sem="TE")
            E.wait("vector", "TE")
            E.op("vector", "tensor_copy", ebb[:, :], ps_e[:, :], sem="VE")
            E.wait("tensor", "VE")
            for jc in range(4):
                for d in range(8):
                    E.dma("sync", xn32c[:, d, :],
                          xn32_dram[d * 128:(d + 1) * 128,
                                    jc * 512:(jc + 1) * 512])
                E.wait("tensor", "LD")
                for d in range(8):
                    E.op("tensor", "matmul", ps_l[:, :], gws[:, d, :],
                         xn32c[:, d, :], start=(d == 0), stop=(d == 7),
                         sem="TE" if d == 7 else None)
                E.wait("vector", "TE")
                E.op("vector", "tensor_copy",
                     lgf[0:8, jc * 512:(jc + 1) * 512], ps_l[:, :], sem="VE")
                E.wait("tensor", "VE")
                E.wait("sync", "TE")
            for g in range(16):
                E.op("tensor", "transpose", ps_t[:, :],
                     lgf[0:8, g * 128:(g + 1) * 128], C["identf"][0:8, 0:8],
                     sem="TE")
                E.wait("vector", "TE")
                E.op("vector", "tensor_copy", ltm[:, g, :], ps_t[:, :],
                     sem="VE")
                E.wait("tensor", "VE")
            E.wait("vector", "VE")
            for g in range(16):
                E.op("vector", "max", top8[:, g, :], ltm[:, g, :],
                     sem="VE" if g == 15 else None)
            E.wait("vector", "VE")
            for g in range(16):
                E.op("vector", "tensor_scalar", sct[:, g, :], ltm[:, g, :],
                     top8[:, g, 0:1], None, op0=ALU.subtract,
                     sem="VE" if g == 15 else None)
            E.wait("vector", "VE")
            E.wait("scalar", "VE")
            for g in range(16):
                E.op("scalar", "activation", sct[:, g, :], sct[:, g, :],
                     AF.Exp, sem="SE" if g == 15 else None)
            E.wait("vector", "SE")
            for g in range(16):
                E.op("vector", "tensor_reduce", smx[:, g, 0:1], sct[:, g, :],
                     axis=mybir.AxisListType.X, op=ALU.add,
                     sem="VE" if g == 15 else None)
            E.wait("vector", "VE")
            for g in range(16):
                E.op("vector", "reciprocal", smx[:, g, 1:2], smx[:, g, 0:1],
                     sem="VE" if g == 15 else None)
            E.wait("vector", "VE")
            for g in range(16):
                E.op("vector", "tensor_scalar", sct[:, g, :], sct[:, g, :],
                     smx[:, g, 1:2], None, op0=ALU.mult,
                     sem="VE" if g == 15 else None)
            E.wait("vector", "VE")
            for g in range(16):
                E.op("vector", "tensor_add", btm[:, g, :], sct[:, g, :],
                     ebb[:, :], sem="VE" if g == 15 else None)
            E.wait("vector", "VE")
            for g in range(16):
                E.op("vector", "max", top8[:, g, :], btm[:, g, :],
                     sem="VE" if g == 15 else None)
            E.wait("vector", "VE")
            for g in range(16):
                E.op("vector", "tensor_scalar", eqt[:, g, :], btm[:, g, :],
                     top8[:, g, 0:1], None, op0=ALU.is_equal,
                     sem="VE" if g == 15 else None)
            E.wait("vector", "VE")
            for g in range(16):
                E.op("vector", "tensor_scalar", btm[:, g, :], btm[:, g, :],
                     top8[:, g, 1:2], None, op0=ALU.is_equal,
                     sem="VE" if g == 15 else None)
            E.wait("vector", "VE")
            for g in range(16):
                E.op("vector", "tensor_add", eqt[:, g, :], eqt[:, g, :],
                     btm[:, g, :], sem="VE" if g == 15 else None)
            E.wait("vector", "VE")
            for g in range(16):
                E.op("vector", "tensor_mul", sct[:, g, :], sct[:, g, :],
                     eqt[:, g, :], sem="VE" if g == 15 else None)
            E.wait("vector", "VE")
            E.wait("tensor", "VE")
            for g in range(16):
                E.op("tensor", "transpose", ps_t2[:, :], sct[:, g, :],
                     C["identf"][:], sem="TE")
                E.wait("vector", "TE")
                E.op("vector", "tensor_copy", cfm[0:8, g * 128:(g + 1) * 128],
                     ps_t2[0:8, :], sem="VE")
                E.wait("tensor", "VE")
            E.barrier()
            E.flush()

        with (
            nc.sbuf_tensor("xnc", [128, 8, NR], BF16) as xnc,
            nc.sbuf_tensor("xe", [128, 8, NR], BF16) as xe,
            nc.sbuf_tensor("hhg", [128, 4, NR], BF16) as hhg,
            nc.sbuf_tensor("w13", [128, 8, 1024], BF16) as w13,
            nc.sbuf_tensor("w2g", [128, 4, 1024], BF16) as w2g,
            nc.sbuf_tensor("gsc2", [128, 4, 512], F32) as gsc2,
            nc.psum_tensor("ps_g", [128, 2048], F32) as ps_g,
            nc.psum_tensor("ps_u", [128, 2048], F32) as ps_u,
        ):
            for d in range(8):
                E.dma("sync", xnc[:, d, :], xn_dram[d * 128:(d + 1) * 128, :])
            E.wait("tensor", "LD")
            E.wait("vector", "LD")
            for e in range(8):
                for jc in range(4):
                    E.op("tensor", "matmul", ps_g[:, 0:512],
                         selc8[0:8, e * 128:(e + 1) * 128],
                         cfm[0:8, jc * 512:(jc + 1) * 512],
                         start=True, stop=True, sem="TE")
                    E.wait("vector", "TE")
                    for d in range(8):
                        E.op("vector", "tensor_mul",
                             xe[:, d, jc * 512:(jc + 1) * 512],
                             xnc[:, d, jc * 512:(jc + 1) * 512],
                             ps_g[:, 0:512], sem="VE" if d == 7 else None)
                    E.wait("tensor", "VE")
                    E.wait("scalar", "VE")
                for hg in range(6):
                    h0 = hg * 4
                    nht = min(4, 22 - h0)
                    E.wait("gpsimd", "TE")
                    E.wait("gpsimd", "VE")
                    for d in range(8):
                        E.dma("gpsimd", w13[:, d, 0:nht * 128],
                              few1[e, d * 128:(d + 1) * 128,
                                   h0 * 128:(h0 + nht) * 128], sem="LD2")
                        E.dma("gpsimd", w13[:, d, 512:512 + nht * 128],
                              few3[e, d * 128:(d + 1) * 128,
                                   h0 * 128:(h0 + nht) * 128], sem="LD2")
                    for hi in range(nht):
                        E.dma("gpsimd", w2g[:, hi, :],
                              few2[e, (h0 + hi) * 128:(h0 + hi + 1) * 128, :],
                              sem="LD2")
                    E.wait("tensor", "LD2")
                    subs = []
                    for jc in range(4):
                        for hp in range(0, nht, 2):
                            subs.append((jc, hp, min(2, nht - hp)))
                    ve_done = {}
                    for si, (jc, hp, nh2) in enumerate(subs):
                        par = si % 2
                        if si >= 2:
                            E.wait("tensor", "VE", ve_done[si - 2])
                        for hi in range(nh2):
                            b = par * 2 + hi
                            for d in range(8):
                                E.op("tensor", "matmul",
                                     ps_g[:, b * 512:(b + 1) * 512],
                                     w13[:, d, (hp + hi) * 128:
                                         (hp + hi + 1) * 128],
                                     xe[:, d, jc * 512:(jc + 1) * 512],
                                     start=(d == 0), stop=(d == 7))
                            for d in range(8):
                                E.op("tensor", "matmul",
                                     ps_u[:, b * 512:(b + 1) * 512],
                                     w13[:, d, 512 + (hp + hi) * 128:
                                         512 + (hp + hi + 1) * 128],
                                     xe[:, d, jc * 512:(jc + 1) * 512],
                                     start=(d == 0), stop=(d == 7),
                                     sem="TE" if (hi == nh2 - 1 and d == 7)
                                     else None)
                        E.wait("scalar", "TE")
                        for hi in range(nh2):
                            b = par * 2 + hi
                            E.op("scalar", "activation", gsc2[:, b, :],
                                 ps_g[:, b * 512:(b + 1) * 512], AF.Silu,
                                 sem="SE" if hi == nh2 - 1 else None)
                        E.wait("vector", "SE")
                        for hi in range(nh2):
                            b = par * 2 + hi
                            E.op("vector", "tensor_mul",
                                 hhg[:, hp + hi, jc * 512:(jc + 1) * 512],
                                 gsc2[:, b, :],
                                 ps_u[:, b * 512:(b + 1) * 512],
                                 sem="VE" if hi == nh2 - 1 else None)
                        ve_done[si] = E.cnt["VE"]
                    E.wait("tensor", "VE")
                    E.wait("scalar", "VE")
                    slots = [ps_g[:, 0:512], ps_g[:, 512:1024],
                             ps_u[:, 0:512], ps_u[:, 512:1024]]
                    dve_done = {}
                    it = 0
                    for jc in range(4):
                        for dt in range(8):
                            sl = slots[it % 4]
                            if it >= 4:
                                E.wait("tensor", "VE", dve_done[it - 4])
                            for hi in range(nht):
                                E.op("tensor", "matmul", sl,
                                     w2g[:, hi, dt * 128:(dt + 1) * 128],
                                     hhg[:, hi, jc * 512:(jc + 1) * 512],
                                     start=(hi == 0), stop=(hi == nht - 1),
                                     sem="TE" if hi == nht - 1 else None)
                            E.wait("vector", "TE")
                            a_ap = acc[:, dt, jc * 512:(jc + 1) * 512]
                            if e == 0 and hg == 0:
                                E.op("vector", "tensor_copy", a_ap, sl,
                                     sem="VE")
                            else:
                                E.op("vector", "tensor_add", a_ap, a_ap, sl,
                                     sem="VE")
                            dve_done[it] = E.cnt["VE"]
                            it += 1
                    E.wait("tensor", "VE")
            E.barrier()
            E.flush()

        with (
            nc.sbuf_tensor("xf4", [128, 512], F32) as xf4,
            nc.sbuf_tensor("af4", [128, 512], F32) as af4,
            nc.sbuf_tensor("of4", [128, 512], F32) as of4,
        ):
            for jc in range(4):
                for dt in range(8):
                    E.dma("sync", xf4[:],
                          xfm[dt * 128:(dt + 1) * 128, jc * 512:(jc + 1) * 512])
                    E.dma("sync", af4[:],
                          att_dram[dt * 128:(dt + 1) * 128,
                                   jc * 512:(jc + 1) * 512])
                    E.wait("vector", "LD")
                    E.op("vector", "tensor_add", of4[:], xf4[:], af4[:],
                         sem="VE")
                    E.wait("vector", "VE")
                    E.op("vector", "tensor_add", of4[:], of4[:],
                         acc[:, dt, jc * 512:(jc + 1) * 512], sem="VE")
                    E.wait("sync", "VE")
                    E.dma("sync", x3fm[dt * 128:(dt + 1) * 128,
                                       jc * 512:(jc + 1) * 512], of4[:],
                          sem="ST")
                    E.wait("vector", "ST")
            E.wait("sync", "ST")
            E.barrier()
            E.flush()

    return nc



_L2_CACHE = {}


def run_l2(inputs, x2):
    if "nc" not in _L2_CACHE:
        _L2_CACHE["nc"] = build_l2()
    nc = _L2_CACHE["nc"]
    w = inputs
    shared = dict(
        ln1g=tile_bias(w["L2_attn_ln1_g"]), ln1b=tile_bias(w["L2_attn_ln1_b"]),
        ln2g=tile_bias(w["L2_attn_ln2_g"]), ln2b=tile_bias(w["L2_attn_ln2_b"]),
        tmk=tile_bias(w["L2_attn_time_mix_k"]),
        tmv=tile_bias(w["L2_attn_time_mix_v"]),
        tmr=tile_bias(w["L2_attn_time_mix_r"]),
        tdec=tile_bias(w["L2_attn_time_decay"]),
        tfir=tile_bias(w["L2_attn_time_first"]),
        cmk=tile_bias(w["L2_attn_cmix_k"]), cmr=tile_bias(w["L2_attn_cmix_r"]),
        wkT=np.ascontiguousarray(w["L2_attn_Wk"].T),
        wvT=np.ascontiguousarray(w["L2_attn_Wv"].T),
        wrT=np.ascontiguousarray(w["L2_attn_Wr"].T),
        woT=np.ascontiguousarray(w["L2_attn_Wo"].T),
        wckT=np.ascontiguousarray(w["L2_attn_Wck"].T),
        wcvT=np.ascontiguousarray(w["L2_attn_Wcv"].T),
        wcrT=np.ascontiguousarray(w["L2_attn_Wcr"].T),
        gwT=np.ascontiguousarray(w["L2_ffn_gate_w"].T),
        ebrow=np.ascontiguousarray(w["L2_ffn_expert_bias"][None, :]),
        few1=np.ascontiguousarray(w["L2_ffn_w1"]),
        few2=np.ascontiguousarray(w["L2_ffn_w2"]),
        few3=np.ascontiguousarray(w["L2_ffn_w3"]),
    )
    in_maps = []
    for c in range(N_CORES):
        sl = x2[:, c * 512:(c + 1) * 512, :]
        rows = sl.reshape(4 * 512, D)
        m = dict(shared)
        m["xfm"] = np.ascontiguousarray(rows.T)
        in_maps.append(m)
    res = run_bass_kernel_spmd(nc, in_maps, core_ids=list(range(N_CORES)))
    _L2_CACHE["res"] = res.results
    x3 = np.empty_like(x2)
    for c in range(N_CORES):
        out = res.results[c]["x3fm"].T.reshape(4, 512, D)
        x3[:, c * 512:(c + 1) * 512, :] = out
    return x3




def build_l3a():
    nc = bass.Bass("TRN2", target_bir_lowering=False)
    xfm = nc.dram_tensor("xfm", [D, T], F32, kind="ExternalInput").ap()
    w_xc = nc.dram_tensor("w_xc", [D, 2048], F32, kind="ExternalInput").ap()
    w_z = nc.dram_tensor("w_z", [D, 1024], F32, kind="ExternalInput").ap()
    convw = nc.dram_tensor("convw", [16, 128, 4], F32, kind="ExternalInput").ap()
    convb = nc.dram_tensor("convb", [16, 128, 1], F32, kind="ExternalInput").ap()
    w_x = nc.dram_tensor("w_x", [2048, 32], F32, kind="ExternalInput").ap()
    w_dt = nc.dram_tensor("w_dt", [16, 1024], F32, kind="ExternalInput").ap()
    bdt = nc.dram_tensor("bdt", [8, 128, 1], F32, kind="ExternalInput").ap()
    alog = nc.dram_tensor("alog", [8, 128, 16], F32, kind="ExternalInput").ap()
    dvec = nc.dram_tensor("dvec", [8, 128, 1], F32, kind="ExternalInput").ap()

    ygz = nc.dram_tensor("ygz", [1024, T], BF16, kind="ExternalOutput").ap()

    xn_dram = nc.dram_tensor("l3xn", [D, T], BF16, kind="Internal").ap()
    xc_dram = nc.dram_tensor("l3xc", [2048, T], BF16, kind="Internal").ap()
    z_dram = nc.dram_tensor("l3z", [1024, T], BF16, kind="Internal").ap()
    xs_dram = nc.dram_tensor("l3xs", [2048, T], BF16, kind="Internal").ap()
    delta_dram = nc.dram_tensor("l3delta", [1024, T], F32, kind="Internal").ap()
    bmr_dram = nc.dram_tensor("l3bmr", [16, T], F32, kind="Internal").ap()

    with ExitStack() as ctx:
        sems = std_sems(nc, ctx)
        block = ctx.enter_context(nc.Block())
        E = Emit(nc, block, sems)
        C = emit_consts(E, ctx, nc)
        register_fconsts(E, ctx, nc, [LN_EPS, -0.5, 1.0 / D, -1.0])

        bdt_s = ctx.enter_context(nc.sbuf_tensor("bdt_s", [128, 8], F32))
        dv_s = ctx.enter_context(nc.sbuf_tensor("dv_s", [128, 8], F32))
        acol = ctx.enter_context(nc.sbuf_tensor("acol", [128, 8, 16], F32))
        cw_s = ctx.enter_context(nc.sbuf_tensor("cw_s", [128, 16, 4], F32))
        cb_s = ctx.enter_context(nc.sbuf_tensor("cb_s", [128, 16], F32))

        for d8 in range(8):
            E.dma("sync", bdt_s[:, d8:d8 + 1], bdt[d8])
            E.dma("sync", dv_s[:, d8:d8 + 1], dvec[d8])
            E.dma("sync", acol[:, d8, :], alog[d8])
        for nt in range(16):
            E.dma("sync", cw_s[:, nt, :], convw[nt])
            E.dma("sync", cb_s[:, nt:nt + 1], convb[nt])
        E.wait("scalar", "LD")
        E.op("scalar", "activation", acol[:, :, :], acol[:, :, :], AF.Exp,
             sem="SE")
        E.wait("vector", "SE")
        E.op("vector", "tensor_scalar", acol[:, :, :], acol[:, :, :], -1.0,
             None, op0=ALU.mult, sem="VE")
        E.barrier()
        E.flush()

        with (
            nc.sbuf_tensor("xt", [128, 8, 512], F32) as xt,
            nc.sbuf_tensor("sq", [128, 8, 512], F32) as sq,
            nc.sbuf_tensor("xnb", [128, 8, 512], BF16) as xnb,
            nc.sbuf_tensor("strow", [1, 512, 4], F32) as strow,
            nc.psum_tensor("psA", [1, 512], F32) as psA,
            nc.psum_tensor("psB", [1, 512], F32) as psB,
            nc.psum_tensor("psb", [128, 1024], F32) as psb,
        ):
            for j in range(8):
                for d in range(8):
                    E.dma("sync", xt[:, d, :],
                          xfm[d * 128:(d + 1) * 128, j * 512:(j + 1) * 512])
                E.wait("scalar", "LD")
                E.wait("tensor", "LD")
                xts = [xt[:, d, :] for d in range(8)]
                sqs = [sq[:, d, :] for d in range(8)]
                te = emit_ln_fm(E, C, xts, sqs, psA, psB, psb, strow, D)
                E.wait("vector", "TE", te)
                for d in range(8):
                    E.op("vector", "tensor_sub", sq[:, d, :], xt[:, d, :],
                         psb[:, 0:512], sem="VE" if d == 7 else None)
                E.wait("vector", "VE")
                for d in range(8):
                    E.op("vector", "tensor_mul", xnb[:, d, :], sq[:, d, :],
                         psb[:, 512:1024], sem="VE" if d == 7 else None)
                E.wait("sync", "VE")
                for d in range(8):
                    E.dma("sync", xn_dram[d * 128:(d + 1) * 128,
                                          j * 512:(j + 1) * 512],
                          xnb[:, d, :], sem="ST")
                E.wait("vector", "ST")
                E.wait("scalar", "VE")
                E.wait("tensor", "VE")
            E.wait("sync", "ST")
            E.barrier()
            E.flush()

        with (
            nc.sbuf_tensor("wxc", [128, 8, 2048], BF16) as wxc,
            nc.sbuf_tensor("wz", [128, 8, 1024], BF16) as wz,
            nc.sbuf_tensor("xnc", [128, 8, 512], BF16) as xnc,
            nc.sbuf_tensor("oev", [128, 512], BF16) as oev,
            nc.psum_tensor("psm", [128, 2048], F32) as psm,
        ):
            for d in range(8):
                E.dma("gpsimd", wxc[:, d, :], w_xc[d * 128:(d + 1) * 128, :],
                      sem="LD2")
                E.dma("gpsimd", wz[:, d, :], w_z[d * 128:(d + 1) * 128, :],
                      sem="LD2")
            E.wait("tensor", "LD2")
            for jc in range(8):
                E.wait("sync", "TE")
                for d in range(8):
                    E.dma("sync", xnc[:, d, :],
                          xn_dram[d * 128:(d + 1) * 128,
                                  jc * 512:(jc + 1) * 512])
                E.wait("tensor", "LD")
                for ot in range(24):
                    sl = psm[:, (ot % 4) * 512:(ot % 4) * 512 + 512]
                    wsrc = (wxc[:, :, ot * 128:(ot + 1) * 128] if ot < 16
                            else wz[:, :, (ot - 16) * 128:(ot - 15) * 128])
                    for d in range(8):
                        E.op("tensor", "matmul", sl,
                             wsrc[:, d, :], xnc[:, d, :],
                             start=(d == 0), stop=(d == 7),
                             sem="TE" if d == 7 else None)
                    E.wait("vector", "TE")
                    E.op("vector", "tensor_copy", oev[:, :], sl, sem="VE")
                    E.wait("sync", "VE")
                    dst = (xc_dram[ot * 128:(ot + 1) * 128,
                                   jc * 512:(jc + 1) * 512] if ot < 16 else
                           z_dram[(ot - 16) * 128:(ot - 15) * 128,
                                  jc * 512:(jc + 1) * 512])
                    E.dma("sync", dst, oev[:, :], sem="ST")
                    E.wait("vector", "ST")
                    E.wait("tensor", "VE")
            E.wait("sync", "ST")
            E.barrier()
            E.flush()

        with (
            nc.sbuf_tensor("xcc", [128, 16, 515], BF16) as xcc,
            nc.sbuf_tensor("cacc", [128, 512], F32) as cacc,
            nc.sbuf_tensor("xso", [128, 512], BF16) as xso,
        ):
            for jc in range(8):
                E.wait("sync", "VE")
                for nt in range(16):
                    if jc == 0:
                        E.op("vector", "memset", xcc[:, nt, 0:3], 0.0,
                             sem="VE" if nt == 15 else None)
                    else:
                        E.dma("sync", xcc[:, nt, 0:3],
                              xc_dram[nt * 128:(nt + 1) * 128,
                                      jc * 512 - 3:jc * 512])
                    E.dma("sync", xcc[:, nt, 3:515],
                          xc_dram[nt * 128:(nt + 1) * 128,
                                  jc * 512:(jc + 1) * 512])
                if jc == 0:
                    E.wait("sync", "VE")
                    E.wait("vector", "VE")
                E.wait("vector", "LD")
                E.wait("scalar", "LD")
                for nt in range(16):
                    E.op("vector", "tensor_scalar", cacc[:, :],
                         xcc[:, nt, 0:512], cw_s[:, nt, 0:1], None,
                         op0=ALU.mult, sem="VE")
                    E.wait("vector", "VE")
                    for kk in range(1, 4):
                        E.op("vector", "scalar_tensor_tensor", cacc[:, :],
                             xcc[:, nt, kk:kk + 512], cw_s[:, nt, kk:kk + 1],
                             cacc[:, :], op0=ALU.mult, op1=ALU.add, sem="VE")
                        E.wait("vector", "VE")
                    E.wait("scalar", "VE")
                    E.op("scalar", "activation", xso[:, :], cacc[:, :],
                         AF.Silu, bias=cb_s[:, nt:nt + 1], sem="SE")
                    E.wait("sync", "SE")
                    E.dma("sync", xs_dram[nt * 128:(nt + 1) * 128,
                                          jc * 512:(jc + 1) * 512],
                          xso[:, :], sem="ST")
                    E.wait("scalar", "ST")
                    E.wait("vector", "SE")
            E.wait("sync", "ST")
            E.barrier()
            E.flush()

        with (
            nc.sbuf_tensor("wxs", [128, 16, 32], BF16) as wxs,
            nc.sbuf_tensor("xst", [128, 16, 512], BF16) as xst,
            nc.sbuf_tensor("draw", [16, T], F32) as draw,
            nc.sbuf_tensor("bmrs", [16, 512], F32) as bmrs,
            nc.sbuf_tensor("wdts", [16, 1024], BF16) as wdts,
            nc.sbuf_tensor("drb", [16, 512], BF16) as drb,
            nc.sbuf_tensor("spo", [128, 512], F32) as spo,
            nc.sbuf_tensor("spo2", [128, 512], F32) as spo2,
            nc.psum_tensor("psx", [16, 512], F32) as psx,
            nc.psum_tensor("psx2", [16, 512], F32) as psx2,
            nc.psum_tensor("psd", [128, 2048], F32) as psd,
        ):
            for nt in range(16):
                E.dma("gpsimd", wxs[:, nt, :], w_x[nt * 128:(nt + 1) * 128, :],
                      sem="LD2")
            E.dma("gpsimd", wdts[:, :], w_dt[:, :], sem="LD2")
            E.wait("tensor", "LD2")
            for jc in range(8):
                E.wait("sync", "TE")
                for nt in range(16):
                    E.dma("sync", xst[:, nt, :],
                          xs_dram[nt * 128:(nt + 1) * 128,
                                  jc * 512:(jc + 1) * 512])
                E.wait("tensor", "LD")
                for nt in range(16):
                    E.op("tensor", "matmul", psx[:, :], wxs[:, nt, 0:16],
                         xst[:, nt, :], start=(nt == 0), stop=(nt == 15))
                for nt in range(16):
                    E.op("tensor", "matmul", psx2[:, :], wxs[:, nt, 16:32],
                         xst[:, nt, :], start=(nt == 0), stop=(nt == 15),
                         sem="TE" if nt == 15 else None)
                E.wait("vector", "TE")
                E.op("vector", "tensor_copy", draw[0:16, jc * 512:(jc + 1) * 512],
                     psx[0:16, :], sem="VE")
                E.op("vector", "tensor_copy", bmrs[0:16, :], psx2[0:16, :],
                     sem="VE")
                E.wait("sync", "VE")
                E.dma("sync", bmr_dram[:, jc * 512:(jc + 1) * 512],
                      bmrs[0:16, :], sem="ST")
                E.wait("tensor", "VE")
                E.wait("vector", "ST")
            E.wait("sync", "ST")
            E.wait("vector", "VE")
            for jc in range(8):
                E.op("vector", "tensor_copy", drb[0:16, 0:512],
                     draw[0:16, jc * 512:(jc + 1) * 512], sem="VE")
                E.wait("tensor", "VE")
                for ot in range(8):
                    sl = psd[:, (ot % 4) * 512:(ot % 4) * 512 + 512]
                    E.op("tensor", "matmul", sl,
                         wdts[0:16, ot * 128:(ot + 1) * 128], drb[0:16, 0:512],
                         start=True, stop=True, sem="TE")
                    E.wait("vector", "TE")
                    E.op("vector", "tensor_scalar", spo[:, :], sl,
                         bdt_s[:, ot:ot + 1], None, op0=ALU.add, sem="VE")
                    E.wait("vector", "VE")
                    E.wait("scalar", "VE")
                    E.op("scalar", "activation", spo2[:, :], spo[:, :], AF.Abs,
                         sem="SE")
                    E.wait("scalar", "SE")
                    E.op("scalar", "activation", spo2[:, :], spo2[:, :],
                         AF.Exp, scale=-1.0, sem="SE")
                    E.wait("scalar", "SE")
                    E.op("scalar", "activation", spo2[:, :], spo2[:, :],
                         AF.Ln, bias=1.0, sem="SE")
                    E.wait("vector", "SE")
                    E.op("vector", "tensor_scalar", spo[:, :], spo[:, :], 0.0,
                         None, op0=ALU.max, sem="VE")
                    E.wait("vector", "VE")
                    E.op("vector", "tensor_add", spo[:, :], spo[:, :],
                         spo2[:, :], sem="VE")
                    E.wait("sync", "VE")
                    E.dma("sync", delta_dram[ot * 128:(ot + 1) * 128,
                                             jc * 512:(jc + 1) * 512],
                          spo[:, :], sem="ST")
                    E.wait("vector", "ST")
                    E.wait("scalar", "ST")
                    E.wait("tensor", "VE")
            E.wait("sync", "ST")
            E.barrier()
            E.flush()

        NHOFF = 1024
        with (
            nc.sbuf_tensor("dlt", [128, 8, 512], F32) as dlt,
            nc.sbuf_tensor("xsc", [128, 8, 512], BF16) as xsc,
            nc.sbuf_tensor("zc", [128, 8, 512], BF16) as zc,
            nc.sbuf_tensor("duc", [128, 8, 512], F32) as duc,
            nc.sbuf_tensor("brep", [128, 512], F32) as brep,
            nc.sbuf_tensor("dA8", [128, 8, 512], F32) as dA8,
            nc.sbuf_tensor("dBu8", [128, 8, 512], F32) as dBu8,
            nc.sbuf_tensor("h8", [128, 8, 512], F32) as h8,
            nc.sbuf_tensor("hsc", [128, 512], F32) as hsc,
            nc.sbuf_tensor("carry", [128, 128], F32) as carry,
            nc.sbuf_tensor("yac", [128, 8, 512], F32) as yac,
            nc.sbuf_tensor("sz", [128, 512], F32) as sz,
            nc.sbuf_tensor("ygo", [128, 512], BF16) as ygo,
        ):
            E.op("vector", "memset", carry[:, :], 0.0, sem="VE")
            E.wait("vector", "VE")
            for jc in range(8):
                E.wait("sync", "VE")
                E.wait("sync", "SE")
                for nt in range(8):
                    E.dma("sync", dlt[:, nt, :],
                          delta_dram[nt * 128:(nt + 1) * 128,
                                     jc * 512:(jc + 1) * 512])
                    E.dma("sync", xsc[:, nt, :],
                          xs_dram[nt * 128:(nt + 1) * 128,
                                  jc * 512:(jc + 1) * 512])
                    E.dma("sync", zc[:, nt, :],
                          z_dram[nt * 128:(nt + 1) * 128,
                                 jc * 512:(jc + 1) * 512])
                E.wait("vector", "LD")
                E.wait("scalar", "LD")
                for nt in range(8):
                    E.op("vector", "tensor_mul", duc[:, nt, :], dlt[:, nt, :],
                         xsc[:, nt, :], sem="VE" if nt == 7 else None)
                E.wait("vector", "VE")
                E.op("vector", "memset", yac[:, :, :], 0.0, sem="VE")
                E.wait("vector", "VE")
                E.wait("gpsimd", "VE")
                for dd in range(16):
                    E.dma("gpsimd", brep[:, :],
                          bmr_dram[dd, jc * 512:(jc + 1) * 512]
                          .partition_broadcast(128), sem="LD2")
                    E.wait("vector", "LD2")
                    E.wait("scalar", "LD2")
                    for nt in range(8):
                        E.op("scalar", "activation", dA8[:, nt, :],
                             dlt[:, nt, :], AF.Exp,
                             scale=acol[:, nt, dd:dd + 1],
                             sem="SE" if nt == 7 else None)
                    for nt in range(8):
                        E.op("vector", "tensor_mul", dBu8[:, nt, :],
                             duc[:, nt, :], brep[:, :],
                             sem="VE" if nt == 7 else None)
                    E.wait("vector", "VE")
                    E.wait("vector", "SE")
                    for nt in range(8):
                        cidx = nt * 16 + dd
                        E.op("vector", "tensor_tensor_scan", h8[:, nt, :],
                             dA8[:, nt, :], dBu8[:, nt, :],
                             carry[:, cidx:cidx + 1],
                             op0=ALU.mult, op1=ALU.add,
                             sem="VE" if nt == 7 else None)
                    E.wait("vector", "VE")
                    for nt in range(8):
                        cidx = nt * 16 + dd
                        E.op("vector", "tensor_copy", carry[:, cidx:cidx + 1],
                             h8[:, nt, 511:512],
                             sem="VE" if nt == 7 else None)
                    for nt in range(8):
                        E.op("vector", "tensor_mul", dBu8[:, nt, :],
                             h8[:, nt, :], brep[:, :],
                             sem="VE" if nt == 7 else None)
                    E.wait("vector", "VE")
                    for nt in range(8):
                        E.op("vector", "tensor_add", yac[:, nt, :],
                             yac[:, nt, :], dBu8[:, nt, :],
                             sem="VE" if nt == 7 else None)
                    E.wait("vector", "VE")
                    E.wait("scalar", "VE")
                    E.wait("gpsimd", "VE")
                for nt in range(8):
                    E.op("scalar", "activation", sz[:, :], zc[:, nt, :],
                         AF.Silu, sem="SE")
                    E.wait("vector", "SE")
                    E.op("vector", "scalar_tensor_tensor", hsc[:, :],
                         xsc[:, nt, :], dv_s[:, nt:nt + 1], yac[:, nt, :],
                         op0=ALU.mult, op1=ALU.add, sem="VE")
                    E.wait("vector", "VE")
                    E.op("vector", "tensor_mul", ygo[:, :], hsc[:, :],
                         sz[:, :], sem="VE")
                    E.wait("sync", "VE")
                    E.dma("sync", ygz[nt * 128:(nt + 1) * 128,
                                      jc * 512:(jc + 1) * 512],
                          ygo[:, :], sem="ST")
                    E.wait("vector", "ST")
                    E.wait("scalar", "VE")
            E.wait("sync", "ST")
            E.barrier()
            E.flush()

    return nc



def build_l3b():
    nc = bass.Bass("TRN2", target_bir_lowering=False)
    xq3 = nc.dram_tensor("xq3", [D, 2048], F32, kind="ExternalInput").ap()
    ygzo = nc.dram_tensor("ygzo", [2048, 2048], BF16, kind="ExternalInput").ap()
    wout = nc.dram_tensor("wout", [2048, D], F32, kind="ExternalInput").ap()
    wT3 = nc.dram_tensor("wT3", [D, D], F32, kind="ExternalInput").ap()
    bT3 = nc.dram_tensor("bT3", [8, 128, 1], F32, kind="ExternalInput").ap()
    ldT = nc.dram_tensor("ldT", [100, 4], F32, kind="ExternalInput").ap()
    luT = nc.dram_tensor("luT", [4, 50], F32, kind="ExternalInput").ap()
    daux = nc.dram_tensor("daux", [100, 1024], F32, kind="ExternalInput").ap()
    uaT = nc.dram_tensor("uaT", [50, 1024], F32, kind="ExternalInput").ap()

    outfm = nc.dram_tensor("outfm", [D, 2048], F32, kind="ExternalOutput").ap()

    xn_dram = nc.dram_tensor("l3bxn", [D, 2048], BF16, kind="Internal").ap()

    with ExitStack() as ctx:
        sems = std_sems(nc, ctx)
        block = ctx.enter_context(nc.Block())
        E = Emit(nc, block, sems)
        C = emit_consts(E, ctx, nc)
        register_fconsts(E, ctx, nc, [LN_EPS, -0.5, 1.0 / D])

        bT3s = ctx.enter_context(nc.sbuf_tensor("bT3s", [128, 8], F32))
        for d in range(8):
            E.dma("sync", bT3s[:, d:d + 1], bT3[d])
        E.barrier()
        E.flush()

        with (
            nc.sbuf_tensor("xt", [128, 8, 512], F32) as xt,
            nc.sbuf_tensor("sq", [128, 8, 512], F32) as sq,
            nc.sbuf_tensor("xnb", [128, 8, 512], BF16) as xnb,
            nc.sbuf_tensor("strow", [1, 512, 4], F32) as strow,
            nc.psum_tensor("psA", [1, 512], F32) as psA,
            nc.psum_tensor("psB", [1, 512], F32) as psB,
            nc.psum_tensor("psb", [128, 1024], F32) as psb,
        ):
            for j in range(4):
                for d in range(8):
                    E.dma("sync", xt[:, d, :],
                          xq3[d * 128:(d + 1) * 128, j * 512:(j + 1) * 512])
                E.wait("scalar", "LD")
                E.wait("tensor", "LD")
                xts = [xt[:, d, :] for d in range(8)]
                sqs = [sq[:, d, :] for d in range(8)]
                te = emit_ln_fm(E, C, xts, sqs, psA, psB, psb, strow, D)
                E.wait("vector", "TE", te)
                for d in range(8):
                    E.op("vector", "tensor_sub", sq[:, d, :], xt[:, d, :],
                         psb[:, 0:512], sem="VE" if d == 7 else None)
                E.wait("vector", "VE")
                for d in range(8):
                    E.op("vector", "tensor_mul", xnb[:, d, :], sq[:, d, :],
                         psb[:, 512:1024], sem="VE" if d == 7 else None)
                E.wait("sync", "VE")
                for d in range(8):
                    E.dma("sync", xn_dram[d * 128:(d + 1) * 128,
                                          j * 512:(j + 1) * 512],
                          xnb[:, d, :], sem="ST")
                E.wait("vector", "ST")
                E.wait("scalar", "VE")
                E.wait("tensor", "VE")
            E.wait("sync", "ST")
            E.barrier()
            E.flush()

        es2 = ExitStack()
        ldTs = es2.enter_context(nc.sbuf_tensor("ldTs", [100, 4], BF16))
        luTs = es2.enter_context(nc.sbuf_tensor("luTs", [4, 50], BF16))
        dauxs = es2.enter_context(nc.sbuf_tensor("dauxs", [100, 1024], BF16))
        uaTs = es2.enter_context(nc.sbuf_tensor("uaTs", [50, 8, 128], BF16))
        d1s = es2.enter_context(nc.sbuf_tensor("d1s", [4, 1024], BF16))
        d2Ts = es2.enter_context(nc.sbuf_tensor("d2Ts", [128, 8, 50], BF16))
        wos = es2.enter_context(nc.sbuf_tensor("wos", [128, 16, 1024], BF16))
        wT3s = es2.enter_context(nc.sbuf_tensor("wT3s", [128, 8, 1024], BF16))
        ygt = es2.enter_context(nc.sbuf_tensor("ygt", [128, 16, 512], BF16))
        xnc = es2.enter_context(nc.sbuf_tensor("xnc", [128, 8, 512], BF16))
        t1s = es2.enter_context(nc.sbuf_tensor("t1s", [50, 512], BF16))
        xf5 = es2.enter_context(nc.sbuf_tensor("xf5", [128, 512], F32))
        of5 = es2.enter_context(nc.sbuf_tensor("of5", [128, 512], F32))
        ps1 = es2.enter_context(nc.psum_tensor("ps1", [4, 1024], F32))
        ps2 = es2.enter_context(nc.psum_tensor("ps2", [128, 50], F32))
        ps3 = es2.enter_context(nc.psum_tensor("ps3", [50, 512], F32))
        pso = es2.enter_context(nc.psum_tensor("pso", [128, 2048], F32))
        if True:
            E.dma("gpsimd", ldTs[:, :], ldT[:, :], sem="LD2")
            E.dma("gpsimd", luTs[:, :], luT[:, :], sem="LD2")
            E.dma("gpsimd", dauxs[:, :], daux[:, :], sem="LD2")
            for u in range(8):
                E.dma("gpsimd", uaTs[0:50, u, :], uaT[:, u * 128:(u + 1) * 128],
                      sem="LD2")
            for nt in range(16):
                E.dma("gpsimd", wos[:, nt, :], wout[nt * 128:(nt + 1) * 128, :],
                      sem="LD2")
            for d in range(8):
                E.dma("gpsimd", wT3s[:, d, :], wT3[d * 128:(d + 1) * 128, :],
                      sem="LD2")
            E.wait("tensor", "LD2")
            for hc in range(2):
                E.op("tensor", "matmul", ps1[:, hc * 512:(hc + 1) * 512],
                     ldTs[0:100, :], dauxs[0:100, hc * 512:(hc + 1) * 512],
                     start=True, stop=True, sem="TE" if hc == 1 else None)
            E.wait("vector", "TE")
            E.op("vector", "tensor_copy", d1s[0:4, :], ps1[0:4, :], sem="VE")
            E.wait("tensor", "VE")
            for ch in range(8):
                E.op("tensor", "matmul", ps2[:, :],
                     d1s[0:4, ch * 128:(ch + 1) * 128], luTs[0:4, :],
                     start=True, stop=True, sem="TE")
                E.wait("vector", "TE")
                E.op("vector", "tensor_copy", d2Ts[:, ch, :], ps2[:, :],
                     sem="VE")
                E.wait("tensor", "VE")
            for jc in range(4):
                E.wait("sync", "TE")
                for nt in range(16):
                    E.dma("sync", ygt[:, nt, :],
                          ygzo[nt * 128:(nt + 1) * 128,
                               jc * 512:(jc + 1) * 512])
                for d in range(8):
                    E.dma("sync", xnc[:, d, :],
                          xn_dram[d * 128:(d + 1) * 128,
                                  jc * 512:(jc + 1) * 512])
                E.wait("tensor", "LD")
                for d in range(8):
                    E.op("tensor", "matmul", ps3[:, :], d2Ts[:, d, :],
                         xnc[:, d, :], start=(d == 0), stop=(d == 7),
                         sem="TE" if d == 7 else None)
                E.wait("vector", "TE")
                E.op("vector", "tensor_copy", t1s[0:50, :], ps3[0:50, :],
                     sem="VE")
                E.wait("tensor", "VE")
                for ot in range(8):
                    slA = pso[:, (ot % 4) * 512:(ot % 4) * 512 + 512]
                    for nt in range(16):
                        E.op("tensor", "matmul", slA,
                             wos[:, nt, ot * 128:(ot + 1) * 128], ygt[:, nt, :],
                             start=(nt == 0), stop=False)
                    for d in range(8):
                        E.op("tensor", "matmul", slA,
                             wT3s[:, d, ot * 128:(ot + 1) * 128], xnc[:, d, :],
                             start=False, stop=False)
                    E.op("tensor", "matmul", slA,
                         uaTs[0:50, ot, :], t1s[0:50, :],
                         start=False, stop=True, sem="TE")
                    E.dma("sync", xf5[:, :],
                          xq3[ot * 128:(ot + 1) * 128,
                              jc * 512:(jc + 1) * 512])
                    E.wait("vector", "LD")
                    E.wait("vector", "TE")
                    E.op("vector", "tensor_scalar", of5[:, :], slA,
                         bT3s[:, ot:ot + 1], None, op0=ALU.add, sem="VE")
                    E.wait("vector", "VE")
                    E.op("vector", "tensor_add", of5[:, :], of5[:, :],
                         xf5[:, :], sem="VE")
                    E.wait("sync", "VE")
                    E.dma("sync", outfm[ot * 128:(ot + 1) * 128,
                                        jc * 512:(jc + 1) * 512], of5[:, :],
                          sem="ST")
                    E.wait("vector", "ST")
                    E.wait("tensor", "VE")
            E.wait("sync", "ST")
            E.barrier()
            E.flush()
        es2.close()

    return nc


_L3_CACHE = {}


def run_l3(inputs, x3):
    w = inputs
    if "nca" not in _L3_CACHE:
        _L3_CACHE["nca"] = build_l3a()
        _L3_CACHE["ncb"] = build_l3b()
    nca, ncb = _L3_CACHE["nca"], _L3_CACHE["ncb"]

    w_in = np.asarray(w["L3_attn_W_in"], np.float32)
    conv_w = np.asarray(w["L3_attn_conv_w"], np.float32).reshape(2048, 4)
    conv_b = np.asarray(w["L3_attn_conv_b"], np.float32)
    alog_full = np.asarray(w["L3_attn_A_log"], np.float32)
    b_dt = np.asarray(w["L3_attn_b_dt"], np.float32)
    dvec_full = np.asarray(w["L3_attn_D"], np.float32)
    in_maps = []
    for c in range(N_CORES):
        b, nh = c // 2, c % 2
        nsl = slice(1024 * nh, 1024 * (nh + 1))
        perm = np.concatenate([np.arange(1024 * nh, 1024 * (nh + 1)),
                               np.arange(1024 * (1 - nh), 1024 * (2 - nh))])
        m = dict(
            xfm=np.ascontiguousarray(x3[b].T),
            w_xc=np.ascontiguousarray(w_in[:, 0:2048][:, perm]),
            w_z=np.ascontiguousarray(w_in[:, 2048 + 1024 * nh:
                                          2048 + 1024 * (nh + 1)]),
            convw=np.ascontiguousarray(conv_w[perm].reshape(16, 128, 4)),
            convb=np.ascontiguousarray(conv_b[perm].reshape(16, 128, 1)),
            w_x=np.ascontiguousarray(np.asarray(w["L3_attn_W_x"],
                                                np.float32)[perm]),
            w_dt=np.ascontiguousarray(w["L3_attn_W_dt"][:, nsl]),
            bdt=tile_bias(b_dt[nsl]),
            alog=np.ascontiguousarray(alog_full[nsl].reshape(8, 128, 16)),
            dvec=tile_bias(dvec_full[nsl]),
        )
        in_maps.append(m)
    res = run_bass_kernel_spmd(nca, in_maps, core_ids=list(range(N_CORES)))
    _L3_CACHE["resa"] = res.results
    ygz_full = np.empty((4, 2048, T), dtype=BF)
    for c in range(N_CORES):
        b, nh = c // 2, c % 2
        ygz_full[b, 1024 * nh:1024 * (nh + 1)] = res.results[c]["ygz"]

    shared_b = dict(
        wout=np.ascontiguousarray(w["L3_attn_W_out"]),
        wT3=np.ascontiguousarray(np.asarray(w["L3_ffn_W"], np.float32).T),
        bT3=tile_bias(w["L3_ffn_b"]),
        ldT=np.ascontiguousarray(np.asarray(w["L3_ffn_lora_down"],
                                            np.float32).T),
        luT=np.ascontiguousarray(np.asarray(w["L3_ffn_lora_up"],
                                            np.float32).T),
        daux=np.ascontiguousarray(w["L3_ffn_down_aux"]),
        uaT=np.ascontiguousarray(np.asarray(w["L3_ffn_up_aux"],
                                            np.float32).T),
    )
    in_maps_b = []
    for c in range(N_CORES):
        b, h = c // 2, c % 2
        rows = slice(h * 2048, (h + 1) * 2048)
        m = dict(shared_b)
        m["xq3"] = np.ascontiguousarray(x3[b][rows].T)
        m["ygzo"] = np.ascontiguousarray(ygz_full[b][:, rows.start:rows.stop])
        in_maps_b.append(m)
    res_b = run_bass_kernel_spmd(ncb, in_maps_b, core_ids=list(range(N_CORES)))
    _L3_CACHE["resb"] = res_b.results
    out = np.empty_like(x3)
    for c in range(N_CORES):
        b, h = c // 2, c % 2
        out[b, h * 2048:(h + 1) * 2048] = res_b.results[c]["outfm"].T
    return out



_L0_CACHE = {}


def run_l0(inputs, x):
    if "nc" not in _L0_CACHE:
        _L0_CACHE["nc"] = build_l0()
    nc = _L0_CACHE["nc"]

    w = inputs
    shared = dict(
        w_in=np.ascontiguousarray(w["L0_attn_W_in"]),
        b_in_row=np.ascontiguousarray(w["L0_attn_b_in"][None, :]),
        b_in_t=tile_bias(w["L0_attn_b_in"]),
        ln_g_t=tile_bias(w["L0_attn_ln_g"]),
        ln_b_t=tile_bias(w["L0_attn_ln_b"]),
        qkvec=np.stack([tile_bias(w["L0_attn_gamma_q"])[0],
                        tile_bias(w["L0_attn_beta_q"])[0],
                        tile_bias(w["L0_attn_gamma_k"])[0],
                        tile_bias(w["L0_attn_beta_k"])[0]]),
        w_out=np.ascontiguousarray(w["L0_attn_W_out"]),
        b_out_t=tile_bias(w["L0_attn_b_out"]),
        router_wT=np.ascontiguousarray(w["L0_ffn_router_w"].T),
        router_b_t=tile_bias(w["L0_ffn_router_bias"]),
        gate_p=np.ascontiguousarray(w["L0_ffn_gate_proj"]),
        up_p=np.ascontiguousarray(w["L0_ffn_up_proj"]),
        down_p=np.ascontiguousarray(w["L0_ffn_down_proj"]),
        shT_gate=np.ascontiguousarray(w["L0_ffn_sh_gate"].T),
        shT_up=np.ascontiguousarray(w["L0_ffn_sh_up"].T),
        shT_down=np.ascontiguousarray(w["L0_ffn_sh_down"].T),
    )
    in_maps = []
    for c in range(N_CORES):
        b, h = c // 2, c % 2
        xb = x[b]
        own = xb[h * 2048:(h + 1) * 2048]
        oth = xb[(1 - h) * 2048:(2 - h) * 2048]
        xperm = np.concatenate([own, oth], axis=0)
        m = dict(shared)
        m["xfm"] = np.ascontiguousarray(xperm.T)
        in_maps.append(m)
    res = run_bass_kernel_spmd(nc, in_maps, core_ids=list(range(N_CORES)))
    _L0_CACHE["res"] = res.results
    x1 = np.empty_like(x)
    for c in range(N_CORES):
        b, h = c // 2, c % 2
        x1[b, h * 2048:(h + 1) * 2048] = res.results[c]["x1fm"].T
    return x1


def tile_bias(arr):
    arr = np.asarray(arr, np.float32)
    n = arr.shape[0]
    nt = (n + 127) // 128
    out = np.zeros((nt, 128, 1), np.float32)
    out[:, :, 0].reshape(-1)[:n] = arr
    return out


def kernel(**inputs):
    x = np.asarray(inputs["L0_attn_x"], np.float32)
    x1 = run_l0(inputs, x)
    x2 = run_l1(inputs, x1)
    x3 = run_l2(inputs, x2)
    out = run_l3(inputs, x3)
    return np.ascontiguousarray(out.astype(np.float32))


# version 08
# speedup vs baseline: 1.2888x; 1.0065x over previous
import math
from contextlib import ExitStack

import numpy as np
import ml_dtypes
import concourse.bass as bass
import concourse.mybir as mybir
from concourse.bass_utils import run_bass_kernel_spmd

F32 = mybir.dt.float32
BF16 = mybir.dt.bfloat16
AF = mybir.ActivationFunctionType
ALU = mybir.AluOpType
BF = ml_dtypes.bfloat16

N_CORES = 8
DEBUG_L0 = False
D = 1024
T = 4096
LN_EPS = 1e-5

ENGINES = ("sync", "gpsimd", "tensor", "vector", "scalar")


class Emit:

    MAXQ = 4

    def __init__(self, nc, block, sems):
        self.nc = nc
        self.block = block
        self.sem = sems
        self.q = {e: [] for e in ENGINES}
        self.cnt = {s: 0 for s in sems}
        self.issuer_drain = {}
        self.consumer_wait = {}

    def wait(self, eng, sem, val=None):
        if val is None:
            val = self.cnt[sem]
        s = self.sem[sem]
        self.q[eng].append(lambda h, s=s, val=val: h.wait_ge(s, val))
        self.consumer_wait[sem] = max(self.consumer_wait.get(sem, 0), val)
        return val

    def op(self, eng, meth, *args, sem=None, inc=1, **kw):
        s = self.sem[sem] if sem else None

        def f(h, meth=meth, args=args, kw=kw, s=s, inc=inc):
            r = getattr(h, meth)(*args, **kw)
            if s is not None:
                r.then_inc(s, inc)
        self.q[eng].append(f)
        if sem:
            self.cnt[sem] += inc
            return self.cnt[sem]
        return None

    def dma(self, eng, out, in_, sem="LD"):
        base = self.issuer_drain.get(sem, 0)
        if (self.consumer_wait.get(sem, 0) > base or
                self.cnt[sem] + 16 - base > 16 * self.MAXQ):
            v = self.wait(eng, sem)
            self.issuer_drain[sem] = v
        return self.op(eng, "dma_start", sem=sem, inc=16, out=out, in_=in_)

    def flush(self):
        for eng in ENGINES:
            lst = self.q[eng]
            if not lst:
                continue
            self.q[eng] = []

            def body(h, lst=lst):
                for f in lst:
                    f(h)
            getattr(self.block, eng)(body)

    def barrier(self, engines=ENGINES):
        for e in engines:
            if e in ("vector", "gpsimd"):
                self.op(e, "engine_nop", sem="BAR", inc=1)
            else:
                self.op(e, "nop", sem="BAR", inc=1)
        tgt = self.cnt["BAR"]
        for e in engines:
            self.wait(e, "BAR", tgt)


def std_sems(nc, ctx):
    sems = {}
    for s in ("LD", "LD2", "ST", "TE", "VE", "SE", "GE", "BAR"):
        sems[s] = ctx.enter_context(nc.semaphore(s))
    return sems


def register_fconsts(E, ctx, nc, values):
    values = [float(v) for v in values]
    if not values:
        return
    cst = ctx.enter_context(nc.sbuf_tensor("c_fconsts", [1, len(values)], F32))
    for i, v in enumerate(values):
        E.op("vector", "memset", cst[0:1, i:i + 1], v)
        nc.const_aps.register(F32, v, cst[0:1, i:i + 1])
    E.op("vector", "engine_nop", sem="VE")


def emit_consts(E, ctx, nc):
    ones_col = ctx.enter_context(nc.sbuf_tensor("c_ones_col", [128, 1], F32))
    ones_colb = ctx.enter_context(nc.sbuf_tensor("c_ones_colb", [128, 1], BF16))
    ones_row = ctx.enter_context(nc.sbuf_tensor("c_ones_row", [1, 128], F32))
    ones_rowb = ctx.enter_context(nc.sbuf_tensor("c_ones_rowb", [1, 128], BF16))
    ident = ctx.enter_context(nc.sbuf_tensor("c_ident", [128, 128], BF16))
    identf = ctx.enter_context(nc.sbuf_tensor("c_identf", [128, 128], F32))
    idsc = ctx.enter_context(nc.sbuf_tensor("c_idsc", [128, 128], F32))

    E.op("gpsimd", "iota", idsc[:], [[1, 128]], sem="GE", base=0,
         channel_multiplier=-1, allow_small_or_imprecise_dtypes=True)
    E.wait("vector", "GE")
    E.op("vector", "memset", ones_col[:], 1.0)
    E.op("vector", "memset", ones_colb[:], 1.0)
    E.op("vector", "memset", ones_row[:], 1.0)
    E.op("vector", "memset", ones_rowb[:], 1.0)
    E.op("vector", "tensor_scalar", identf[:], idsc[:], 0.0, 1.0,
         op0=ALU.is_equal, op1=ALU.mult, sem="VE")
    E.wait("vector", "VE")
    E.op("vector", "tensor_copy", ident[:], identf[:], sem="VE")
    return dict(ones_col=ones_col, ones_colb=ones_colb, ones_row=ones_row,
                ones_rowb=ones_rowb, ident=ident, identf=identf)


def emit_ln_fm(E, C, x_tiles, sq_tiles, psA, psB, psb, strow, dim, eps=LN_EPS,
               ones=None):
    nd = len(x_tiles)
    W = x_tiles[0].shape[-1]
    if ones is None:
        ones = C["ones_col"]
    for d in range(nd):
        E.op("scalar", "activation", sq_tiles[d], x_tiles[d], AF.Square,
             sem="SE" if d == nd - 1 else None)
    se = E.cnt["SE"]
    E.wait("tensor", "SE")
    E.wait("tensor", "VE")
    for d in range(nd):
        E.op("tensor", "matmul", psA[:], ones[:, 0:1], x_tiles[d],
             start=(d == 0), stop=(d == nd - 1))
    E.wait("tensor", "SE", se)
    for d in range(nd):
        E.op("tensor", "matmul", psB[:], C["ones_col"][:, 0:1], sq_tiles[d],
             start=(d == 0), stop=(d == nd - 1),
             sem="TE" if d == nd - 1 else None)
    te = E.cnt["TE"]
    E.wait("scalar", "TE", te)
    E.op("scalar", "mul", strow[:, :, 0], psA[:], 1.0 / dim)
    E.op("scalar", "mul", strow[:, :, 1], psB[:], 1.0 / dim, sem="SE")
    E.wait("scalar", "SE")
    E.op("scalar", "activation", strow[:, :, 2], strow[:, :, 0], AF.Square,
         sem="SE")
    E.wait("vector", "SE")
    E.op("vector", "tensor_sub", strow[:, :, 3], strow[:, :, 1], strow[:, :, 2],
         sem="VE")
    E.wait("scalar", "VE")
    E.op("scalar", "activation", strow[:, :, 3], strow[:, :, 3], AF.Ln,
         bias=eps, sem="SE")
    E.wait("scalar", "SE")
    E.op("scalar", "activation", strow[:, :, 3], strow[:, :, 3], AF.Exp,
         scale=-0.5, sem="SE")
    E.wait("tensor", "SE")
    E.op("tensor", "matmul", psb[:, 0:W], C["ones_row"][0:1, :], strow[:, :, 0],
         start=True, stop=True)
    E.op("tensor", "matmul", psb[:, W:2 * W], C["ones_row"][0:1, :],
         strow[:, :, 3], start=True, stop=True, sem="TE")
    return E.cnt["TE"]



def build_l0():
    nc = bass.Bass("TRN2", target_bir_lowering=False)
    xfm = nc.dram_tensor("xfm", [D, T], F32, kind="ExternalInput").ap()
    w_in = nc.dram_tensor("w_in", [D, 4224], F32, kind="ExternalInput").ap()
    b_in_row = nc.dram_tensor("b_in_row", [1, 4224], F32, kind="ExternalInput").ap()
    b_in_t = nc.dram_tensor("b_in_t", [33, 128, 1], F32, kind="ExternalInput").ap()
    ln_g_t = nc.dram_tensor("ln_g_t", [8, 128, 1], F32, kind="ExternalInput").ap()
    ln_b_t = nc.dram_tensor("ln_b_t", [8, 128, 1], F32, kind="ExternalInput").ap()
    qkvec = nc.dram_tensor("qkvec", [4, 128, 1], F32, kind="ExternalInput").ap()
    w_out = nc.dram_tensor("w_out", [2048, D], F32, kind="ExternalInput").ap()
    b_out_t = nc.dram_tensor("b_out_t", [8, 128, 1], F32, kind="ExternalInput").ap()
    router_wT = nc.dram_tensor("router_wT", [D, 16], F32, kind="ExternalInput").ap()
    router_b_t = nc.dram_tensor("router_b_t", [1, 128, 1], F32, kind="ExternalInput").ap()
    gate_p = nc.dram_tensor("gate_p", [16, D, 512], F32, kind="ExternalInput").ap()
    up_p = nc.dram_tensor("up_p", [16, D, 512], F32, kind="ExternalInput").ap()
    down_p = nc.dram_tensor("down_p", [16, 512, D], F32, kind="ExternalInput").ap()
    shT_gate = nc.dram_tensor("shT_gate", [D, 1024], F32, kind="ExternalInput").ap()
    shT_up = nc.dram_tensor("shT_up", [D, 1024], F32, kind="ExternalInput").ap()
    shT_down = nc.dram_tensor("shT_down", [1024, D], F32, kind="ExternalInput").ap()

    x1fm = nc.dram_tensor("x1fm", [D, 2048], F32, kind="ExternalOutput").ap()

    if DEBUG_L0:
        dbg_xn = nc.dram_tensor("dbg_xn", [D, 2048], BF16, kind="ExternalOutput").ap()
        dbg_v = nc.dram_tensor("dbg_v", [T, 2048], BF16, kind="ExternalOutput").ap()
        dbg_u = nc.dram_tensor("dbg_u", [2048, 2048], BF16, kind="ExternalOutput").ap()
        dbg_att = nc.dram_tensor("dbg_att", [D, 2048], F32, kind="ExternalOutput").ap()
        dbg_pvu = nc.dram_tensor("dbg_pvu", [2048, 2048], BF16, kind="ExternalOutput").ap()
        dbg_acc = nc.dram_tensor("dbg_acc", [D, 2048], F32, kind="ExternalOutput").ap()
        dbg_cfm = nc.dram_tensor("dbg_cfm", [16, 2048], F32, kind="ExternalOutput").ap()
        dbg_ltm = nc.dram_tensor("dbg_ltm", [128, 16, 16], F32, kind="ExternalOutput").ap()
        dbg_t8 = nc.dram_tensor("dbg_t8", [128, 16, 8], F32, kind="ExternalOutput").ap()
        dbg_lgf = nc.dram_tensor("dbg_lgf", [16, 2048], F32, kind="ExternalOutput").ap()
        dbg_sml = nc.dram_tensor("dbg_sml", [128, 16, 5], F32, kind="ExternalOutput").ap()
        dbg_ctm = nc.dram_tensor("dbg_ctm", [128, 16, 16], F32, kind="ExternalOutput").ap()
    v_dram = nc.dram_tensor("v_dram", [T, 2048], BF16, kind="Internal").ap()
    u_dram = nc.dram_tensor("u_dram", [2048, 2048], BF16, kind="Internal").ap()
    xn_dram = nc.dram_tensor("xn_dram", [D, 2048], BF16, kind="Internal").ap()
    xn32_dram = nc.dram_tensor("xn32_dram", [D, 2048], F32, kind="Internal").ap()
    pvu_dram = nc.dram_tensor("pvu_dram", [2048, 2048], BF16, kind="Internal").ap()
    att_dram = nc.dram_tensor("att_dram", [D, 2048], F32, kind="Internal").ap()

    with ExitStack() as ctx:
        sems = std_sems(nc, ctx)
        block = ctx.enter_context(nc.Block())
        E = Emit(nc, block, sems)
        C = emit_consts(E, ctx, nc)
        register_fconsts(E, ctx, nc,
                         [LN_EPS, -0.5, 1.0 / D, 1.0 / T])

        lng = ctx.enter_context(nc.sbuf_tensor("lng", [128, 8], F32))
        lnb = ctx.enter_context(nc.sbuf_tensor("lnb", [128, 8], F32))
        bint = ctx.enter_context(nc.sbuf_tensor("bint", [128, 33], F32))
        qkv4 = ctx.enter_context(nc.sbuf_tensor("qkv4", [128, 4], F32))
        boutt = ctx.enter_context(nc.sbuf_tensor("boutt", [128, 8], F32))
        rbt = ctx.enter_context(nc.sbuf_tensor("rbt", [128, 1], F32))
        qf = ctx.enter_context(nc.sbuf_tensor("qf", [128, 2048], BF16))
        kf = ctx.enter_context(nc.sbuf_tensor("kf", [128, T], BF16))

        for d in range(8):
            E.dma("sync", lng[:, d:d + 1], ln_g_t[d])
            E.dma("sync", lnb[:, d:d + 1], ln_b_t[d])
            E.dma("sync", boutt[:, d:d + 1], b_out_t[d])
        for t in range(33):
            E.dma("sync", bint[:, t:t + 1], b_in_t[t])
        for i in range(4):
            E.dma("sync", qkv4[:, i:i + 1], qkvec[i])
        E.dma("sync", rbt[:], router_b_t[0])
        E.barrier()
        E.flush()

        with (
            nc.sbuf_tensor("xln", [128, 8, T], BF16) as xln,
        ):
            with (
                nc.sbuf_tensor("xt", [128, 8, 512], F32) as xt,
                nc.sbuf_tensor("sq", [128, 8, 512], F32) as sq,
                nc.sbuf_tensor("xnf", [128, 8, 512], F32) as xnf,
                nc.sbuf_tensor("xnb", [128, 8, 512], BF16) as xnb,
                nc.sbuf_tensor("strow", [1, 512, 4], F32) as strow,
                nc.psum_tensor("psA", [1, 512], F32) as psA,
                nc.psum_tensor("psB", [1, 512], F32) as psB,
                nc.psum_tensor("psb", [128, 1024], F32) as psb,
            ):
                for j in range(T // 512):
                    own = j < 4
                    for d in range(8):
                        E.dma("sync", xt[:, d, :],
                              xfm[d * 128:(d + 1) * 128, j * 512:(j + 1) * 512])
                    E.wait("scalar", "LD")
                    E.wait("tensor", "LD")
                    xts = [xt[:, d, :] for d in range(8)]
                    sqs = [sq[:, d, :] for d in range(8)]
                    te = emit_ln_fm(E, C, xts, sqs, psA, psB, psb, strow, D)
                    E.wait("vector", "TE", te)
                    for d in range(8):
                        E.op("vector", "tensor_sub", xnf[:, d, :], xt[:, d, :],
                             psb[:, 0:512])
                        E.op("vector", "tensor_mul", xnf[:, d, :], xnf[:, d, :],
                             psb[:, 512:1024],
                             sem="VE" if d == 7 else None)
                    ve = E.cnt["VE"]
                    if own:
                        for d in range(8):
                            E.op("vector", "tensor_copy", xnb[:, d, :],
                                 xnf[:, d, :], sem="VE" if d == 7 else None)
                        E.wait("sync", "VE")
                        for d in range(8):
                            E.dma("sync", xn_dram[d * 128:(d + 1) * 128,
                                                  j * 512:(j + 1) * 512],
                                  xnb[:, d, :], sem="ST")
                            E.dma("sync", xn32_dram[d * 128:(d + 1) * 128,
                                                    j * 512:(j + 1) * 512],
                                  xnf[:, d, :], sem="ST")
                    E.wait("scalar", "VE", ve)
                    E.wait("tensor", "VE", ve)
                    xnts = [xnf[:, d, :] for d in range(8)]
                    te = emit_ln_fm(E, C, xnts, sqs, psA, psB, psb, strow, D)
                    E.wait("vector", "TE", te)
                    for d in range(8):
                        E.op("vector", "tensor_sub", sq[:, d, :], xnf[:, d, :],
                             psb[:, 0:512])
                        E.op("vector", "tensor_mul", sq[:, d, :], sq[:, d, :],
                             psb[:, 512:1024])
                        E.op("vector", "tensor_scalar",
                             xln[:, d, j * 512:(j + 1) * 512],
                             sq[:, d, :], lng[:, d:d + 1], lnb[:, d:d + 1],
                             op0=ALU.mult, op1=ALU.add,
                             sem="VE" if d == 7 else None)
                    E.wait("sync", "VE")
                    E.wait("sync", "SE")
                E.wait("sync", "ST")
                E.barrier()
                E.flush()

            with (
                nc.sbuf_tensor("wsb", [128, 8, 4224], BF16) as wsb,
                nc.sbuf_tensor("binrow", [1, 4224], F32) as binrow,
                nc.sbuf_tensor("zf", [128, T], BF16) as zf,
                nc.sbuf_tensor("ev1", [128, 512], F32) as ev1,
                nc.sbuf_tensor("ub", [128, 512], BF16) as ub,
                nc.sbuf_tensor("bvb", [128, 2048], F32) as bvb,
                nc.psum_tensor("psm", [128, 2048], F32) as psm,
                nc.psum_tensor("psv", [128, 2048], F32) as psv,
            ):
                E.dma("sync", binrow[:], b_in_row[:])
                for d in range(8):
                    E.dma("gpsimd", wsb[:, d, :], w_in[d * 128:(d + 1) * 128, :],
                          sem="LD2")
                E.wait("tensor", "LD2")

                for f in range(16):
                    for jc in range(4):
                        sl = psm[:, (f % 4) * 512:(f % 4) * 512 + 512]
                        for d in range(8):
                            E.op("tensor", "matmul", sl,
                                 wsb[:, d, f * 128:(f + 1) * 128],
                                 xln[:, d, jc * 512:(jc + 1) * 512],
                                 start=(d == 0), stop=(d == 7),
                                 sem="TE" if d == 7 else None)
                        E.wait("scalar", "TE")
                        E.op("scalar", "activation", ub[:], sl, AF.Silu,
                             bias=bint[:, f:f + 1], sem="SE")
                        E.wait("sync", "SE")
                        E.dma("sync", u_dram[f * 128:(f + 1) * 128,
                                             jc * 512:(jc + 1) * 512],
                              ub[:], sem="ST")
                        E.wait("tensor", "SE")
                        E.wait("scalar", "ST")
                for jc in range(8):
                    sl = psm[:, (jc % 4) * 512:(jc % 4) * 512 + 512]
                    for d in range(8):
                        E.op("tensor", "matmul", sl,
                             wsb[:, d, 4096:4224],
                             xln[:, d, jc * 512:(jc + 1) * 512],
                             start=(d == 0), stop=(d == 7),
                             sem="TE" if d == 7 else None)
                    E.wait("scalar", "TE")
                    E.op("scalar", "activation", zf[:, jc * 512:(jc + 1) * 512],
                         sl, AF.Silu, bias=bint[:, 32:33], sem="SE")
                    E.wait("tensor", "SE")
                E.wait("vector", "SE")
                E.op("vector", "tensor_scalar", qf[:], zf[:, 0:2048],
                     qkv4[:, 0:1], qkv4[:, 1:2], op0=ALU.mult, op1=ALU.add)
                E.op("vector", "tensor_scalar", kf[:], zf[:],
                     qkv4[:, 2:3], qkv4[:, 3:4], op0=ALU.mult, op1=ALU.add,
                     sem="VE")

                for ec in range(4):
                    E.op("tensor", "matmul", psv[:, ec * 512:(ec + 1) * 512],
                         C["ones_row"][0:1, :],
                         binrow[0:1, 2048 + ec * 512:2048 + (ec + 1) * 512],
                         start=True, stop=True,
                         sem="TE" if ec == 3 else None)
                E.wait("vector", "TE")
                E.op("vector", "tensor_copy", bvb[:], psv[:], sem="VE")
                E.wait("tensor", "VE")

                for tt in range(32):
                    for ec in range(4):
                        sl = psv[:, ec * 512:(ec + 1) * 512]
                        for d in range(8):
                            E.op("tensor", "matmul", sl,
                                 xln[:, d, tt * 128:(tt + 1) * 128],
                                 wsb[:, d, 2048 + ec * 512:2048 + (ec + 1) * 512],
                                 start=(d == 0), stop=(d == 7),
                                 sem="TE" if d == 7 else None)
                        E.wait("vector", "TE")
                        E.op("vector", "tensor_add", ev1[:], sl,
                             bvb[:, ec * 512:(ec + 1) * 512], sem="VE")
                        E.wait("scalar", "VE")
                        E.op("scalar", "activation", ub[:], ev1[:], AF.Silu,
                             sem="SE")
                        E.wait("sync", "SE")
                        E.dma("sync", v_dram[tt * 128:(tt + 1) * 128,
                                             ec * 512:(ec + 1) * 512],
                              ub[:], sem="ST")
                        E.wait("tensor", "VE")
                        E.wait("vector", "SE")
                        E.wait("scalar", "ST")
                E.wait("sync", "ST")
                E.barrier()
                E.flush()

        with (
            nc.sbuf_tensor("vsb", [128, 32, 2048], BF16) as vsb,
            nc.sbuf_tensor("pt", [128, 32, 256], BF16) as pt,
            nc.sbuf_tensor("sc1", [128, 256], F32) as sc1,
            nc.sbuf_tensor("pvu", [128, 16, 256], BF16) as pvu,
            nc.sbuf_tensor("ut", [128, 256], BF16) as ut,
            nc.psum_tensor("pss", [128, 512], F32) as pss,
            nc.psum_tensor("psp", [128, 8 * 256], F32) as psp,
        ):
            for tt in range(32):
                E.dma("sync", vsb[:, tt, :], v_dram[tt * 128:(tt + 1) * 128, :])
            E.wait("tensor", "LD")

            for qc in range(8):
                q_ap = qf[:, qc * 256:(qc + 1) * 256]
                for kt in range(32):
                    sl = pss[:, (kt % 2) * 256:(kt % 2) * 256 + 256]
                    E.op("tensor", "matmul", sl,
                         kf[:, kt * 128:(kt + 1) * 128], q_ap,
                         start=True, stop=True, sem="TE")
                    E.wait("vector", "TE")
                    E.op("vector", "tensor_scalar", sc1[:], sl, 0.0, None,
                         op0=ALU.max, sem="VE")
                    E.wait("scalar", "VE")
                    E.op("scalar", "activation", pt[:, kt, :], sc1[:],
                         AF.Square, scale=1.0 / T, sem="SE")
                    E.wait("tensor", "VE")
                    E.wait("vector", "SE")
                E.wait("tensor", "SE")
                for eg in range(2):
                    for kt in range(32):
                        for et in range(8):
                            e_abs = eg * 8 + et
                            E.op("tensor", "matmul",
                                 psp[:, et * 256:(et + 1) * 256],
                                 vsb[:, kt, e_abs * 128:(e_abs + 1) * 128],
                                 pt[:, kt, :],
                                 start=(kt == 0), stop=(kt == 31),
                                 sem="TE" if (kt == 31 and et == 7) else None)
                    te = E.cnt["TE"]
                    for et in range(8):
                        e_abs = eg * 8 + et
                        E.dma("sync", ut[:],
                              u_dram[e_abs * 128:(e_abs + 1) * 128,
                                     qc * 256:(qc + 1) * 256])
                        E.wait("vector", "LD")
                        if et == 0:
                            E.wait("vector", "TE", te)
                        E.op("vector", "tensor_mul", pvu[:, e_abs, :],
                             psp[:, et * 256:(et + 1) * 256], ut[:], sem="VE")
                        E.wait("sync", "VE")
                    E.wait("tensor", "VE")
                E.wait("sync", "VE")
                for et in range(16):
                    E.dma("sync", pvu_dram[et * 128:(et + 1) * 128,
                                           qc * 256:(qc + 1) * 256],
                          pvu[:, et, :], sem="ST")
                E.wait("tensor", "ST")
                E.wait("vector", "ST")
                E.wait("scalar", "ST")
            E.barrier()
            E.flush()

        with (
            nc.sbuf_tensor("wob", [128, 16, 1024], BF16) as wob,
            nc.sbuf_tensor("pvt", [128, 16, 512], BF16) as pvt,
            nc.sbuf_tensor("xnt", [128, 512], BF16) as xnt,
            nc.sbuf_tensor("attf", [128, 512], F32) as attf,
            nc.psum_tensor("pso", [128, 8 * 512], F32) as pso,
        ):
            for et in range(16):
                E.dma("gpsimd", wob[:, et, :], w_out[et * 128:(et + 1) * 128, :],
                      sem="LD2")
            E.wait("tensor", "LD2")
            for jc in range(4):
                for et in range(16):
                    E.dma("sync", pvt[:, et, :],
                          pvu_dram[et * 128:(et + 1) * 128,
                                   jc * 512:(jc + 1) * 512])
                E.wait("tensor", "LD")
                for dt in range(8):
                    for et in range(16):
                        E.op("tensor", "matmul", pso[:, dt * 512:(dt + 1) * 512],
                             wob[:, et, dt * 128:(dt + 1) * 128], pvt[:, et, :],
                             start=(et == 0), stop=(et == 15),
                             sem="TE" if (et == 15 and dt == 7) else None)
                te = E.cnt["TE"]
                for dt in range(8):
                    E.dma("sync", xnt[:], xn_dram[dt * 128:(dt + 1) * 128,
                                                  jc * 512:(jc + 1) * 512])
                    E.wait("vector", "LD")
                    if dt == 0:
                        E.wait("vector", "TE", te)
                    E.op("vector", "tensor_scalar", attf[:],
                         pso[:, dt * 512:(dt + 1) * 512], boutt[:, dt:dt + 1],
                         None, op0=ALU.add)
                    E.op("vector", "tensor_add", attf[:], attf[:], xnt[:],
                         sem="VE")
                    E.wait("sync", "VE")
                    E.dma("sync", att_dram[dt * 128:(dt + 1) * 128,
                                           jc * 512:(jc + 1) * 512], attf[:],
                          sem="ST")
                    E.wait("vector", "ST")
                E.wait("tensor", "VE")
                E.wait("sync", "TE")
            E.wait("sync", "ST")
            E.barrier()
            E.flush()

        with (
            nc.sbuf_tensor("xnr", [128, 8, 2048], BF16) as xnr,
            nc.sbuf_tensor("acc", [128, 8, 2048], F32) as acc,
            nc.sbuf_tensor("cfm", [16, 2048], F32) as cfm,
            nc.sbuf_tensor("selc", [16, 16 * 128], F32) as selc,
        ):
            for d in range(8):
                E.dma("sync", xnr[:, d, :], xn_dram[d * 128:(d + 1) * 128, :])
            E.op("gpsimd", "iota", selc[:], [[1, 16], [0, 128]], sem="GE",
                 base=0, channel_multiplier=-1,
                 allow_small_or_imprecise_dtypes=True)
            E.wait("vector", "GE")
            E.op("vector", "tensor_scalar", selc[:], selc[:], 0.0, 1.0,
                 op0=ALU.is_equal, op1=ALU.mult, sem="VE")
            sel_ve = E.cnt["VE"]

            with (
                nc.sbuf_tensor("rwt", [128, 8, 16], F32) as rwt,
                nc.sbuf_tensor("xn32", [128, 8, 512], F32) as xn32,
                nc.sbuf_tensor("lgf", [16, 2048], F32) as lgf,
                nc.sbuf_tensor("ltm", [128, 16, 16], F32) as ltm,
                nc.sbuf_tensor("top8", [128, 16, 8], F32) as top8,
                nc.sbuf_tensor("sml", [128, 16, 5], F32) as sml,
                nc.sbuf_tensor("ctm", [128, 16, 16], F32) as ctm,
                nc.psum_tensor("ps_l", [16, 512], F32) as ps_l,
                nc.psum_tensor("ps_t", [128, 16], F32) as ps_t,
                nc.psum_tensor("ps_t2", [16, 128], F32) as ps_t2,
            ):
                for d in range(8):
                    E.dma("sync", rwt[:, d, :],
                          router_wT[d * 128:(d + 1) * 128, :])
                E.wait("tensor", "LD")
                for jc in range(4):
                    for d in range(8):
                        E.dma("sync", xn32[:, d, :],
                              xn32_dram[d * 128:(d + 1) * 128,
                                        jc * 512:(jc + 1) * 512])
                    E.wait("tensor", "LD")
                    for d in range(8):
                        E.op("tensor", "matmul", ps_l[:, :],
                             rwt[:, d, :], xn32[:, d, :],
                             start=(d == 0), stop=(d == 7),
                             sem="TE" if d == 7 else None)
                    E.wait("vector", "TE")
                    E.wait("sync", "TE")
                    E.op("vector", "tensor_scalar",
                         lgf[0:16, jc * 512:(jc + 1) * 512],
                         ps_l[:, :], rbt[0:16, 0:1], None, op0=ALU.add,
                         sem="VE")
                    E.wait("tensor", "VE")
                for g in range(16):
                    E.op("tensor", "transpose", ps_t[:, :],
                         lgf[0:16, g * 128:(g + 1) * 128],
                         C["identf"][0:16, 0:16], sem="TE")
                    E.wait("vector", "TE")
                    E.op("vector", "tensor_copy", ltm[:, g, :], ps_t[:, :],
                         sem="VE")
                    E.wait("tensor", "VE")
                for g in range(16):
                    E.op("vector", "max", top8[:, g, :], ltm[:, g, :],
                         sem="VE" if g == 15 else None)
                E.wait("vector", "VE")
                for g in range(16):
                    E.op("vector", "tensor_sub", sml[:, g, 0:1],
                         top8[:, g, 1:2], top8[:, g, 0:1],
                         sem="VE" if g == 15 else None)
                E.wait("scalar", "VE")
                for g in range(16):
                    E.op("scalar", "activation", sml[:, g, 1:2], sml[:, g, 0:1],
                         AF.Exp, sem="SE" if g == 15 else None)
                E.wait("vector", "SE")
                for g in range(16):
                    E.op("vector", "tensor_scalar", sml[:, g, 2:3],
                         sml[:, g, 1:2], 1.0, None, op0=ALU.add,
                         sem="VE" if g == 15 else None)
                E.wait("vector", "VE")
                for g in range(16):
                    E.op("vector", "reciprocal", sml[:, g, 3:4], sml[:, g, 2:3],
                         sem="VE" if g == 15 else None)
                E.wait("vector", "VE")
                for g in range(16):
                    E.op("vector", "tensor_mul", sml[:, g, 4:5],
                         sml[:, g, 1:2], sml[:, g, 3:4],
                         sem="VE" if g == 15 else None)
                E.wait("vector", "VE")
                for g in range(16):
                    E.op("vector", "tensor_scalar", ctm[:, g, :], ltm[:, g, :],
                         top8[:, g, 0:1], sml[:, g, 3:4],
                         op0=ALU.is_equal, op1=ALU.mult)
                    E.op("vector", "tensor_scalar", ltm[:, g, :], ltm[:, g, :],
                         top8[:, g, 1:2], sml[:, g, 4:5],
                         op0=ALU.is_equal, op1=ALU.mult,
                         sem="VE" if g == 15 else None)
                E.wait("vector", "VE")
                for g in range(16):
                    E.op("vector", "tensor_add", ctm[:, g, :], ctm[:, g, :],
                         ltm[:, g, :], sem="VE" if g == 15 else None)
                E.wait("tensor", "VE")
                for g in range(16):
                    E.op("tensor", "transpose", ps_t2[:, :],
                         ctm[:, g, :], C["identf"][:], sem="TE")
                    E.wait("vector", "TE")
                    E.op("vector", "tensor_copy", cfm[0:16, g * 128:(g + 1) * 128],
                         ps_t2[0:16, :], sem="VE")
                    E.wait("tensor", "VE")
                if DEBUG_L0:
                    E.wait("sync", "VE")
                    E.dma("sync", dbg_ltm[:, :, :], ltm[:, :, :], sem="ST")
                    E.dma("sync", dbg_t8[:, :, :], top8[:, :, :], sem="ST")
                    E.dma("sync", dbg_lgf[:], lgf[0:16, :], sem="ST")
                    E.dma("sync", dbg_sml[:, :, :], sml[:, :, :], sem="ST")
                    E.dma("sync", dbg_ctm[:, :, :], ctm[:, :, :], sem="ST")
                    E.wait("sync", "ST")
                E.barrier()
                E.flush()

            with (
                nc.sbuf_tensor("ew", [128, 2, 8, 1024], BF16) as ew,
                nc.sbuf_tensor("dw", [128, 2, 4, 1024], BF16) as dw,
                nc.sbuf_tensor("gsc", [128, 4, 512], F32) as gsc,
                nc.sbuf_tensor("hb", [128, 4, 512], BF16) as hb,
                nc.sbuf_tensor("cbe", [128, 4, 512], BF16) as cbe,
                nc.psum_tensor("ps_g", [128, 4 * 512], F32) as ps_g,
                nc.psum_tensor("ps_u", [128, 4 * 512], F32) as ps_u,
            ):
                te_done = {}
                ve_add = {}
                for e in range(16):
                    buf = e % 2
                    if e >= 2:
                        E.wait("gpsimd", "TE", te_done[e - 2])
                    for d in range(8):
                        E.dma("gpsimd", ew[:, buf, d, 0:512],
                              gate_p[e, d * 128:(d + 1) * 128, :], sem="LD2")
                        E.dma("gpsimd", ew[:, buf, d, 512:1024],
                              up_p[e, d * 128:(d + 1) * 128, :], sem="LD2")
                    for ht in range(4):
                        E.dma("gpsimd", dw[:, buf, ht, :],
                              down_p[e, ht * 128:(ht + 1) * 128, :], sem="LD2")
                    ld2_e = E.cnt["LD2"]

                    ve_cbe = {}
                    if e == 0:
                        E.wait("tensor", "VE", sel_ve)
                    for jc in range(4):
                        if e > 0:
                            E.wait("tensor", "VE", ve_add[jc])
                        E.op("tensor", "matmul",
                             ps_g[:, jc * 512:(jc + 1) * 512],
                             selc[0:16, e * 128:(e + 1) * 128],
                             cfm[0:16, jc * 512:(jc + 1) * 512],
                             start=True, stop=True, sem="TE")
                    E.wait("vector", "TE")
                    for jc in range(4):
                        E.op("vector", "tensor_copy", cbe[:, jc, :],
                             ps_g[:, jc * 512:(jc + 1) * 512], sem="VE")
                        ve_cbe[jc] = E.cnt["VE"]

                    E.wait("tensor", "LD2", ld2_e)
                    for jc in range(4):
                        xn_ap = [xnr[:, d, jc * 512:(jc + 1) * 512]
                                 for d in range(8)]
                        te_g = {}
                        te_u = {}
                        for ht in range(4):
                            E.wait("tensor", "VE",
                                   ve_cbe[ht] if jc == 0 else ve_add[ht])
                            for d in range(8):
                                E.op("tensor", "matmul",
                                     ps_g[:, ht * 512:(ht + 1) * 512],
                                     ew[:, buf, d, ht * 128:(ht + 1) * 128],
                                     xn_ap[d],
                                     start=(d == 0), stop=(d == 7),
                                     sem="TE" if d == 7 else None)
                            te_g[ht] = E.cnt["TE"]
                        for ht in range(4):
                            if e > 0 or jc > 0:
                                E.wait("tensor", "VE", ve_add[4 + ht])
                            for d in range(8):
                                E.op("tensor", "matmul",
                                     ps_u[:, ht * 512:(ht + 1) * 512],
                                     ew[:, buf, d,
                                        512 + ht * 128:512 + (ht + 1) * 128],
                                     xn_ap[d],
                                     start=(d == 0), stop=(d == 7),
                                     sem="TE" if d == 7 else None)
                            te_u[ht] = E.cnt["TE"]
                        se_h = {}
                        for ht in range(4):
                            E.wait("scalar", "TE", te_g[ht])
                            E.op("scalar", "activation", gsc[:, ht, :],
                                 ps_g[:, ht * 512:(ht + 1) * 512], AF.Silu,
                                 sem="SE")
                            se_h[ht] = E.cnt["SE"]
                        for ht in range(4):
                            E.wait("vector", "SE", se_h[ht])
                            E.wait("vector", "TE", te_u[ht])
                            E.op("vector", "tensor_mul", hb[:, ht, :],
                                 gsc[:, ht, :],
                                 ps_u[:, ht * 512:(ht + 1) * 512],
                                 sem="VE" if ht == 3 else None)
                        E.wait("vector", "VE")
                        for ht in range(4):
                            E.op("vector", "tensor_mul", hb[:, ht, :],
                                 hb[:, ht, :], cbe[:, jc, :],
                                 sem="VE" if ht == 3 else None)
                        ve_hb = E.cnt["VE"]
                        E.wait("tensor", "SE")
                        E.wait("tensor", "VE", ve_hb)
                        te_d = {}
                        for dt in range(8):
                            psd = (ps_g[:, dt * 512:(dt + 1) * 512] if dt < 4
                                   else ps_u[:, (dt - 4) * 512:(dt - 3) * 512])
                            for ht in range(4):
                                E.op("tensor", "matmul", psd,
                                     dw[:, buf, ht, dt * 128:(dt + 1) * 128],
                                     hb[:, ht, :],
                                     start=(ht == 0), stop=(ht == 3),
                                     sem="TE" if ht == 3 else None)
                            te_d[dt] = E.cnt["TE"]
                        te_done[e] = E.cnt["TE"]
                        for dt in range(8):
                            psd = (ps_g[:, dt * 512:(dt + 1) * 512] if dt < 4
                                   else ps_u[:, (dt - 4) * 512:(dt - 3) * 512])
                            a_ap = acc[:, dt, jc * 512:(jc + 1) * 512]
                            E.wait("vector", "TE", te_d[dt])
                            if e == 0:
                                E.op("vector", "tensor_copy", a_ap, psd,
                                     sem="VE")
                            else:
                                E.op("vector", "tensor_add", a_ap, a_ap, psd,
                                     sem="VE")
                            ve_add[dt] = E.cnt["VE"]
                E.wait("tensor", "VE")
                E.barrier()
                E.flush()

            with (
                nc.sbuf_tensor("sg", [128, 8, 1024], BF16) as sg,
                nc.sbuf_tensor("su", [128, 8, 1024], BF16) as su,
                nc.sbuf_tensor("sd", [128, 8, 1024], BF16) as sd,
                nc.sbuf_tensor("gs2", [128, 8, 512], BF16) as gs2,
                nc.sbuf_tensor("hb2", [128, 8, 512], BF16) as hb2,
                nc.psum_tensor("ps_a", [128, 4 * 512], F32) as ps_a,
                nc.psum_tensor("ps_b2", [128, 4 * 512], F32) as ps_b2,
            ):
                def psx(i):
                    return (ps_a[:, i * 512:(i + 1) * 512] if i < 4
                            else ps_b2[:, (i - 4) * 512:(i - 3) * 512])

                for d in range(8):
                    E.dma("gpsimd", sg[:, d, :], shT_gate[d * 128:(d + 1) * 128, :],
                          sem="LD2")
                    E.dma("gpsimd", su[:, d, :], shT_up[d * 128:(d + 1) * 128, :],
                          sem="LD2")
                    E.dma("gpsimd", sd[:, d, :], shT_down[d * 128:(d + 1) * 128, :],
                          sem="LD2")
                E.wait("tensor", "LD2")
                vadd = {}
                for jc in range(4):
                    xn_ap = [xnr[:, d, jc * 512:(jc + 1) * 512] for d in range(8)]
                    tg = {}
                    tu = {}
                    sh = {}
                    td = {}
                    for ht in range(8):
                        if jc > 0:
                            E.wait("tensor", "VE", vadd[ht])
                        for d in range(8):
                            E.op("tensor", "matmul", psx(ht),
                                 sg[:, d, ht * 128:(ht + 1) * 128], xn_ap[d],
                                 start=(d == 0), stop=(d == 7),
                                 sem="TE" if d == 7 else None)
                        tg[ht] = E.cnt["TE"]
                    for ht in range(8):
                        E.wait("scalar", "TE", tg[ht])
                        E.op("scalar", "activation", gs2[:, ht, :], psx(ht),
                             AF.Silu, sem="SE")
                        sh[ht] = E.cnt["SE"]
                    for ht in range(8):
                        E.wait("tensor", "SE", sh[ht])
                        for d in range(8):
                            E.op("tensor", "matmul", psx(ht),
                                 su[:, d, ht * 128:(ht + 1) * 128], xn_ap[d],
                                 start=(d == 0), stop=(d == 7),
                                 sem="TE" if d == 7 else None)
                        tu[ht] = E.cnt["TE"]
                    for ht in range(8):
                        E.wait("vector", "TE", tu[ht])
                        E.op("vector", "tensor_mul", hb2[:, ht, :], gs2[:, ht, :],
                             psx(ht), sem="VE")
                    ve_hb2 = E.cnt["VE"]
                    E.wait("tensor", "VE", ve_hb2)
                    for dt in range(8):
                        for ht in range(8):
                            E.op("tensor", "matmul", psx(dt),
                                 sd[:, ht, dt * 128:(dt + 1) * 128], hb2[:, ht, :],
                                 start=(ht == 0), stop=(ht == 7),
                                 sem="TE" if ht == 7 else None)
                        td[dt] = E.cnt["TE"]
                    for dt in range(8):
                        a_ap = acc[:, dt, jc * 512:(jc + 1) * 512]
                        E.wait("vector", "TE", td[dt])
                        E.op("vector", "tensor_add", a_ap, a_ap, psx(dt),
                             sem="VE")
                        vadd[dt] = E.cnt["VE"]
                E.wait("tensor", "VE")
                E.barrier()
                E.flush()

            with (
                nc.sbuf_tensor("xf2", [128, 512], F32) as xf2,
                nc.sbuf_tensor("af2", [128, 512], F32) as af2,
                nc.sbuf_tensor("of2", [128, 512], F32) as of2,
            ):
                for jc in range(4):
                    for dt in range(8):
                        E.dma("sync", xf2[:],
                              xfm[dt * 128:(dt + 1) * 128,
                                  jc * 512:(jc + 1) * 512])
                        E.dma("sync", af2[:],
                              att_dram[dt * 128:(dt + 1) * 128,
                                       jc * 512:(jc + 1) * 512])
                        E.wait("vector", "LD")
                        E.op("vector", "tensor_add", of2[:], xf2[:], af2[:])
                        E.op("vector", "tensor_add", of2[:], of2[:],
                             acc[:, dt, jc * 512:(jc + 1) * 512], sem="VE")
                        E.wait("sync", "VE")
                        E.dma("sync", x1fm[dt * 128:(dt + 1) * 128,
                                           jc * 512:(jc + 1) * 512], of2[:],
                              sem="ST")
                        E.wait("vector", "ST")
                if DEBUG_L0:
                    E.dma("sync", dbg_cfm[:], cfm[0:16, :], sem="ST")
                    E.dma("sync", dbg_xn[:], xn_dram[:], sem="ST")
                    E.dma("sync", dbg_v[:], v_dram[:], sem="ST")
                    E.dma("sync", dbg_u[:], u_dram[:], sem="ST")
                    E.dma("sync", dbg_att[:], att_dram[:], sem="ST")
                    E.dma("sync", dbg_pvu[:], pvu_dram[:], sem="ST")
                    for dt in range(8):
                        for jc in range(4):
                            E.dma("sync",
                                  dbg_acc[dt * 128:(dt + 1) * 128,
                                          jc * 512:(jc + 1) * 512],
                                  acc[:, dt, jc * 512:(jc + 1) * 512], sem="ST")
                E.wait("sync", "ST")
                E.barrier()
                E.flush()

    return nc




def build_l1(upto=99):
    nc = bass.Bass("TRN2", target_bir_lowering=False)
    xfull = nc.dram_tensor("xfull", [D, T], F32, kind="ExternalInput").ap()
    xq = nc.dram_tensor("xq", [D, 2048], F32, kind="ExternalInput").ap()
    wq = nc.dram_tensor("wq", [D, D], F32, kind="ExternalInput").ap()
    wdn = nc.dram_tensor("wdn", [D, 512], F32, kind="ExternalInput").ap()
    wku = nc.dram_tensor("wku", [512, D], F32, kind="ExternalInput").ap()
    wvu = nc.dram_tensor("wvu", [512, D], F32, kind="ExternalInput").ap()
    wo = nc.dram_tensor("wo", [D, D], F32, kind="ExternalInput").ap()
    w1 = nc.dram_tensor("w1", [D, 4096], F32, kind="ExternalInput").ap()
    b1_t = nc.dram_tensor("b1_t", [32, 128, 1], F32, kind="ExternalInput").ap()
    w2 = nc.dram_tensor("w2", [4096, D], F32, kind="ExternalInput").ap()
    b2_t = nc.dram_tensor("b2_t", [8, 128, 1], F32, kind="ExternalInput").ap()
    cosq = nc.dram_tensor("cosq", [2048, 512], F32, kind="ExternalInput").ap()
    sinq = nc.dram_tensor("sinq", [2048, 512], F32, kind="ExternalInput").ap()
    cosk = nc.dram_tensor("cosk", [T, 512], F32, kind="ExternalInput").ap()
    sink = nc.dram_tensor("sink", [T, 512], F32, kind="ExternalInput").ap()
    mbcol = nc.dram_tensor("mbcol", [128, 32, 4], F32, kind="ExternalInput").ap()
    mbq = nc.dram_tensor("mbq", [4, 6, 2, 128], F32, kind="ExternalInput").ap()

    x2fm = nc.dram_tensor("x2fm", [D, 2048], F32, kind="ExternalOutput").ap()

    att_dram = nc.dram_tensor("l1att", [D, 2048], F32, kind="Internal").ap()
    xqn_dram = nc.dram_tensor("l1xqn", [D, 2048], BF16, kind="Internal").ap()
    lat_dram = nc.dram_tensor("l1lat", [512, T], BF16, kind="Internal").ap()
    k_dram = nc.dram_tensor("l1k", [8, 128, T], BF16, kind="Internal").ap()
    q_dram = nc.dram_tensor("l1q", [8, 128, 2048], BF16, kind="Internal").ap()
    v_dram = nc.dram_tensor("l1v", [T, 16, 65], BF16, kind="Internal").ap()

    with ExitStack() as ctx:
        sems = std_sems(nc, ctx)
        block = ctx.enter_context(nc.Block())
        E = Emit(nc, block, sems)
        C = emit_consts(E, ctx, nc)
        register_fconsts(E, ctx, nc, [LN_EPS, -0.5, 1.0 / D, 0.125, -20.0])

        b2t = ctx.enter_context(nc.sbuf_tensor("b2t", [128, 8], F32))
        mbc = ctx.enter_context(nc.sbuf_tensor("mbc", [128, 32, 4], F32))
        mbqs = ctx.enter_context(nc.sbuf_tensor("mbqs", [2, 4, 6, 128], F32))
        ind2 = ctx.enter_context(nc.sbuf_tensor("ind2", [2, 512], F32))
        ind2b = ctx.enter_context(nc.sbuf_tensor("ind2b", [2, 512], F32))

        for d in range(8):
            E.dma("sync", b2t[:, d:d + 1], b2_t[d])
        E.dma("sync", mbc[:, :, :], mbcol[:, :, :])
        for p in range(4):
            for t in range(6):
                E.dma("sync", mbqs[:, p, t, :], mbq[p, t, :, :])
        E.op("gpsimd", "iota", ind2[0:2, :], [[1, 512]], base=0,
             channel_multiplier=0, allow_small_or_imprecise_dtypes=True)
        E.op("gpsimd", "iota", ind2b[0:2, :], [[0, 512]], base=0,
             channel_multiplier=1, allow_small_or_imprecise_dtypes=True,
             sem="GE")
        E.wait("vector", "GE")
        E.op("vector", "tensor_scalar", ind2[0:2, :], ind2[0:2, :], 255.5,
             None, op0=ALU.is_gt, sem="VE")
        E.wait("vector", "VE")
        E.op("vector", "tensor_tensor", ind2[0:2, :], ind2[0:2, :],
             ind2b[0:2, :], op=ALU.is_equal, sem="VE")
        E.barrier()
        E.flush()

        with (
            nc.sbuf_tensor("xt", [128, 8, 256], F32) as xt,
            nc.sbuf_tensor("sq", [128, 8, 256], F32) as sq,
            nc.sbuf_tensor("xnb", [128, 8, 256], BF16) as xnb,
            nc.sbuf_tensor("wds", [128, 8, 512], BF16) as wds,
            nc.sbuf_tensor("latc", [128, 4, 256], BF16) as latc,
            nc.sbuf_tensor("strow", [1, 256, 4], F32) as strow,
            nc.psum_tensor("psA", [1, 256], F32) as psA,
            nc.psum_tensor("psB", [1, 256], F32) as psB,
            nc.psum_tensor("psb", [128, 512], F32) as psb,
            nc.psum_tensor("psm", [128, 1024], F32) as psm,
        ):
            for d in range(8):
                E.dma("gpsimd", wds[:, d, :], wdn[d * 128:(d + 1) * 128, :],
                      sem="LD2")
            E.wait("tensor", "LD2")
            for src_ap, nch, is_own in ((xfull, 16, False), (xq, 8, True)):
                for j in range(nch):
                    for d in range(8):
                        E.dma("sync", xt[:, d, :],
                              src_ap[d * 128:(d + 1) * 128,
                                     j * 256:(j + 1) * 256])
                    E.wait("scalar", "LD")
                    E.wait("tensor", "LD")
                    xts = [xt[:, d, :] for d in range(8)]
                    sqs = [sq[:, d, :] for d in range(8)]
                    te = emit_ln_fm(E, C, xts, sqs, psA, psB, psb, strow, D)
                    E.wait("vector", "TE", te)
                    for d in range(8):
                        E.op("vector", "tensor_sub", sq[:, d, :], xt[:, d, :],
                             psb[:, 0:256], sem="VE" if d == 7 else None)
                    E.wait("vector", "VE")
                    for d in range(8):
                        E.op("vector", "tensor_mul", sq[:, d, :], sq[:, d, :],
                             psb[:, 256:512], sem="VE" if d == 7 else None)
                    E.wait("vector", "VE")
                    for d in range(8):
                        E.op("vector", "tensor_copy", xnb[:, d, :], sq[:, d, :],
                             sem="VE" if d == 7 else None)
                    ve = E.cnt["VE"]
                    if is_own:
                        E.wait("sync", "VE", ve)
                        for d in range(8):
                            E.dma("sync", xqn_dram[d * 128:(d + 1) * 128,
                                                   j * 256:(j + 1) * 256],
                                  xnb[:, d, :], sem="ST")
                        E.wait("scalar", "ST")
                    else:
                        E.wait("tensor", "VE", ve)
                        for lt in range(4):
                            sl = psm[:, lt * 256:(lt + 1) * 256]
                            for d in range(8):
                                E.op("tensor", "matmul", sl,
                                     wds[:, d, lt * 128:(lt + 1) * 128],
                                     xnb[:, d, :],
                                     start=(d == 0), stop=(d == 7),
                                     sem="TE" if (lt == 3 and d == 7) else None)
                        E.wait("vector", "TE")
                        for lt in range(4):
                            E.op("vector", "tensor_copy", latc[:, lt, :],
                                 psm[:, lt * 256:(lt + 1) * 256],
                                 sem="VE" if lt == 3 else None)
                        E.wait("sync", "VE")
                        for lt in range(4):
                            E.dma("sync", lat_dram[lt * 128:(lt + 1) * 128,
                                                   j * 256:(j + 1) * 256],
                                  latc[:, lt, :], sem="ST")
                        E.wait("tensor", "VE")
                        E.wait("vector", "ST")
                    E.wait("sync", "VE")
                    E.wait("scalar", "VE")
            E.wait("sync", "ST")
            E.barrier()
            E.flush()
        if upto <= 1:
            return nc

        with (
            nc.sbuf_tensor("wus", [128, 4, 2048], BF16) as wus,
            nc.sbuf_tensor("wqs", [128, 8, 1024], BF16) as wqs,
            nc.sbuf_tensor("latt", [128, 4, 128], BF16) as latt,
            nc.sbuf_tensor("xqt", [128, 8, 128], BF16) as xqt,
            nc.sbuf_tensor("rtm", [128, 1024], F32) as rtm,
            nc.sbuf_tensor("rsc", [128, 512], F32) as rsc,
            nc.sbuf_tensor("cst", [128, 1024], F32) as cst,
            nc.sbuf_tensor("rtb", [128, 1024], BF16) as rtb,
            nc.sbuf_tensor("ktb", [128, 128], BF16) as ktb,
            nc.sbuf_tensor("vtb", [128, 16, 65], BF16) as vtb,
            nc.psum_tensor("psk", [128, 1024], F32) as psk,
            nc.psum_tensor("psv", [128, 1024], F32) as psv,
            nc.psum_tensor("pstp", [128, 128], BF16) as pstp,
        ):
            for lt in range(4):
                E.dma("gpsimd", wus[:, lt, 0:1024],
                      wku[lt * 128:(lt + 1) * 128, :], sem="LD2")
                E.dma("gpsimd", wus[:, lt, 1024:2048],
                      wvu[lt * 128:(lt + 1) * 128, :], sem="LD2")
            for d in range(8):
                E.dma("gpsimd", wqs[:, d, :], wq[d * 128:(d + 1) * 128, :],
                      sem="LD2")
            E.op("vector", "memset", vtb[:, :, 64:65], 1.0, sem="VE")
            E.wait("tensor", "LD2")

            def rope_block(src_psum, cos_ap, sin_ap, dst_dram, tt):
                E.dma("sync", cst[:, 0:512], cos_ap)
                E.dma("sync", cst[:, 512:1024], sin_ap)
                E.wait("vector", "LD")
                kE = src_psum[:, 0:1024:2]
                kO = src_psum[:, 1:1024:2]
                E.op("vector", "tensor_mul", rtm[:, 0:1024:2], kE,
                     cst[:, 0:512])
                E.op("vector", "tensor_mul", rsc[:], kO, cst[:, 512:1024],
                     sem="VE")
                E.wait("vector", "VE")
                E.op("vector", "tensor_sub", rtm[:, 0:1024:2],
                     rtm[:, 0:1024:2], rsc[:])
                E.op("vector", "tensor_mul", rtm[:, 1:1024:2], kE,
                     cst[:, 512:1024], sem="VE")
                E.wait("vector", "VE")
                E.op("vector", "tensor_mul", rsc[:], kO, cst[:, 0:512],
                     sem="VE")
                E.wait("vector", "VE")
                E.op("vector", "tensor_add", rtm[:, 1:1024:2],
                     rtm[:, 1:1024:2], rsc[:], sem="VE")
                E.wait("vector", "VE")
                E.op("vector", "tensor_copy", rtb[:], rtm[:], sem="VE")
                ve = E.cnt["VE"]
                E.wait("tensor", "VE", ve)
                for dtile in range(8):
                    E.op("tensor", "transpose", pstp[:, :],
                         rtb[:, dtile * 128:(dtile + 1) * 128],
                         C["ident"][:], sem="TE")
                    E.wait("vector", "TE")
                    E.op("vector", "tensor_copy", ktb[:, :], pstp[:, :],
                         sem="VE")
                    E.wait("sync", "VE")
                    E.dma("sync", dst_dram[dtile, :, tt * 128:(tt + 1) * 128],
                          ktb[:, :], sem="ST")
                    E.wait("tensor", "VE")
                    E.wait("vector", "ST")
                E.wait("sync", "VE")

            for tt in range(32):
                for lt in range(4):
                    E.dma("sync", latt[:, lt, :],
                          lat_dram[lt * 128:(lt + 1) * 128,
                                   tt * 128:(tt + 1) * 128])
                E.wait("tensor", "LD")
                for ec in range(2):
                    for lt in range(4):
                        E.op("tensor", "matmul", psk[:, ec * 512:(ec + 1) * 512],
                             latt[:, lt, :],
                             wus[:, lt, ec * 512:(ec + 1) * 512],
                             start=(lt == 0), stop=(lt == 3))
                for ec in range(2):
                    for lt in range(4):
                        E.op("tensor", "matmul", psv[:, ec * 512:(ec + 1) * 512],
                             latt[:, lt, :],
                             wus[:, lt, 1024 + ec * 512:1024 + (ec + 1) * 512],
                             start=(lt == 0), stop=(lt == 3),
                             sem="TE" if (ec == 1 and lt == 3) else None)
                te = E.cnt["TE"]
                E.wait("vector", "TE", te)
                E.op("vector", "tensor_copy", vtb[:, :, 0:64], psv[:, :],
                     sem="VE")
                E.wait("sync", "VE")
                E.dma("sync", v_dram[tt * 128:(tt + 1) * 128, :, :],
                      vtb[:, :, :], sem="ST")
                E.wait("vector", "ST")
                rope_block(psk, cosk[tt * 128:(tt + 1) * 128, :],
                           sink[tt * 128:(tt + 1) * 128, :], k_dram, tt)
                E.wait("sync", "TE")
            for tt in range(16):
                for d in range(8):
                    E.dma("sync", xqt[:, d, :],
                          xqn_dram[d * 128:(d + 1) * 128,
                                   tt * 128:(tt + 1) * 128])
                E.wait("tensor", "LD")
                for ec in range(2):
                    for d in range(8):
                        E.op("tensor", "matmul", psk[:, ec * 512:(ec + 1) * 512],
                             xqt[:, d, :],
                             wqs[:, d, ec * 512:(ec + 1) * 512],
                             start=(d == 0), stop=(d == 7),
                             sem="TE" if (ec == 1 and d == 7) else None)
                te = E.cnt["TE"]
                E.wait("vector", "TE", te)
                rope_block(psk, cosq[tt * 128:(tt + 1) * 128, :],
                           sinq[tt * 128:(tt + 1) * 128, :], q_dram, tt)
                E.wait("sync", "TE")
            E.wait("sync", "ST")
            E.barrier()
            E.flush()
        if upto <= 3:
            return nc

        with (
          nc.sbuf_tensor("oatt", [64, 16, 2048], BF16) as oatt,
        ):
          with (
            nc.sbuf_tensor("kq", [128, T], BF16) as kq,
            nc.sbuf_tensor("qq", [128, 2048], BF16) as qq,
            nc.sbuf_tensor("vq", [128, 32, 130], BF16) as vq,
            nc.sbuf_tensor("ptile", [128, 2, 512], BF16) as ptile,
            nc.sbuf_tensor("lrec", [65, 512], F32) as lrec,
            nc.sbuf_tensor("ones64", [65, 128], F32) as ones64,
            nc.sbuf_tensor("otmp", [64, 512], F32) as otmp,
            nc.psum_tensor("pss", [128, 1024], F32) as pss,
            nc.psum_tensor("pso", [65, 2, 512], F32) as pso,
            nc.psum_tensor("psl", [128, 512], F32) as psl,
          ):
              E.op("vector", "memset", ones64[64:65, :], 1.0, sem="VE")
              E.wait("tensor", "VE")
              last_se = [None]
              for qtile in range(8):
                  E.dma("sync", kq[:, :], k_dram[qtile, :, :])
                  E.dma("sync", qq[:, :], q_dram[qtile, :, :])
                  for kvt in range(32):
                      E.dma("sync", vq[:, kvt, :],
                            v_dram[kvt * 128:(kvt + 1) * 128,
                                   2 * qtile:2 * qtile + 2, :]
                            .rearrange("p h e -> p (h e)"))
                  E.wait("tensor", "LD")
                  E.wait("scalar", "LD")
                  for p in range(4):
                      nkvt = 8 * p + 8
                      for half in range(2):
                          hh = 2 * qtile + half
                          qpart = 64 * half

                          def s_mm(kvt, sem_exp_prev):
                              sslot = kvt % 2
                              s_ap = pss[:, sslot * 512:(sslot + 1) * 512]
                              if sem_exp_prev is not None:
                                  E.wait("tensor", "SE", sem_exp_prev)
                              if kvt >= 8 * p + 2:
                                  E.op("tensor", "matmul", s_ap,
                                       kq[qpart:qpart + 64,
                                          kvt * 128:(kvt + 1) * 128],
                                       qq[qpart:qpart + 64,
                                          p * 512:(p + 1) * 512],
                                       start=True, stop=False)
                                  E.op("tensor", "matmul", s_ap,
                                       mbqs[0:2, p, kvt - 8 * p - 2, :],
                                       ind2[0:2, :], start=False, stop=True,
                                       sem="TE")
                              else:
                                  E.op("tensor", "matmul", s_ap,
                                       kq[qpart:qpart + 64,
                                          kvt * 128:(kvt + 1) * 128],
                                       qq[qpart:qpart + 64,
                                          p * 512:(p + 1) * 512],
                                       start=True, stop=True, sem="TE")
                              return E.cnt["TE"]

                          te_of = {}
                          se_of = {}
                          te_of[0] = s_mm(0, last_se[0])
                          for kvt in range(nkvt):
                              sslot = kvt % 2
                              s_ap = pss[:, sslot * 512:(sslot + 1) * 512]
                              E.wait("scalar", "TE", te_of[kvt])
                              E.op("scalar", "activation", ptile[:, sslot, :],
                                   s_ap, AF.Exp, scale=0.125,
                                   bias=mbc[:, kvt, p:p + 1], sem="SE")
                              se_of[kvt] = E.cnt["SE"]
                              if kvt + 1 < nkvt:
                                  te_of[kvt + 1] = s_mm(
                                      kvt + 1, se_of.get(kvt - 1))
                              E.wait("tensor", "SE", se_of[kvt])
                              E.op("tensor", "matmul", pso[:, half, :],
                                   vq[:, kvt, half * 65:half * 65 + 65],
                                   ptile[:, sslot, :],
                                   start=(kvt == 0), stop=(kvt == nkvt - 1),
                                   sem="TE" if kvt == nkvt - 1 else None)
                          last_se[0] = max(se_of[nkvt - 1], se_of[nkvt - 2])
                          te = E.cnt["TE"]
                          E.wait("vector", "TE", te)
                          E.op("vector", "reciprocal", lrec[64:65, :],
                               pso[64:65, half, :], sem="VE")
                          E.wait("vector", "VE")
                          E.op("vector", "tensor_copy", otmp[:, :],
                               pso[0:64, half, :], sem="VE")
                          E.wait("tensor", "VE")
                          E.op("tensor", "matmul", psl[:, :],
                               ones64[64:65, :], lrec[64:65, :],
                               start=True, stop=True, sem="TE")
                          E.wait("vector", "TE")
                          E.op("vector", "tensor_mul",
                               oatt[0:64, hh, p * 512:(p + 1) * 512],
                               otmp[:, :], psl[0:64, :], sem="VE")
                          E.wait("tensor", "VE")
                  E.wait("sync", "VE")
              E.barrier()
              E.flush()

          with (
              nc.sbuf_tensor("wos2", [64, 16, 1024], BF16) as wos2,
              nc.sbuf_tensor("aev", [128, 512], F32) as aev,
              nc.psum_tensor("psw", [128, 4 * 512], F32) as psw,
          ):
              for h2 in range(16):
                  E.dma("gpsimd", wos2[0:64, h2, :],
                        wo[64 * h2:64 * h2 + 64, :], sem="LD2")
              E.wait("tensor", "LD2")
              for dt in range(8):
                  for jc in range(4):
                      sl = psw[:, (jc % 4) * 512:(jc % 4) * 512 + 512]
                      for h2 in range(16):
                          E.op("tensor", "matmul", sl,
                               wos2[0:64, h2, dt * 128:(dt + 1) * 128],
                               oatt[0:64, h2, jc * 512:(jc + 1) * 512],
                               start=(h2 == 0), stop=(h2 == 15),
                               sem="TE" if h2 == 15 else None)
                      E.wait("vector", "TE")
                      E.op("vector", "tensor_copy", aev[:], sl, sem="VE")
                      E.wait("sync", "VE")
                      E.dma("sync", att_dram[dt * 128:(dt + 1) * 128,
                                             jc * 512:(jc + 1) * 512],
                            aev[:], sem="ST")
                      E.wait("vector", "ST")
                      E.wait("tensor", "VE")
              E.wait("sync", "ST")
              E.barrier()
              E.flush()

        with (
            nc.sbuf_tensor("w1s", [128, 8, 4096], BF16) as w1s,
            nc.sbuf_tensor("w2s", [128, 32, 1024], BF16) as w2s,
            nc.sbuf_tensor("b1s", [128, 32], F32) as b1s,
            nc.sbuf_tensor("xqc", [128, 8, 512], BF16) as xqc,
            nc.sbuf_tensor("hcb", [128, 32, 512], BF16) as hcb,
            nc.sbuf_tensor("xf3", [128, 512], F32) as xf3,
            nc.sbuf_tensor("af3", [128, 512], F32) as af3,
            nc.sbuf_tensor("of3", [128, 512], F32) as of3,
            nc.psum_tensor("psf", [128, 4 * 512], F32) as psf,
            nc.psum_tensor("psf2", [128, 4 * 512], F32) as psf2,
        ):
            for d in range(8):
                E.dma("gpsimd", w1s[:, d, :], w1[d * 128:(d + 1) * 128, :],
                      sem="LD2")
            for ht in range(32):
                E.dma("gpsimd", w2s[:, ht, :], w2[ht * 128:(ht + 1) * 128, :],
                      sem="LD2")
            for ht in range(32):
                E.dma("sync", b1s[:, ht:ht + 1], b1_t[ht])
            E.wait("tensor", "LD2")
            E.wait("tensor", "LD")
            E.wait("scalar", "LD")
            for jc in range(4):
                for d in range(8):
                    E.dma("sync", xqc[:, d, :],
                          xqn_dram[d * 128:(d + 1) * 128,
                                   jc * 512:(jc + 1) * 512])
                E.wait("tensor", "LD")
                for hg in range(4):
                    for hi in range(8):
                        ht = hg * 8 + hi
                        ps = (psf[:, hi * 512:(hi + 1) * 512] if hi < 4 else
                              psf2[:, (hi - 4) * 512:(hi - 3) * 512])
                        for d in range(8):
                            E.op("tensor", "matmul", ps,
                                 w1s[:, d, ht * 128:(ht + 1) * 128],
                                 xqc[:, d, :],
                                 start=(d == 0), stop=(d == 7),
                                 sem="TE" if (hi == 7 and d == 7) else None)
                    E.wait("scalar", "TE")
                    for hi in range(8):
                        ht = hg * 8 + hi
                        ps = (psf[:, hi * 512:(hi + 1) * 512] if hi < 4 else
                              psf2[:, (hi - 4) * 512:(hi - 3) * 512])
                        E.op("scalar", "activation", hcb[:, ht, :], ps,
                             AF.Relu, bias=b1s[:, ht:ht + 1],
                             sem="SE" if hi == 7 else None)
                    E.wait("tensor", "SE")
                for dt in range(8):
                    ps = (psf[:, (dt % 4) * 512:(dt % 4) * 512 + 512] if dt < 4
                          else psf2[:, (dt - 4) * 512:(dt - 3) * 512])
                    for ht in range(32):
                        E.op("tensor", "matmul", ps,
                             w2s[:, ht, dt * 128:(dt + 1) * 128], hcb[:, ht, :],
                             start=(ht == 0), stop=(ht == 31),
                             sem="TE" if (dt == 7 and ht == 31) else None)
                te = E.cnt["TE"]
                for dt in range(8):
                    ps = (psf[:, (dt % 4) * 512:(dt % 4) * 512 + 512] if dt < 4
                          else psf2[:, (dt - 4) * 512:(dt - 3) * 512])
                    E.dma("sync", xf3[:], xq[dt * 128:(dt + 1) * 128,
                                             jc * 512:(jc + 1) * 512])
                    E.dma("sync", af3[:], att_dram[dt * 128:(dt + 1) * 128,
                                                   jc * 512:(jc + 1) * 512])
                    E.wait("vector", "LD")
                    if dt == 0:
                        E.wait("vector", "TE", te)
                    E.op("vector", "tensor_scalar", of3[:], ps,
                         b2t[:, dt:dt + 1], None, op0=ALU.add, sem="VE")
                    E.wait("vector", "VE")
                    E.op("vector", "tensor_add", of3[:], of3[:], xf3[:],
                         sem="VE")
                    E.wait("vector", "VE")
                    E.op("vector", "tensor_add", of3[:], of3[:], af3[:],
                         sem="VE")
                    E.wait("sync", "VE")
                    E.dma("sync", x2fm[dt * 128:(dt + 1) * 128,
                                       jc * 512:(jc + 1) * 512], of3[:],
                          sem="ST")
                    E.wait("vector", "ST")
                E.wait("tensor", "VE")
                E.wait("sync", "TE")
            E.wait("sync", "ST")
            E.barrier()
            E.flush()

    return nc


_L1_CACHE = {}
ROPE_THETA = 10000.0


def _rope_tables_np(TT, Dh):
    inv = 1.0 / ROPE_THETA ** (np.arange(0, Dh, 2, dtype=np.float32) / Dh)
    ang = np.outer(np.arange(TT, dtype=np.float32), inv)
    return np.cos(ang).astype(np.float32), np.sin(ang).astype(np.float32)


def run_l1(inputs, x1):
    if "nc" not in _L1_CACHE:
        _L1_CACHE["nc"] = build_l1()
    nc = _L1_CACHE["nc"]
    w = inputs
    cos1, sin1 = _rope_tables_np(T, 64)
    cosk_full = np.tile(cos1, (1, 16))
    sink_full = np.tile(sin1, (1, 16))
    shared = dict(
        wq=np.ascontiguousarray(w["L1_attn_wq"]),
        wdn=np.ascontiguousarray(w["L1_attn_w_down"]),
        wku=np.ascontiguousarray(w["L1_attn_wk_up"]),
        wvu=np.ascontiguousarray(w["L1_attn_wv_up"]),
        wo=np.ascontiguousarray(w["L1_attn_wo"]),
        w1=np.ascontiguousarray(w["L1_ffn_w1"]),
        b1_t=tile_bias(w["L1_ffn_b1"]),
        w2=np.ascontiguousarray(w["L1_ffn_w2"]),
        b2_t=tile_bias(w["L1_ffn_b2"]),
        cosk=cosk_full, sink=sink_full,
    )
    in_maps = []
    for c in range(N_CORES):
        b, h = c // 2, c % 2
        pages = np.arange(h, 16, 2)
        own_rows = (pages[:, None] * 256 + np.arange(256)[None, :]).reshape(-1)
        m = dict(shared)
        m["xfull"] = np.ascontiguousarray(x1[b].T)
        m["xq"] = np.ascontiguousarray(x1[b][own_rows].T)
        m["cosq"] = np.ascontiguousarray(cosk_full[own_rows])
        m["sinq"] = np.ascontiguousarray(sink_full[own_rows])
        mb = np.full((128, 32, 4), -20.0, np.float32)
        mq = np.zeros((4, 6, 2, 128), np.float32)
        for p in range(4):
            pg0 = 4 * p + h
            pg1 = 4 * p + 2 + h
            for kvt in range(8 * p + 8):
                kv_lo = kvt * 128
                kv_page = kv_lo // 256
                full0 = kv_page > pg0
                full1 = kv_page > pg1
                if full0 and full1:
                    mb[:, kvt, p] = -1e9
                elif full0 or full1:
                    t = kvt - (8 * p + 2)
                    assert 0 <= t < 6, (p, kvt, h)
                    if full0:
                        mq[p, t, 0, :] = -1e9
                    if full1:
                        mq[p, t, 1, :] = -1e9
        m["mbcol"] = mb
        m["mbq"] = mq
        in_maps.append(m)
    res = run_bass_kernel_spmd(nc, in_maps, core_ids=list(range(N_CORES)))
    _L1_CACHE["res"] = res.results
    x2 = np.empty_like(x1)
    for c in range(N_CORES):
        b, h = c // 2, c % 2
        pages = np.arange(h, 16, 2)
        own_rows = (pages[:, None] * 256 + np.arange(256)[None, :]).reshape(-1)
        x2[b][own_rows] = res.results[c]["x2fm"].T
    return x2




def build_l2():
    nc = bass.Bass("TRN2", target_bir_lowering=False)
    NR = 2048
    xfm = nc.dram_tensor("xfm", [D, NR], F32, kind="ExternalInput").ap()
    ln1g = nc.dram_tensor("ln1g", [8, 128, 1], F32, kind="ExternalInput").ap()
    ln1b = nc.dram_tensor("ln1b", [8, 128, 1], F32, kind="ExternalInput").ap()
    ln2g = nc.dram_tensor("ln2g", [8, 128, 1], F32, kind="ExternalInput").ap()
    ln2b = nc.dram_tensor("ln2b", [8, 128, 1], F32, kind="ExternalInput").ap()
    tmk = nc.dram_tensor("tmk", [8, 128, 1], F32, kind="ExternalInput").ap()
    tmv = nc.dram_tensor("tmv", [8, 128, 1], F32, kind="ExternalInput").ap()
    tmr = nc.dram_tensor("tmr", [8, 128, 1], F32, kind="ExternalInput").ap()
    tdec = nc.dram_tensor("tdec", [8, 128, 1], F32, kind="ExternalInput").ap()
    tfir = nc.dram_tensor("tfir", [8, 128, 1], F32, kind="ExternalInput").ap()
    cmk = nc.dram_tensor("cmk", [8, 128, 1], F32, kind="ExternalInput").ap()
    cmr = nc.dram_tensor("cmr", [8, 128, 1], F32, kind="ExternalInput").ap()
    wkT = nc.dram_tensor("wkT", [D, D], F32, kind="ExternalInput").ap()
    wvT = nc.dram_tensor("wvT", [D, D], F32, kind="ExternalInput").ap()
    wrT = nc.dram_tensor("wrT", [D, D], F32, kind="ExternalInput").ap()
    woT = nc.dram_tensor("woT", [D, D], F32, kind="ExternalInput").ap()
    wckT = nc.dram_tensor("wckT", [D, 4096], F32, kind="ExternalInput").ap()
    wcvT = nc.dram_tensor("wcvT", [4096, D], F32, kind="ExternalInput").ap()
    wcrT = nc.dram_tensor("wcrT", [D, D], F32, kind="ExternalInput").ap()
    gwT = nc.dram_tensor("gwT", [D, 8], F32, kind="ExternalInput").ap()
    ebrow = nc.dram_tensor("ebrow", [1, 8], F32, kind="ExternalInput").ap()
    few1 = nc.dram_tensor("few1", [8, D, 2816], F32, kind="ExternalInput").ap()
    few2 = nc.dram_tensor("few2", [8, 2816, D], F32, kind="ExternalInput").ap()
    few3 = nc.dram_tensor("few3", [8, D, 2816], F32, kind="ExternalInput").ap()

    x3fm = nc.dram_tensor("x3fm", [D, NR], F32, kind="ExternalOutput").ap()

    xn_dram = nc.dram_tensor("l2xn", [D, NR], BF16, kind="Internal").ap()
    xn32_dram = nc.dram_tensor("l2xn32", [D, NR], F32, kind="Internal").ap()
    att_dram = nc.dram_tensor("l2att", [D, NR], F32, kind="Internal").ap()

    with ExitStack() as ctx:
        sems = std_sems(nc, ctx)
        block = ctx.enter_context(nc.Block())
        E = Emit(nc, block, sems)
        C = emit_consts(E, ctx, nc)
        register_fconsts(E, ctx, nc, [LN_EPS, -0.5, 1.0 / D])

        P8 = {}
        for nm, ap_ in (("ln1g", ln1g), ("ln1b", ln1b), ("ln2g", ln2g),
                        ("ln2b", ln2b), ("tmk", tmk), ("tmv", tmv),
                        ("tmr", tmr), ("tdec", tdec), ("tfir", tfir),
                        ("cmk", cmk), ("cmr", cmr)):
            t = ctx.enter_context(nc.sbuf_tensor("p_" + nm, [128, 8], F32))
            for d in range(8):
                E.dma("sync", t[:, d:d + 1], ap_[d])
            P8[nm] = t
        drv = ctx.enter_context(nc.sbuf_tensor("p_drv", [128, 8, 4], F32))
        E.wait("scalar", "LD")
        E.op("scalar", "activation", drv[:, :, 0], P8["tdec"][:, :], AF.Exp,
             sem="SE")
        E.wait("vector", "SE")
        E.op("vector", "tensor_scalar", drv[:, :, 0], drv[:, :, 0], -1.0, None,
             op0=ALU.mult, sem="VE")
        E.wait("vector", "LD")
        E.op("vector", "tensor_scalar", drv[:, :, 1], P8["tmk"][:, :], -1.0,
             1.0, op0=ALU.mult, op1=ALU.add)
        E.op("vector", "tensor_scalar", drv[:, :, 2], P8["tmv"][:, :], -1.0,
             1.0, op0=ALU.mult, op1=ALU.add)
        E.op("vector", "tensor_scalar", drv[:, :, 3], P8["tmr"][:, :], -1.0,
             1.0, op0=ALU.mult, op1=ALU.add, sem="VE")
        E.barrier()
        E.flush()

        with (
            nc.sbuf_tensor("xt", [128, 8, 512], F32) as xt,
            nc.sbuf_tensor("sq", [128, 8, 512], F32) as sq,
            nc.sbuf_tensor("xnb", [128, 8, 512], BF16) as xnb,
            nc.sbuf_tensor("strow", [1, 512, 4], F32) as strow,
            nc.psum_tensor("psA", [1, 512], F32) as psA,
            nc.psum_tensor("psB", [1, 512], F32) as psB,
            nc.psum_tensor("psb", [128, 1024], F32) as psb,
        ):
            for j in range(4):
                for d in range(8):
                    E.dma("sync", xt[:, d, :],
                          xfm[d * 128:(d + 1) * 128, j * 512:(j + 1) * 512])
                E.wait("scalar", "LD")
                E.wait("tensor", "LD")
                xts = [xt[:, d, :] for d in range(8)]
                sqs = [sq[:, d, :] for d in range(8)]
                te = emit_ln_fm(E, C, xts, sqs, psA, psB, psb, strow, D)
                E.wait("vector", "TE", te)
                for d in range(8):
                    E.op("vector", "tensor_sub", sq[:, d, :], xt[:, d, :],
                         psb[:, 0:512], sem="VE" if d == 7 else None)
                E.wait("vector", "VE")
                for d in range(8):
                    E.op("vector", "tensor_mul", sq[:, d, :], sq[:, d, :],
                         psb[:, 512:1024], sem="VE" if d == 7 else None)
                E.wait("vector", "VE")
                for d in range(8):
                    E.op("vector", "tensor_copy", xnb[:, d, :], sq[:, d, :],
                         sem="VE" if d == 7 else None)
                E.wait("sync", "VE")
                for d in range(8):
                    E.dma("sync", xn_dram[d * 128:(d + 1) * 128,
                                          j * 512:(j + 1) * 512],
                          xnb[:, d, :], sem="ST")
                    E.dma("sync", xn32_dram[d * 128:(d + 1) * 128,
                                            j * 512:(j + 1) * 512],
                          sq[:, d, :], sem="ST")
                E.wait("scalar", "ST")
                E.wait("vector", "ST")
                E.wait("tensor", "VE")
            E.wait("sync", "ST")
            E.barrier()
            E.flush()

        kvr_ctx = ExitStack()
        kf = kvr_ctx.enter_context(nc.sbuf_tensor("l2k", [128, 8, NR], BF16))
        vf = kvr_ctx.enter_context(nc.sbuf_tensor("l2v", [128, 8, NR], BF16))
        rf = kvr_ctx.enter_context(nc.sbuf_tensor("l2r", [128, 8, NR], BF16))
        with (
            nc.sbuf_tensor("xx", [128, 8, NR], BF16) as xx,
        ):
          with (
            nc.sbuf_tensor("xt2", [128, 8, 512], F32) as xt,
            nc.sbuf_tensor("sq2", [128, 8, 512], F32) as sq,
            nc.sbuf_tensor("strow2", [1, 512, 4], F32) as strow,
            nc.psum_tensor("psA2", [1, 512], F32) as psA,
            nc.psum_tensor("psB2", [1, 512], F32) as psB,
            nc.psum_tensor("psb2", [128, 1024], F32) as psb,
          ):
            for j in range(4):
                for d in range(8):
                    E.dma("sync", xt[:, d, :],
                          xn32_dram[d * 128:(d + 1) * 128,
                                    j * 512:(j + 1) * 512])
                E.wait("scalar", "LD")
                E.wait("tensor", "LD")
                xts = [xt[:, d, :] for d in range(8)]
                sqs = [sq[:, d, :] for d in range(8)]
                te = emit_ln_fm(E, C, xts, sqs, psA, psB, psb, strow, D)
                E.wait("vector", "TE", te)
                for d in range(8):
                    E.op("vector", "tensor_sub", sq[:, d, :], xt[:, d, :],
                         psb[:, 0:512], sem="VE" if d == 7 else None)
                E.wait("vector", "VE")
                for d in range(8):
                    E.op("vector", "tensor_mul", sq[:, d, :], sq[:, d, :],
                         psb[:, 512:1024], sem="VE" if d == 7 else None)
                E.wait("vector", "VE")
                for d in range(8):
                    E.op("vector", "tensor_scalar",
                         xx[:, d, j * 512:(j + 1) * 512], sq[:, d, :],
                         P8["ln1g"][:, d:d + 1], P8["ln1b"][:, d:d + 1],
                         op0=ALU.mult, op1=ALU.add,
                         sem="VE" if d == 7 else None)
                E.wait("tensor", "VE")
                E.wait("scalar", "VE")
                E.wait("sync", "VE")
            E.barrier()
            E.flush()
          with (
            nc.sbuf_tensor("mixb", [128, 8, NR], BF16) as mixb,
            nc.sbuf_tensor("wt2", [128, 8, 1024], BF16) as wt2,
            nc.psum_tensor("psm2", [128, 2048], F32) as psm,
          ):
            for mixsel, dst, act in (("tmk", kf, None), ("tmv", vf, None),
                                     ("tmr", rf, AF.Sigmoid)):
                omc = {"tmk": 1, "tmv": 2, "tmr": 3}[mixsel]
                wsrc = {"tmk": wkT, "tmv": wvT, "tmr": wrT}[mixsel]
                for d in range(8):
                    E.op("vector", "tensor_scalar", mixb[:, d, 0:512],
                         xx[:, d, 0:512], P8[mixsel][:, d:d + 1], None,
                         op0=ALU.mult, sem="VE" if d == 7 else None)
                E.wait("vector", "VE")
                for d in range(8):
                    E.op("vector", "tensor_scalar", mixb[:, d, 512:NR],
                         xx[:, d, 512:NR], P8[mixsel][:, d:d + 1], None,
                         op0=ALU.mult, sem="VE" if d == 7 else None)
                E.wait("vector", "VE")
                for d in range(8):
                    E.op("vector", "scalar_tensor_tensor", mixb[:, d, 512:NR],
                         xx[:, d, 0:NR - 512], drv[:, d:d + 1, omc],
                         mixb[:, d, 512:NR], op0=ALU.mult, op1=ALU.add,
                         sem="VE" if d == 7 else None)
                ve = E.cnt["VE"]
                E.wait("gpsimd", "TE")
                for d in range(8):
                    E.dma("gpsimd", wt2[:, d, :],
                          wsrc[d * 128:(d + 1) * 128, :], sem="LD2")
                E.wait("tensor", "LD2")
                E.wait("tensor", "VE", ve)
                evkey = "VE" if act is None else "SE"
                ev_done = {}
                it = 0
                for ot in range(8):
                    for jc in range(4):
                        sl = psm[:, (it % 4) * 512:(it % 4) * 512 + 512]
                        if it >= 4:
                            E.wait("tensor", evkey, ev_done[it - 4])
                        for d in range(8):
                            E.op("tensor", "matmul", sl,
                                 wt2[:, d, ot * 128:(ot + 1) * 128],
                                 mixb[:, d, jc * 512:(jc + 1) * 512],
                                 start=(d == 0), stop=(d == 7),
                                 sem="TE" if d == 7 else None)
                        if act is None:
                            E.wait("vector", "TE")
                            E.op("vector", "tensor_copy",
                                 dst[:, ot, jc * 512:(jc + 1) * 512], sl,
                                 sem="VE")
                            ev_done[it] = E.cnt["VE"]
                        else:
                            E.wait("scalar", "TE")
                            E.op("scalar", "activation",
                                 dst[:, ot, jc * 512:(jc + 1) * 512], sl,
                                 act, sem="SE")
                            ev_done[it] = E.cnt["SE"]
                        it += 1
                E.wait("tensor", evkey)
            E.barrier()
            E.flush()

        with (
            nc.sbuf_tensor("aa", [128, 8, 512], F32) as aa,
            nc.sbuf_tensor("bb", [128, 8, 512], F32) as bb,
            nc.sbuf_tensor("pp", [128, 8, 512], F32) as pp,
            nc.sbuf_tensor("s1", [128, 8, 512], F32) as s1,
            nc.sbuf_tensor("s2", [128, 8, 512], F32) as s2,
            nc.sbuf_tensor("s3", [128, 8, 512], F32) as s3,
        ):
            def vsem(meth, *a, **kw):
                E.op("vector", meth, *a, sem="VE", **kw)
                E.wait("vector", "VE")
                E.wait("scalar", "VE")

            E.op("vector", "memset", aa[:, :, :], 0.0)
            E.op("vector", "memset", bb[:, :, :], 0.0)
            E.op("vector", "memset", pp[:, :, :], -1e38, sem="VE")
            E.wait("vector", "VE")
            E.wait("scalar", "VE")
            for t in range(4):
                ks = kf[:, :, t * 512:(t + 1) * 512]
                vs = vf[:, :, t * 512:(t + 1) * 512]
                rs = rf[:, :, t * 512:(t + 1) * 512]
                A = aa[:, :, :]
                Bb = bb[:, :, :]
                P = pp[:, :, :]
                S1 = s1[:, :, :]
                S2 = s2[:, :, :]
                S3 = s3[:, :, :]
                for d in range(8):
                    E.op("vector", "tensor_scalar", s1[:, d, :], ks[:, d, :],
                         P8["tfir"][:, d:d + 1], None, op0=ALU.add,
                         sem="VE" if d == 7 else None)
                E.wait("vector", "VE")
                E.wait("scalar", "VE")
                vsem("tensor_tensor", S2, P, S1, op=ALU.max)
                E.op("vector", "tensor_sub", S1, S1, S2)
                E.op("vector", "tensor_sub", S3, P, S2, sem="VE")
                E.wait("scalar", "VE")
                E.wait("vector", "VE")
                E.op("scalar", "activation", S1, S1, AF.Exp)
                E.op("scalar", "activation", S3, S3, AF.Exp, sem="SE")
                E.wait("vector", "SE")
                vsem("tensor_mul", S2, S3, Bb)
                vsem("tensor_add", S2, S2, S1)
                vsem("reciprocal", S2, S2)
                E.op("vector", "tensor_mul", S3, S3, A)
                E.op("vector", "tensor_mul", S1, S1, vs, sem="VE")
                E.wait("vector", "VE")
                vsem("tensor_add", S3, S3, S1)
                vsem("tensor_mul", S3, S3, S2)
                vsem("tensor_mul", rs, rs, S3)
                for d in range(8):
                    E.op("vector", "tensor_scalar", s1[:, d, :], pp[:, d, :],
                         drv[:, d:d + 1, 0], None, op0=ALU.add,
                         sem="VE" if d == 7 else None)
                E.wait("vector", "VE")
                E.wait("scalar", "VE")
                vsem("tensor_tensor", S2, S1, ks, op=ALU.max)
                E.op("vector", "tensor_sub", S1, S1, S2)
                E.op("vector", "tensor_tensor", S3, ks, S2,
                     op=ALU.subtract, sem="VE")
                E.wait("scalar", "VE")
                E.wait("vector", "VE")
                E.op("scalar", "activation", S1, S1, AF.Exp)
                E.op("scalar", "activation", S3, S3, AF.Exp, sem="SE")
                E.wait("vector", "SE")
                E.op("vector", "tensor_mul", A, A, S1)
                E.op("vector", "tensor_mul", Bb, Bb, S1, sem="VE")
                E.wait("vector", "VE")
                vsem("tensor_mul", S1, S3, vs)
                E.op("vector", "tensor_add", A, A, S1)
                E.op("vector", "tensor_add", Bb, Bb, S3, sem="VE")
                E.wait("vector", "VE")
                E.wait("scalar", "VE")
                vsem("tensor_copy", P, S2)
            E.barrier()
            E.flush()

        xi_ctx = ExitStack()
        xi = xi_ctx.enter_context(
            nc.sbuf_tensor("l2xi", [128, 8, NR], BF16, side="right"))
        with (
            nc.sbuf_tensor("wt3", [128, 8, 1024], BF16) as wt3,
            nc.sbuf_tensor("xnt2", [128, 4, 512], BF16) as xnt2,
            nc.psum_tensor("psm4", [128, 2048], F32) as psm,
        ):
            for d in range(8):
                E.dma("gpsimd", wt3[:, d, :], woT[d * 128:(d + 1) * 128, :],
                      sem="LD2")
            E.wait("tensor", "LD2")
            ev4 = {}
            ld4 = {}
            it = 0
            for ot in range(8):
                for jc in range(4):
                    sl = psm[:, (it % 4) * 512:(it % 4) * 512 + 512]
                    xb = xnt2[:, it % 4, :]
                    deng = "sync" if it % 2 == 0 else "gpsimd"
                    dsem = "LD" if it % 2 == 0 else "LD2"
                    if it >= 4:
                        E.wait(deng, "VE", ev4[it - 4])
                    E.dma(deng, xb,
                          xn_dram[ot * 128:(ot + 1) * 128,
                                  jc * 512:(jc + 1) * 512], sem=dsem)
                    ld4[it] = (dsem, E.cnt[dsem])
                    if it >= 4:
                        E.wait("tensor", "VE", ev4[it - 4])
                    for d in range(8):
                        E.op("tensor", "matmul", sl,
                             wt3[:, d, ot * 128:(ot + 1) * 128],
                             rf[:, d, jc * 512:(jc + 1) * 512],
                             start=(d == 0), stop=(d == 7),
                             sem="TE" if d == 7 else None)
                    E.wait("vector", ld4[it][0], ld4[it][1])
                    E.wait("vector", "TE")
                    E.op("vector", "tensor_add",
                         xi[:, ot, jc * 512:(jc + 1) * 512], sl, xb,
                         sem="VE")
                    ev4[it] = E.cnt["VE"]
                    it += 1
            E.wait("tensor", "VE")
            E.wait("sync", "VE")
            E.wait("gpsimd", "VE")
            E.barrier()
            E.flush()
        kvr_ctx.close()

        with (
            nc.sbuf_tensor("yy", [128, 8, NR], BF16) as yy,
        ):
            with (
                nc.sbuf_tensor("sq5", [128, 8, 512], F32) as sq,
                nc.sbuf_tensor("strow5", [1, 512, 4], F32) as strow,
                nc.psum_tensor("psA5", [1, 512], F32) as psA,
                nc.psum_tensor("psB5", [1, 512], F32) as psB,
                nc.psum_tensor("psb5", [128, 1024], F32) as psb,
            ):
                for j in range(4):
                    xts = [xi[:, d, j * 512:(j + 1) * 512] for d in range(8)]
                    sqs = [sq[:, d, :] for d in range(8)]
                    E.wait("scalar", "VE")
                    E.wait("tensor", "VE")
                    te = emit_ln_fm(E, C, xts, sqs, psA, psB, psb, strow, D,
                                    ones=C["ones_colb"])
                    E.wait("vector", "TE", te)
                    for d in range(8):
                        E.op("vector", "tensor_sub", sq[:, d, :], xts[d],
                             psb[:, 0:512], sem="VE" if d == 7 else None)
                    E.wait("vector", "VE")
                    for d in range(8):
                        E.op("vector", "tensor_mul", sq[:, d, :], sq[:, d, :],
                             psb[:, 512:1024], sem="VE" if d == 7 else None)
                    E.wait("vector", "VE")
                    for d in range(8):
                        E.op("vector", "tensor_scalar",
                             yy[:, d, j * 512:(j + 1) * 512], sq[:, d, :],
                             P8["ln2g"][:, d:d + 1], P8["ln2b"][:, d:d + 1],
                             op0=ALU.mult, op1=ALU.add,
                             sem="VE" if d == 7 else None)
                    E.wait("tensor", "VE")
                    E.wait("scalar", "VE")
                E.barrier()
                E.flush()

            with (
                nc.sbuf_tensor("mixc", [128, 8, NR], BF16) as mixc,
                nc.sbuf_tensor("rrall", [128, 8, NR], BF16) as rrall,
                nc.sbuf_tensor("omk", [128, 8, 2], F32) as omk,
            ):
                E.op("vector", "tensor_scalar", omk[:, :, 0], P8["cmk"][:, :],
                     -1.0, 1.0, op0=ALU.mult, op1=ALU.add)
                E.op("vector", "tensor_scalar", omk[:, :, 1], P8["cmr"][:, :],
                     -1.0, 1.0, op0=ALU.mult, op1=ALU.add, sem="VE")
                E.wait("vector", "VE")
                with (
                    nc.sbuf_tensor("mixr2", [128, 8, NR], BF16) as mixr2,
                    nc.sbuf_tensor("wcr_s", [128, 8, 1024], BF16) as wcr_s,
                    nc.psum_tensor("ps_r", [128, 2048], F32) as ps_r,
                ):
                    for d in range(8):
                        E.op("vector", "tensor_scalar", mixr2[:, d, 0:NR],
                             yy[:, d, 0:NR], P8["cmr"][:, d:d + 1], None,
                             op0=ALU.mult, sem="VE" if d == 7 else None)
                    E.wait("vector", "VE")
                    for d in range(8):
                        E.op("vector", "scalar_tensor_tensor",
                             mixr2[:, d, 512:NR], yy[:, d, 0:NR - 512],
                             omk[:, d:d + 1, 1], mixr2[:, d, 512:NR],
                             op0=ALU.mult, op1=ALU.add,
                             sem="VE" if d == 7 else None)
                    E.wait("vector", "VE")
                    E.wait("tensor", "VE")
                    for d in range(8):
                        E.dma("gpsimd", wcr_s[:, d, :],
                              wcrT[d * 128:(d + 1) * 128, :], sem="LD2")
                    E.wait("tensor", "LD2")
                    se_done = {}
                    it = 0
                    for jc in range(4):
                        for ot in range(8):
                            sl = ps_r[:, (it % 4) * 512:(it % 4) * 512 + 512]
                            if it >= 4:
                                E.wait("tensor", "SE", se_done[it - 4])
                            for d in range(8):
                                E.op("tensor", "matmul", sl,
                                     wcr_s[:, d, ot * 128:(ot + 1) * 128],
                                     mixr2[:, d, jc * 512:(jc + 1) * 512],
                                     start=(d == 0), stop=(d == 7),
                                     sem="TE" if d == 7 else None)
                            E.wait("scalar", "TE")
                            E.op("scalar", "activation",
                                 rrall[:, ot, jc * 512:(jc + 1) * 512], sl,
                                 AF.Sigmoid, sem="SE")
                            se_done[it] = E.cnt["SE"]
                            it += 1
                    E.wait("tensor", "SE")
                    E.barrier()
                    E.flush()
                for d in range(8):
                    E.op("vector", "tensor_scalar", mixc[:, d, 0:NR],
                         yy[:, d, 0:NR], P8["cmk"][:, d:d + 1], None,
                         op0=ALU.mult, sem="VE" if d == 7 else None)
                E.wait("vector", "VE")
                for d in range(8):
                    E.op("vector", "scalar_tensor_tensor", mixc[:, d, 512:NR],
                         yy[:, d, 0:NR - 512], omk[:, d:d + 1, 0],
                         mixc[:, d, 512:NR], op0=ALU.mult, op1=ALU.add,
                         sem="VE" if d == 7 else None)
                E.wait("vector", "VE")
                E.wait("tensor", "VE")
                with (
                    nc.sbuf_tensor("hk", [128, 8, 512], BF16) as hk,
                    nc.sbuf_tensor("wck_s", [128, 8, 1024], BF16) as wck_s,
                    nc.sbuf_tensor("wcv_s", [128, 8, 1024], BF16) as wcv_s,
                    nc.sbuf_tensor("sq8", [128, 4, 512], F32) as sq8,
                    nc.sbuf_tensor("aev2", [128, 4, 512], F32) as aev2,
                    nc.sbuf_tensor("vvac", [128, 8, 512], F32) as vvac,
                    nc.psum_tensor("ps_k", [128, 2048], F32) as ps_k,
                    nc.psum_tensor("ps_v2", [128, 2048], F32) as ps_v2,
                ):
                    for jc in range(4):
                        for hg in range(4):
                            E.wait("gpsimd", "TE")
                            for d in range(8):
                                E.dma("gpsimd", wck_s[:, d, :],
                                      wckT[d * 128:(d + 1) * 128,
                                           hg * 1024:(hg + 1) * 1024],
                                      sem="LD2")
                            for ht in range(8):
                                E.dma("gpsimd", wcv_s[:, ht, :],
                                      wcvT[hg * 1024 + ht * 128:
                                           hg * 1024 + (ht + 1) * 128, :],
                                      sem="LD2")
                            E.wait("tensor", "LD2")
                            for ht in range(8):
                                sl = (ps_k[:, ht * 512:(ht + 1) * 512]
                                      if ht < 4 else
                                      ps_v2[:, (ht - 4) * 512:(ht - 3) * 512])
                                for d in range(8):
                                    E.op("tensor", "matmul", sl,
                                         wck_s[:, d, ht * 128:(ht + 1) * 128],
                                         mixc[:, d, jc * 512:(jc + 1) * 512],
                                         start=(d == 0), stop=(d == 7),
                                         sem="TE" if (ht == 7 and d == 7)
                                         else None)
                            E.wait("vector", "TE")
                            for hs in range(2):
                                if hs == 1:
                                    E.wait("vector", "SE")
                                for hq in range(4):
                                    ht = hs * 4 + hq
                                    sl = (ps_k[:, ht * 512:(ht + 1) * 512]
                                          if ht < 4 else
                                          ps_v2[:, (ht - 4) * 512:
                                                (ht - 3) * 512])
                                    E.op("vector", "tensor_scalar",
                                         sq8[:, hq, :], sl, 0.0, None,
                                         op0=ALU.max,
                                         sem="VE" if hq == 3 else None)
                                E.wait("scalar", "VE")
                                for hq in range(4):
                                    ht = hs * 4 + hq
                                    E.op("scalar", "activation",
                                         hk[:, ht, :], sq8[:, hq, :],
                                         AF.Square,
                                         sem="SE" if hq == 3 else None)
                            E.wait("tensor", "SE")
                            E.wait("vector", "SE")
                            dslots = [ps_k[:, 0:512], ps_k[:, 512:1024],
                                      ps_v2[:, 0:512], ps_v2[:, 512:1024]]
                            dv = {}
                            for dt in range(8):
                                sl = dslots[dt % 4]
                                if dt >= 4:
                                    E.wait("tensor", "VE", dv[dt - 4])
                                for ht in range(8):
                                    E.op("tensor", "matmul", sl,
                                         wcv_s[:, ht, dt * 128:(dt + 1) * 128],
                                         hk[:, ht, :],
                                         start=(ht == 0), stop=(ht == 7),
                                         sem="TE" if ht == 7 else None)
                                E.wait("vector", "TE")
                                if hg == 0:
                                    E.op("vector", "tensor_copy",
                                         vvac[:, dt, :], sl, sem="VE")
                                else:
                                    E.op("vector", "tensor_add",
                                         vvac[:, dt, :], vvac[:, dt, :], sl,
                                         sem="VE")
                                dv[dt] = E.cnt["VE"]
                            E.wait("tensor", "VE")
                        E.wait("vector", "VE")
                        for ds in range(2):
                            if ds == 1:
                                E.wait("vector", "ST")
                            for dq in range(4):
                                dt = ds * 4 + dq
                                E.op("vector", "tensor_mul", aev2[:, dq, :],
                                     vvac[:, dt, :],
                                     rrall[:, dt, jc * 512:(jc + 1) * 512],
                                     sem="VE" if dq == 3 else None)
                            E.wait("vector", "VE")
                            for dq in range(4):
                                dt = ds * 4 + dq
                                E.op("vector", "tensor_add", aev2[:, dq, :],
                                     aev2[:, dq, :],
                                     xi[:, dt, jc * 512:(jc + 1) * 512],
                                     sem="VE" if dq == 3 else None)
                            E.wait("sync", "VE")
                            for dq in range(4):
                                dt = ds * 4 + dq
                                E.dma("sync",
                                      att_dram[dt * 128:(dt + 1) * 128,
                                               jc * 512:(jc + 1) * 512],
                                      aev2[:, dq, :], sem="ST")
                        E.wait("vector", "ST")
                    E.wait("sync", "ST")
                    E.barrier()
                    E.flush()
        xi_ctx.close()

        acc = ctx.enter_context(nc.sbuf_tensor("l2acc", [128, 8, NR], F32))
        cfm = ctx.enter_context(nc.sbuf_tensor("l2cfm", [8, NR], F32))
        selc8 = ctx.enter_context(nc.sbuf_tensor("l2sel", [8, 8 * 128], F32))
        with (
            nc.sbuf_tensor("gws", [128, 8, 8], F32) as gws,
            nc.sbuf_tensor("xn32c", [128, 8, 512], F32) as xn32c,
            nc.sbuf_tensor("lgf", [8, NR], F32) as lgf,
            nc.sbuf_tensor("ltm", [128, 16, 8], F32) as ltm,
            nc.sbuf_tensor("sct", [128, 16, 8], F32) as sct,
            nc.sbuf_tensor("btm", [128, 16, 8], F32) as btm,
            nc.sbuf_tensor("eqt", [128, 16, 8], F32) as eqt,
            nc.sbuf_tensor("top8", [128, 16, 8], F32) as top8,
            nc.sbuf_tensor("smx", [128, 16, 2], F32) as smx,
            nc.sbuf_tensor("ebb", [128, 8], F32) as ebb,
            nc.psum_tensor("ps_l", [8, 512], F32) as ps_l,
            nc.psum_tensor("ps_t", [128, 8], F32) as ps_t,
            nc.psum_tensor("ps_t2", [8, 128], F32) as ps_t2,
            nc.psum_tensor("ps_e", [128, 8], F32) as ps_e,
        ):
            E.op("gpsimd", "iota", selc8[:], [[1, 8], [0, 128]], sem="GE",
                 base=0, channel_multiplier=-1,
                 allow_small_or_imprecise_dtypes=True)
            E.wait("vector", "GE")
            E.op("vector", "tensor_scalar", selc8[:], selc8[:], 0.0, 1.0,
                 op0=ALU.is_equal, op1=ALU.mult, sem="VE")
            for d in range(8):
                E.dma("sync", gws[:, d, :], gwT[d * 128:(d + 1) * 128, :])
            E.dma("sync", ebb[0:1, :], ebrow[0:1, :])
            E.wait("tensor", "LD")
            E.op("tensor", "matmul", ps_e[:, :], C["ones_row"][0:1, :],
                 ebb[0:1, :], start=True, stop=True, sem="TE")
            E.wait("vector", "TE")
            E.op("vector", "tensor_copy", ebb[:, :], ps_e[:, :], sem="VE")
            E.wait("tensor", "VE")
            for jc in range(4):
                for d in range(8):
                    E.dma("sync", xn32c[:, d, :],
                          xn32_dram[d * 128:(d + 1) * 128,
                                    jc * 512:(jc + 1) * 512])
                E.wait("tensor", "LD")
                for d in range(8):
                    E.op("tensor", "matmul", ps_l[:, :], gws[:, d, :],
                         xn32c[:, d, :], start=(d == 0), stop=(d == 7),
                         sem="TE" if d == 7 else None)
                E.wait("vector", "TE")
                E.op("vector", "tensor_copy",
                     lgf[0:8, jc * 512:(jc + 1) * 512], ps_l[:, :], sem="VE")
                E.wait("tensor", "VE")
                E.wait("sync", "TE")
            for g in range(16):
                E.op("tensor", "transpose", ps_t[:, :],
                     lgf[0:8, g * 128:(g + 1) * 128], C["identf"][0:8, 0:8],
                     sem="TE")
                E.wait("vector", "TE")
                E.op("vector", "tensor_copy", ltm[:, g, :], ps_t[:, :],
                     sem="VE")
                E.wait("tensor", "VE")
            E.wait("vector", "VE")
            for g in range(16):
                E.op("vector", "max", top8[:, g, :], ltm[:, g, :],
                     sem="VE" if g == 15 else None)
            E.wait("vector", "VE")
            for g in range(16):
                E.op("vector", "tensor_scalar", sct[:, g, :], ltm[:, g, :],
                     top8[:, g, 0:1], None, op0=ALU.subtract,
                     sem="VE" if g == 15 else None)
            E.wait("vector", "VE")
            E.wait("scalar", "VE")
            for g in range(16):
                E.op("scalar", "activation", sct[:, g, :], sct[:, g, :],
                     AF.Exp, sem="SE" if g == 15 else None)
            E.wait("vector", "SE")
            for g in range(16):
                E.op("vector", "tensor_reduce", smx[:, g, 0:1], sct[:, g, :],
                     axis=mybir.AxisListType.X, op=ALU.add,
                     sem="VE" if g == 15 else None)
            E.wait("vector", "VE")
            for g in range(16):
                E.op("vector", "reciprocal", smx[:, g, 1:2], smx[:, g, 0:1],
                     sem="VE" if g == 15 else None)
            E.wait("vector", "VE")
            for g in range(16):
                E.op("vector", "tensor_scalar", sct[:, g, :], sct[:, g, :],
                     smx[:, g, 1:2], None, op0=ALU.mult,
                     sem="VE" if g == 15 else None)
            E.wait("vector", "VE")
            for g in range(16):
                E.op("vector", "tensor_add", btm[:, g, :], sct[:, g, :],
                     ebb[:, :], sem="VE" if g == 15 else None)
            E.wait("vector", "VE")
            for g in range(16):
                E.op("vector", "max", top8[:, g, :], btm[:, g, :],
                     sem="VE" if g == 15 else None)
            E.wait("vector", "VE")
            for g in range(16):
                E.op("vector", "tensor_scalar", eqt[:, g, :], btm[:, g, :],
                     top8[:, g, 0:1], None, op0=ALU.is_equal,
                     sem="VE" if g == 15 else None)
            E.wait("vector", "VE")
            for g in range(16):
                E.op("vector", "tensor_scalar", btm[:, g, :], btm[:, g, :],
                     top8[:, g, 1:2], None, op0=ALU.is_equal,
                     sem="VE" if g == 15 else None)
            E.wait("vector", "VE")
            for g in range(16):
                E.op("vector", "tensor_add", eqt[:, g, :], eqt[:, g, :],
                     btm[:, g, :], sem="VE" if g == 15 else None)
            E.wait("vector", "VE")
            for g in range(16):
                E.op("vector", "tensor_mul", sct[:, g, :], sct[:, g, :],
                     eqt[:, g, :], sem="VE" if g == 15 else None)
            E.wait("vector", "VE")
            E.wait("tensor", "VE")
            for g in range(16):
                E.op("tensor", "transpose", ps_t2[:, :], sct[:, g, :],
                     C["identf"][:], sem="TE")
                E.wait("vector", "TE")
                E.op("vector", "tensor_copy", cfm[0:8, g * 128:(g + 1) * 128],
                     ps_t2[0:8, :], sem="VE")
                E.wait("tensor", "VE")
            E.barrier()
            E.flush()

        with (
            nc.sbuf_tensor("xnc", [128, 8, NR], BF16) as xnc,
            nc.sbuf_tensor("xe", [128, 8, NR], BF16) as xe,
            nc.sbuf_tensor("hhg", [128, 4, NR], BF16) as hhg,
            nc.sbuf_tensor("w13", [128, 8, 1024], BF16) as w13,
            nc.sbuf_tensor("w2g", [128, 4, 1024], BF16) as w2g,
            nc.sbuf_tensor("gsc2", [128, 4, 512], F32) as gsc2,
            nc.psum_tensor("ps_g", [128, 2048], F32) as ps_g,
            nc.psum_tensor("ps_u", [128, 2048], F32) as ps_u,
        ):
            for d in range(8):
                E.dma("sync", xnc[:, d, :], xn_dram[d * 128:(d + 1) * 128, :])
            E.wait("tensor", "LD")
            E.wait("vector", "LD")
            for e in range(8):
                for jc in range(4):
                    E.op("tensor", "matmul", ps_g[:, 0:512],
                         selc8[0:8, e * 128:(e + 1) * 128],
                         cfm[0:8, jc * 512:(jc + 1) * 512],
                         start=True, stop=True, sem="TE")
                    E.wait("vector", "TE")
                    for d in range(8):
                        E.op("vector", "tensor_mul",
                             xe[:, d, jc * 512:(jc + 1) * 512],
                             xnc[:, d, jc * 512:(jc + 1) * 512],
                             ps_g[:, 0:512], sem="VE" if d == 7 else None)
                    E.wait("tensor", "VE")
                    E.wait("scalar", "VE")
                for hg in range(6):
                    h0 = hg * 4
                    nht = min(4, 22 - h0)
                    E.wait("gpsimd", "TE")
                    E.wait("gpsimd", "VE")
                    for d in range(8):
                        E.dma("gpsimd", w13[:, d, 0:nht * 128],
                              few1[e, d * 128:(d + 1) * 128,
                                   h0 * 128:(h0 + nht) * 128], sem="LD2")
                        E.dma("gpsimd", w13[:, d, 512:512 + nht * 128],
                              few3[e, d * 128:(d + 1) * 128,
                                   h0 * 128:(h0 + nht) * 128], sem="LD2")
                    for hi in range(nht):
                        E.dma("gpsimd", w2g[:, hi, :],
                              few2[e, (h0 + hi) * 128:(h0 + hi + 1) * 128, :],
                              sem="LD2")
                    E.wait("tensor", "LD2")
                    subs = []
                    for jc in range(4):
                        for hp in range(0, nht, 2):
                            subs.append((jc, hp, min(2, nht - hp)))
                    ve_done = {}
                    for si, (jc, hp, nh2) in enumerate(subs):
                        par = si % 2
                        if si >= 2:
                            E.wait("tensor", "VE", ve_done[si - 2])
                        for hi in range(nh2):
                            b = par * 2 + hi
                            for d in range(8):
                                E.op("tensor", "matmul",
                                     ps_g[:, b * 512:(b + 1) * 512],
                                     w13[:, d, (hp + hi) * 128:
                                         (hp + hi + 1) * 128],
                                     xe[:, d, jc * 512:(jc + 1) * 512],
                                     start=(d == 0), stop=(d == 7))
                            for d in range(8):
                                E.op("tensor", "matmul",
                                     ps_u[:, b * 512:(b + 1) * 512],
                                     w13[:, d, 512 + (hp + hi) * 128:
                                         512 + (hp + hi + 1) * 128],
                                     xe[:, d, jc * 512:(jc + 1) * 512],
                                     start=(d == 0), stop=(d == 7),
                                     sem="TE" if (hi == nh2 - 1 and d == 7)
                                     else None)
                        E.wait("scalar", "TE")
                        for hi in range(nh2):
                            b = par * 2 + hi
                            E.op("scalar", "activation", gsc2[:, b, :],
                                 ps_g[:, b * 512:(b + 1) * 512], AF.Silu,
                                 sem="SE" if hi == nh2 - 1 else None)
                        E.wait("vector", "SE")
                        for hi in range(nh2):
                            b = par * 2 + hi
                            E.op("vector", "tensor_mul",
                                 hhg[:, hp + hi, jc * 512:(jc + 1) * 512],
                                 gsc2[:, b, :],
                                 ps_u[:, b * 512:(b + 1) * 512],
                                 sem="VE" if hi == nh2 - 1 else None)
                        ve_done[si] = E.cnt["VE"]
                    E.wait("tensor", "VE")
                    E.wait("scalar", "VE")
                    slots = [ps_g[:, 0:512], ps_g[:, 512:1024],
                             ps_u[:, 0:512], ps_u[:, 512:1024]]
                    dve_done = {}
                    it = 0
                    for jc in range(4):
                        for dt in range(8):
                            sl = slots[it % 4]
                            if it >= 4:
                                E.wait("tensor", "VE", dve_done[it - 4])
                            for hi in range(nht):
                                E.op("tensor", "matmul", sl,
                                     w2g[:, hi, dt * 128:(dt + 1) * 128],
                                     hhg[:, hi, jc * 512:(jc + 1) * 512],
                                     start=(hi == 0), stop=(hi == nht - 1),
                                     sem="TE" if hi == nht - 1 else None)
                            E.wait("vector", "TE")
                            a_ap = acc[:, dt, jc * 512:(jc + 1) * 512]
                            if e == 0 and hg == 0:
                                E.op("vector", "tensor_copy", a_ap, sl,
                                     sem="VE")
                            else:
                                E.op("vector", "tensor_add", a_ap, a_ap, sl,
                                     sem="VE")
                            dve_done[it] = E.cnt["VE"]
                            it += 1
                    E.wait("tensor", "VE")
            E.barrier()
            E.flush()

        with (
            nc.sbuf_tensor("xf4", [128, 512], F32) as xf4,
            nc.sbuf_tensor("af4", [128, 512], F32) as af4,
            nc.sbuf_tensor("of4", [128, 512], F32) as of4,
        ):
            for jc in range(4):
                for dt in range(8):
                    E.dma("sync", xf4[:],
                          xfm[dt * 128:(dt + 1) * 128, jc * 512:(jc + 1) * 512])
                    E.dma("sync", af4[:],
                          att_dram[dt * 128:(dt + 1) * 128,
                                   jc * 512:(jc + 1) * 512])
                    E.wait("vector", "LD")
                    E.op("vector", "tensor_add", of4[:], xf4[:], af4[:],
                         sem="VE")
                    E.wait("vector", "VE")
                    E.op("vector", "tensor_add", of4[:], of4[:],
                         acc[:, dt, jc * 512:(jc + 1) * 512], sem="VE")
                    E.wait("sync", "VE")
                    E.dma("sync", x3fm[dt * 128:(dt + 1) * 128,
                                       jc * 512:(jc + 1) * 512], of4[:],
                          sem="ST")
                    E.wait("vector", "ST")
            E.wait("sync", "ST")
            E.barrier()
            E.flush()

    return nc



_L2_CACHE = {}


def run_l2(inputs, x2):
    if "nc" not in _L2_CACHE:
        _L2_CACHE["nc"] = build_l2()
    nc = _L2_CACHE["nc"]
    w = inputs
    shared = dict(
        ln1g=tile_bias(w["L2_attn_ln1_g"]), ln1b=tile_bias(w["L2_attn_ln1_b"]),
        ln2g=tile_bias(w["L2_attn_ln2_g"]), ln2b=tile_bias(w["L2_attn_ln2_b"]),
        tmk=tile_bias(w["L2_attn_time_mix_k"]),
        tmv=tile_bias(w["L2_attn_time_mix_v"]),
        tmr=tile_bias(w["L2_attn_time_mix_r"]),
        tdec=tile_bias(w["L2_attn_time_decay"]),
        tfir=tile_bias(w["L2_attn_time_first"]),
        cmk=tile_bias(w["L2_attn_cmix_k"]), cmr=tile_bias(w["L2_attn_cmix_r"]),
        wkT=np.ascontiguousarray(w["L2_attn_Wk"].T),
        wvT=np.ascontiguousarray(w["L2_attn_Wv"].T),
        wrT=np.ascontiguousarray(w["L2_attn_Wr"].T),
        woT=np.ascontiguousarray(w["L2_attn_Wo"].T),
        wckT=np.ascontiguousarray(w["L2_attn_Wck"].T),
        wcvT=np.ascontiguousarray(w["L2_attn_Wcv"].T),
        wcrT=np.ascontiguousarray(w["L2_attn_Wcr"].T),
        gwT=np.ascontiguousarray(w["L2_ffn_gate_w"].T),
        ebrow=np.ascontiguousarray(w["L2_ffn_expert_bias"][None, :]),
        few1=np.ascontiguousarray(w["L2_ffn_w1"]),
        few2=np.ascontiguousarray(w["L2_ffn_w2"]),
        few3=np.ascontiguousarray(w["L2_ffn_w3"]),
    )
    in_maps = []
    for c in range(N_CORES):
        sl = x2[:, c * 512:(c + 1) * 512, :]
        rows = sl.reshape(4 * 512, D)
        m = dict(shared)
        m["xfm"] = np.ascontiguousarray(rows.T)
        in_maps.append(m)
    res = run_bass_kernel_spmd(nc, in_maps, core_ids=list(range(N_CORES)))
    _L2_CACHE["res"] = res.results
    x3 = np.empty_like(x2)
    for c in range(N_CORES):
        out = res.results[c]["x3fm"].T.reshape(4, 512, D)
        x3[:, c * 512:(c + 1) * 512, :] = out
    return x3




def build_l3a():
    nc = bass.Bass("TRN2", target_bir_lowering=False)
    xfm = nc.dram_tensor("xfm", [D, T], F32, kind="ExternalInput").ap()
    w_xc = nc.dram_tensor("w_xc", [D, 2048], F32, kind="ExternalInput").ap()
    w_z = nc.dram_tensor("w_z", [D, 1024], F32, kind="ExternalInput").ap()
    convw = nc.dram_tensor("convw", [16, 128, 4], F32, kind="ExternalInput").ap()
    convb = nc.dram_tensor("convb", [16, 128, 1], F32, kind="ExternalInput").ap()
    w_x = nc.dram_tensor("w_x", [2048, 32], F32, kind="ExternalInput").ap()
    w_dt = nc.dram_tensor("w_dt", [16, 1024], F32, kind="ExternalInput").ap()
    bdt = nc.dram_tensor("bdt", [8, 128, 1], F32, kind="ExternalInput").ap()
    alog = nc.dram_tensor("alog", [8, 128, 16], F32, kind="ExternalInput").ap()
    dvec = nc.dram_tensor("dvec", [8, 128, 1], F32, kind="ExternalInput").ap()

    ygz = nc.dram_tensor("ygz", [1024, T], BF16, kind="ExternalOutput").ap()

    xn_dram = nc.dram_tensor("l3xn", [D, T], BF16, kind="Internal").ap()
    xc_dram = nc.dram_tensor("l3xc", [2048, T], BF16, kind="Internal").ap()
    z_dram = nc.dram_tensor("l3z", [1024, T], BF16, kind="Internal").ap()
    xs_dram = nc.dram_tensor("l3xs", [2048, T], BF16, kind="Internal").ap()
    delta_dram = nc.dram_tensor("l3delta", [1024, T], F32, kind="Internal").ap()
    bmr_dram = nc.dram_tensor("l3bmr", [16, T], F32, kind="Internal").ap()

    with ExitStack() as ctx:
        sems = std_sems(nc, ctx)
        block = ctx.enter_context(nc.Block())
        E = Emit(nc, block, sems)
        C = emit_consts(E, ctx, nc)
        register_fconsts(E, ctx, nc, [LN_EPS, -0.5, 1.0 / D, -1.0])

        bdt_s = ctx.enter_context(nc.sbuf_tensor("bdt_s", [128, 8], F32))
        dv_s = ctx.enter_context(nc.sbuf_tensor("dv_s", [128, 8], F32))
        acol = ctx.enter_context(nc.sbuf_tensor("acol", [128, 8, 16], F32))
        cw_s = ctx.enter_context(nc.sbuf_tensor("cw_s", [128, 16, 4], F32))
        cb_s = ctx.enter_context(nc.sbuf_tensor("cb_s", [128, 16], F32))

        for d8 in range(8):
            E.dma("sync", bdt_s[:, d8:d8 + 1], bdt[d8])
            E.dma("sync", dv_s[:, d8:d8 + 1], dvec[d8])
            E.dma("sync", acol[:, d8, :], alog[d8])
        for nt in range(16):
            E.dma("sync", cw_s[:, nt, :], convw[nt])
            E.dma("sync", cb_s[:, nt:nt + 1], convb[nt])
        E.wait("scalar", "LD")
        E.op("scalar", "activation", acol[:, :, :], acol[:, :, :], AF.Exp,
             sem="SE")
        E.wait("vector", "SE")
        E.op("vector", "tensor_scalar", acol[:, :, :], acol[:, :, :], -1.0,
             None, op0=ALU.mult, sem="VE")
        E.barrier()
        E.flush()

        with (
            nc.sbuf_tensor("xt", [128, 8, 512], F32) as xt,
            nc.sbuf_tensor("sq", [128, 8, 512], F32) as sq,
            nc.sbuf_tensor("xnb", [128, 8, 512], BF16) as xnb,
            nc.sbuf_tensor("strow", [1, 512, 4], F32) as strow,
            nc.psum_tensor("psA", [1, 512], F32) as psA,
            nc.psum_tensor("psB", [1, 512], F32) as psB,
            nc.psum_tensor("psb", [128, 1024], F32) as psb,
        ):
            for j in range(8):
                for d in range(8):
                    E.dma("sync", xt[:, d, :],
                          xfm[d * 128:(d + 1) * 128, j * 512:(j + 1) * 512])
                E.wait("scalar", "LD")
                E.wait("tensor", "LD")
                xts = [xt[:, d, :] for d in range(8)]
                sqs = [sq[:, d, :] for d in range(8)]
                te = emit_ln_fm(E, C, xts, sqs, psA, psB, psb, strow, D)
                E.wait("vector", "TE", te)
                for d in range(8):
                    E.op("vector", "tensor_sub", sq[:, d, :], xt[:, d, :],
                         psb[:, 0:512], sem="VE" if d == 7 else None)
                E.wait("vector", "VE")
                for d in range(8):
                    E.op("vector", "tensor_mul", xnb[:, d, :], sq[:, d, :],
                         psb[:, 512:1024], sem="VE" if d == 7 else None)
                E.wait("sync", "VE")
                for d in range(8):
                    E.dma("sync", xn_dram[d * 128:(d + 1) * 128,
                                          j * 512:(j + 1) * 512],
                          xnb[:, d, :], sem="ST")
                E.wait("vector", "ST")
                E.wait("scalar", "VE")
                E.wait("tensor", "VE")
            E.wait("sync", "ST")
            E.barrier()
            E.flush()

        with (
            nc.sbuf_tensor("wxc", [128, 8, 2048], BF16) as wxc,
            nc.sbuf_tensor("wz", [128, 8, 1024], BF16) as wz,
            nc.sbuf_tensor("xnc", [128, 8, 512], BF16) as xnc,
            nc.sbuf_tensor("oev", [128, 512], BF16) as oev,
            nc.psum_tensor("psm", [128, 2048], F32) as psm,
        ):
            for d in range(8):
                E.dma("gpsimd", wxc[:, d, :], w_xc[d * 128:(d + 1) * 128, :],
                      sem="LD2")
                E.dma("gpsimd", wz[:, d, :], w_z[d * 128:(d + 1) * 128, :],
                      sem="LD2")
            E.wait("tensor", "LD2")
            for jc in range(8):
                E.wait("sync", "TE")
                for d in range(8):
                    E.dma("sync", xnc[:, d, :],
                          xn_dram[d * 128:(d + 1) * 128,
                                  jc * 512:(jc + 1) * 512])
                E.wait("tensor", "LD")
                for ot in range(24):
                    sl = psm[:, (ot % 4) * 512:(ot % 4) * 512 + 512]
                    wsrc = (wxc[:, :, ot * 128:(ot + 1) * 128] if ot < 16
                            else wz[:, :, (ot - 16) * 128:(ot - 15) * 128])
                    for d in range(8):
                        E.op("tensor", "matmul", sl,
                             wsrc[:, d, :], xnc[:, d, :],
                             start=(d == 0), stop=(d == 7),
                             sem="TE" if d == 7 else None)
                    E.wait("vector", "TE")
                    E.op("vector", "tensor_copy", oev[:, :], sl, sem="VE")
                    E.wait("sync", "VE")
                    dst = (xc_dram[ot * 128:(ot + 1) * 128,
                                   jc * 512:(jc + 1) * 512] if ot < 16 else
                           z_dram[(ot - 16) * 128:(ot - 15) * 128,
                                  jc * 512:(jc + 1) * 512])
                    E.dma("sync", dst, oev[:, :], sem="ST")
                    E.wait("vector", "ST")
                    E.wait("tensor", "VE")
            E.wait("sync", "ST")
            E.barrier()
            E.flush()

        with (
            nc.sbuf_tensor("xcc", [128, 16, 515], BF16) as xcc,
            nc.sbuf_tensor("cacc", [128, 512], F32) as cacc,
            nc.sbuf_tensor("xso", [128, 512], BF16) as xso,
        ):
            for jc in range(8):
                E.wait("sync", "VE")
                for nt in range(16):
                    if jc == 0:
                        E.op("vector", "memset", xcc[:, nt, 0:3], 0.0,
                             sem="VE" if nt == 15 else None)
                    else:
                        E.dma("sync", xcc[:, nt, 0:3],
                              xc_dram[nt * 128:(nt + 1) * 128,
                                      jc * 512 - 3:jc * 512])
                    E.dma("sync", xcc[:, nt, 3:515],
                          xc_dram[nt * 128:(nt + 1) * 128,
                                  jc * 512:(jc + 1) * 512])
                if jc == 0:
                    E.wait("sync", "VE")
                    E.wait("vector", "VE")
                E.wait("vector", "LD")
                E.wait("scalar", "LD")
                for nt in range(16):
                    E.op("vector", "tensor_scalar", cacc[:, :],
                         xcc[:, nt, 0:512], cw_s[:, nt, 0:1], None,
                         op0=ALU.mult, sem="VE")
                    E.wait("vector", "VE")
                    for kk in range(1, 4):
                        E.op("vector", "scalar_tensor_tensor", cacc[:, :],
                             xcc[:, nt, kk:kk + 512], cw_s[:, nt, kk:kk + 1],
                             cacc[:, :], op0=ALU.mult, op1=ALU.add, sem="VE")
                        E.wait("vector", "VE")
                    E.wait("scalar", "VE")
                    E.op("scalar", "activation", xso[:, :], cacc[:, :],
                         AF.Silu, bias=cb_s[:, nt:nt + 1], sem="SE")
                    E.wait("sync", "SE")
                    E.dma("sync", xs_dram[nt * 128:(nt + 1) * 128,
                                          jc * 512:(jc + 1) * 512],
                          xso[:, :], sem="ST")
                    E.wait("scalar", "ST")
                    E.wait("vector", "SE")
            E.wait("sync", "ST")
            E.barrier()
            E.flush()

        with (
            nc.sbuf_tensor("wxs", [128, 16, 32], BF16) as wxs,
            nc.sbuf_tensor("xst", [128, 16, 512], BF16) as xst,
            nc.sbuf_tensor("draw", [16, T], F32) as draw,
            nc.sbuf_tensor("bmrs", [16, 512], F32) as bmrs,
            nc.sbuf_tensor("wdts", [16, 1024], BF16) as wdts,
            nc.sbuf_tensor("drb", [16, 512], BF16) as drb,
            nc.sbuf_tensor("spo", [128, 512], F32) as spo,
            nc.sbuf_tensor("spo2", [128, 512], F32) as spo2,
            nc.psum_tensor("psx", [16, 512], F32) as psx,
            nc.psum_tensor("psx2", [16, 512], F32) as psx2,
            nc.psum_tensor("psd", [128, 2048], F32) as psd,
        ):
            for nt in range(16):
                E.dma("gpsimd", wxs[:, nt, :], w_x[nt * 128:(nt + 1) * 128, :],
                      sem="LD2")
            E.dma("gpsimd", wdts[:, :], w_dt[:, :], sem="LD2")
            E.wait("tensor", "LD2")
            for jc in range(8):
                E.wait("sync", "TE")
                for nt in range(16):
                    E.dma("sync", xst[:, nt, :],
                          xs_dram[nt * 128:(nt + 1) * 128,
                                  jc * 512:(jc + 1) * 512])
                E.wait("tensor", "LD")
                for nt in range(16):
                    E.op("tensor", "matmul", psx[:, :], wxs[:, nt, 0:16],
                         xst[:, nt, :], start=(nt == 0), stop=(nt == 15))
                for nt in range(16):
                    E.op("tensor", "matmul", psx2[:, :], wxs[:, nt, 16:32],
                         xst[:, nt, :], start=(nt == 0), stop=(nt == 15),
                         sem="TE" if nt == 15 else None)
                E.wait("vector", "TE")
                E.op("vector", "tensor_copy", draw[0:16, jc * 512:(jc + 1) * 512],
                     psx[0:16, :], sem="VE")
                E.op("vector", "tensor_copy", bmrs[0:16, :], psx2[0:16, :],
                     sem="VE")
                E.wait("sync", "VE")
                E.dma("sync", bmr_dram[:, jc * 512:(jc + 1) * 512],
                      bmrs[0:16, :], sem="ST")
                E.wait("tensor", "VE")
                E.wait("vector", "ST")
            E.wait("sync", "ST")
            E.wait("vector", "VE")
            for jc in range(8):
                E.op("vector", "tensor_copy", drb[0:16, 0:512],
                     draw[0:16, jc * 512:(jc + 1) * 512], sem="VE")
                E.wait("tensor", "VE")
                for ot in range(8):
                    sl = psd[:, (ot % 4) * 512:(ot % 4) * 512 + 512]
                    E.op("tensor", "matmul", sl,
                         wdts[0:16, ot * 128:(ot + 1) * 128], drb[0:16, 0:512],
                         start=True, stop=True, sem="TE")
                    E.wait("vector", "TE")
                    E.op("vector", "tensor_scalar", spo[:, :], sl,
                         bdt_s[:, ot:ot + 1], None, op0=ALU.add, sem="VE")
                    E.wait("vector", "VE")
                    E.wait("scalar", "VE")
                    E.op("scalar", "activation", spo2[:, :], spo[:, :], AF.Abs,
                         sem="SE")
                    E.wait("scalar", "SE")
                    E.op("scalar", "activation", spo2[:, :], spo2[:, :],
                         AF.Exp, scale=-1.0, sem="SE")
                    E.wait("scalar", "SE")
                    E.op("scalar", "activation", spo2[:, :], spo2[:, :],
                         AF.Ln, bias=1.0, sem="SE")
                    E.wait("vector", "SE")
                    E.op("vector", "tensor_scalar", spo[:, :], spo[:, :], 0.0,
                         None, op0=ALU.max, sem="VE")
                    E.wait("vector", "VE")
                    E.op("vector", "tensor_add", spo[:, :], spo[:, :],
                         spo2[:, :], sem="VE")
                    E.wait("sync", "VE")
                    E.dma("sync", delta_dram[ot * 128:(ot + 1) * 128,
                                             jc * 512:(jc + 1) * 512],
                          spo[:, :], sem="ST")
                    E.wait("vector", "ST")
                    E.wait("scalar", "ST")
                    E.wait("tensor", "VE")
            E.wait("sync", "ST")
            E.barrier()
            E.flush()

        NHOFF = 1024
        with (
            nc.sbuf_tensor("dlt", [128, 8, 512], F32) as dlt,
            nc.sbuf_tensor("xsc", [128, 8, 512], BF16) as xsc,
            nc.sbuf_tensor("zc", [128, 8, 512], BF16) as zc,
            nc.sbuf_tensor("duc", [128, 8, 512], F32) as duc,
            nc.sbuf_tensor("brep", [128, 512], F32) as brep,
            nc.sbuf_tensor("dA8", [128, 8, 512], F32) as dA8,
            nc.sbuf_tensor("dBu8", [128, 8, 512], F32) as dBu8,
            nc.sbuf_tensor("h8", [128, 8, 512], F32) as h8,
            nc.sbuf_tensor("hsc", [128, 512], F32) as hsc,
            nc.sbuf_tensor("carry", [128, 128], F32) as carry,
            nc.sbuf_tensor("yac", [128, 8, 512], F32) as yac,
            nc.sbuf_tensor("sz", [128, 512], F32) as sz,
            nc.sbuf_tensor("ygo", [128, 512], BF16) as ygo,
        ):
            E.op("vector", "memset", carry[:, :], 0.0, sem="VE")
            E.wait("vector", "VE")
            for jc in range(8):
                E.wait("sync", "VE")
                E.wait("sync", "SE")
                for nt in range(8):
                    E.dma("sync", dlt[:, nt, :],
                          delta_dram[nt * 128:(nt + 1) * 128,
                                     jc * 512:(jc + 1) * 512])
                    E.dma("sync", xsc[:, nt, :],
                          xs_dram[nt * 128:(nt + 1) * 128,
                                  jc * 512:(jc + 1) * 512])
                    E.dma("sync", zc[:, nt, :],
                          z_dram[nt * 128:(nt + 1) * 128,
                                 jc * 512:(jc + 1) * 512])
                E.wait("vector", "LD")
                E.wait("scalar", "LD")
                for nt in range(8):
                    E.op("vector", "tensor_mul", duc[:, nt, :], dlt[:, nt, :],
                         xsc[:, nt, :], sem="VE" if nt == 7 else None)
                E.wait("vector", "VE")
                E.op("vector", "memset", yac[:, :, :], 0.0, sem="VE")
                E.wait("vector", "VE")
                E.wait("gpsimd", "VE")
                for dd in range(16):
                    E.dma("gpsimd", brep[:, :],
                          bmr_dram[dd, jc * 512:(jc + 1) * 512]
                          .partition_broadcast(128), sem="LD2")
                    E.wait("vector", "LD2")
                    E.wait("scalar", "LD2")
                    for nt in range(8):
                        E.op("scalar", "activation", dA8[:, nt, :],
                             dlt[:, nt, :], AF.Exp,
                             scale=acol[:, nt, dd:dd + 1],
                             sem="SE" if nt == 7 else None)
                    for nt in range(8):
                        E.op("vector", "tensor_mul", dBu8[:, nt, :],
                             duc[:, nt, :], brep[:, :],
                             sem="VE" if nt == 7 else None)
                    E.wait("vector", "VE")
                    E.wait("vector", "SE")
                    for nt in range(8):
                        cidx = nt * 16 + dd
                        E.op("vector", "tensor_tensor_scan", h8[:, nt, :],
                             dA8[:, nt, :], dBu8[:, nt, :],
                             carry[:, cidx:cidx + 1],
                             op0=ALU.mult, op1=ALU.add,
                             sem="VE" if nt == 7 else None)
                    E.wait("vector", "VE")
                    for nt in range(8):
                        cidx = nt * 16 + dd
                        E.op("vector", "tensor_copy", carry[:, cidx:cidx + 1],
                             h8[:, nt, 511:512],
                             sem="VE" if nt == 7 else None)
                    for nt in range(8):
                        E.op("vector", "tensor_mul", dBu8[:, nt, :],
                             h8[:, nt, :], brep[:, :],
                             sem="VE" if nt == 7 else None)
                    E.wait("vector", "VE")
                    for nt in range(8):
                        E.op("vector", "tensor_add", yac[:, nt, :],
                             yac[:, nt, :], dBu8[:, nt, :],
                             sem="VE" if nt == 7 else None)
                    E.wait("vector", "VE")
                    E.wait("scalar", "VE")
                    E.wait("gpsimd", "VE")
                for nt in range(8):
                    E.op("scalar", "activation", sz[:, :], zc[:, nt, :],
                         AF.Silu, sem="SE")
                    E.wait("vector", "SE")
                    E.op("vector", "scalar_tensor_tensor", hsc[:, :],
                         xsc[:, nt, :], dv_s[:, nt:nt + 1], yac[:, nt, :],
                         op0=ALU.mult, op1=ALU.add, sem="VE")
                    E.wait("vector", "VE")
                    E.op("vector", "tensor_mul", ygo[:, :], hsc[:, :],
                         sz[:, :], sem="VE")
                    E.wait("sync", "VE")
                    E.dma("sync", ygz[nt * 128:(nt + 1) * 128,
                                      jc * 512:(jc + 1) * 512],
                          ygo[:, :], sem="ST")
                    E.wait("vector", "ST")
                    E.wait("scalar", "VE")
            E.wait("sync", "ST")
            E.barrier()
            E.flush()

    return nc



def build_l3b():
    nc = bass.Bass("TRN2", target_bir_lowering=False)
    xq3 = nc.dram_tensor("xq3", [D, 2048], F32, kind="ExternalInput").ap()
    ygzo = nc.dram_tensor("ygzo", [2048, 2048], BF16, kind="ExternalInput").ap()
    wout = nc.dram_tensor("wout", [2048, D], F32, kind="ExternalInput").ap()
    wT3 = nc.dram_tensor("wT3", [D, D], F32, kind="ExternalInput").ap()
    bT3 = nc.dram_tensor("bT3", [8, 128, 1], F32, kind="ExternalInput").ap()
    ldT = nc.dram_tensor("ldT", [100, 4], F32, kind="ExternalInput").ap()
    luT = nc.dram_tensor("luT", [4, 50], F32, kind="ExternalInput").ap()
    daux = nc.dram_tensor("daux", [100, 1024], F32, kind="ExternalInput").ap()
    uaT = nc.dram_tensor("uaT", [50, 1024], F32, kind="ExternalInput").ap()

    outfm = nc.dram_tensor("outfm", [D, 2048], F32, kind="ExternalOutput").ap()

    xn_dram = nc.dram_tensor("l3bxn", [D, 2048], BF16, kind="Internal").ap()

    with ExitStack() as ctx:
        sems = std_sems(nc, ctx)
        block = ctx.enter_context(nc.Block())
        E = Emit(nc, block, sems)
        C = emit_consts(E, ctx, nc)
        register_fconsts(E, ctx, nc, [LN_EPS, -0.5, 1.0 / D])

        bT3s = ctx.enter_context(nc.sbuf_tensor("bT3s", [128, 8], F32))
        for d in range(8):
            E.dma("sync", bT3s[:, d:d + 1], bT3[d])
        E.barrier()
        E.flush()

        with (
            nc.sbuf_tensor("xt", [128, 8, 512], F32) as xt,
            nc.sbuf_tensor("sq", [128, 8, 512], F32) as sq,
            nc.sbuf_tensor("xnb", [128, 8, 512], BF16) as xnb,
            nc.sbuf_tensor("strow", [1, 512, 4], F32) as strow,
            nc.psum_tensor("psA", [1, 512], F32) as psA,
            nc.psum_tensor("psB", [1, 512], F32) as psB,
            nc.psum_tensor("psb", [128, 1024], F32) as psb,
        ):
            for j in range(4):
                for d in range(8):
                    E.dma("sync", xt[:, d, :],
                          xq3[d * 128:(d + 1) * 128, j * 512:(j + 1) * 512])
                E.wait("scalar", "LD")
                E.wait("tensor", "LD")
                xts = [xt[:, d, :] for d in range(8)]
                sqs = [sq[:, d, :] for d in range(8)]
                te = emit_ln_fm(E, C, xts, sqs, psA, psB, psb, strow, D)
                E.wait("vector", "TE", te)
                for d in range(8):
                    E.op("vector", "tensor_sub", sq[:, d, :], xt[:, d, :],
                         psb[:, 0:512], sem="VE" if d == 7 else None)
                E.wait("vector", "VE")
                for d in range(8):
                    E.op("vector", "tensor_mul", xnb[:, d, :], sq[:, d, :],
                         psb[:, 512:1024], sem="VE" if d == 7 else None)
                E.wait("sync", "VE")
                for d in range(8):
                    E.dma("sync", xn_dram[d * 128:(d + 1) * 128,
                                          j * 512:(j + 1) * 512],
                          xnb[:, d, :], sem="ST")
                E.wait("vector", "ST")
                E.wait("scalar", "VE")
                E.wait("tensor", "VE")
            E.wait("sync", "ST")
            E.barrier()
            E.flush()

        es2 = ExitStack()
        ldTs = es2.enter_context(nc.sbuf_tensor("ldTs", [100, 4], BF16))
        luTs = es2.enter_context(nc.sbuf_tensor("luTs", [4, 50], BF16))
        dauxs = es2.enter_context(nc.sbuf_tensor("dauxs", [100, 1024], BF16))
        uaTs = es2.enter_context(nc.sbuf_tensor("uaTs", [50, 8, 128], BF16))
        d1s = es2.enter_context(nc.sbuf_tensor("d1s", [4, 1024], BF16))
        d2Ts = es2.enter_context(nc.sbuf_tensor("d2Ts", [128, 8, 50], BF16))
        wos = es2.enter_context(nc.sbuf_tensor("wos", [128, 16, 1024], BF16))
        wT3s = es2.enter_context(nc.sbuf_tensor("wT3s", [128, 8, 1024], BF16))
        ygt = es2.enter_context(nc.sbuf_tensor("ygt", [128, 16, 512], BF16))
        xnc = es2.enter_context(nc.sbuf_tensor("xnc", [128, 8, 512], BF16))
        t1s = es2.enter_context(nc.sbuf_tensor("t1s", [50, 512], BF16))
        xf5 = es2.enter_context(nc.sbuf_tensor("xf5", [128, 512], F32))
        of5 = es2.enter_context(nc.sbuf_tensor("of5", [128, 512], F32))
        ps1 = es2.enter_context(nc.psum_tensor("ps1", [4, 1024], F32))
        ps2 = es2.enter_context(nc.psum_tensor("ps2", [128, 50], F32))
        ps3 = es2.enter_context(nc.psum_tensor("ps3", [50, 512], F32))
        pso = es2.enter_context(nc.psum_tensor("pso", [128, 2048], F32))
        if True:
            E.dma("gpsimd", ldTs[:, :], ldT[:, :], sem="LD2")
            E.dma("gpsimd", luTs[:, :], luT[:, :], sem="LD2")
            E.dma("gpsimd", dauxs[:, :], daux[:, :], sem="LD2")
            for u in range(8):
                E.dma("gpsimd", uaTs[0:50, u, :], uaT[:, u * 128:(u + 1) * 128],
                      sem="LD2")
            for nt in range(16):
                E.dma("gpsimd", wos[:, nt, :], wout[nt * 128:(nt + 1) * 128, :],
                      sem="LD2")
            for d in range(8):
                E.dma("gpsimd", wT3s[:, d, :], wT3[d * 128:(d + 1) * 128, :],
                      sem="LD2")
            E.wait("tensor", "LD2")
            for hc in range(2):
                E.op("tensor", "matmul", ps1[:, hc * 512:(hc + 1) * 512],
                     ldTs[0:100, :], dauxs[0:100, hc * 512:(hc + 1) * 512],
                     start=True, stop=True, sem="TE" if hc == 1 else None)
            E.wait("vector", "TE")
            E.op("vector", "tensor_copy", d1s[0:4, :], ps1[0:4, :], sem="VE")
            E.wait("tensor", "VE")
            for ch in range(8):
                E.op("tensor", "matmul", ps2[:, :],
                     d1s[0:4, ch * 128:(ch + 1) * 128], luTs[0:4, :],
                     start=True, stop=True, sem="TE")
                E.wait("vector", "TE")
                E.op("vector", "tensor_copy", d2Ts[:, ch, :], ps2[:, :],
                     sem="VE")
                E.wait("tensor", "VE")
            for jc in range(4):
                E.wait("sync", "TE")
                for nt in range(16):
                    E.dma("sync", ygt[:, nt, :],
                          ygzo[nt * 128:(nt + 1) * 128,
                               jc * 512:(jc + 1) * 512])
                for d in range(8):
                    E.dma("sync", xnc[:, d, :],
                          xn_dram[d * 128:(d + 1) * 128,
                                  jc * 512:(jc + 1) * 512])
                E.wait("tensor", "LD")
                for d in range(8):
                    E.op("tensor", "matmul", ps3[:, :], d2Ts[:, d, :],
                         xnc[:, d, :], start=(d == 0), stop=(d == 7),
                         sem="TE" if d == 7 else None)
                E.wait("vector", "TE")
                E.op("vector", "tensor_copy", t1s[0:50, :], ps3[0:50, :],
                     sem="VE")
                E.wait("tensor", "VE")
                for ot in range(8):
                    slA = pso[:, (ot % 4) * 512:(ot % 4) * 512 + 512]
                    for nt in range(16):
                        E.op("tensor", "matmul", slA,
                             wos[:, nt, ot * 128:(ot + 1) * 128], ygt[:, nt, :],
                             start=(nt == 0), stop=False)
                    for d in range(8):
                        E.op("tensor", "matmul", slA,
                             wT3s[:, d, ot * 128:(ot + 1) * 128], xnc[:, d, :],
                             start=False, stop=False)
                    E.op("tensor", "matmul", slA,
                         uaTs[0:50, ot, :], t1s[0:50, :],
                         start=False, stop=True, sem="TE")
                    E.dma("sync", xf5[:, :],
                          xq3[ot * 128:(ot + 1) * 128,
                              jc * 512:(jc + 1) * 512])
                    E.wait("vector", "LD")
                    E.wait("vector", "TE")
                    E.op("vector", "tensor_scalar", of5[:, :], slA,
                         bT3s[:, ot:ot + 1], None, op0=ALU.add, sem="VE")
                    E.wait("vector", "VE")
                    E.op("vector", "tensor_add", of5[:, :], of5[:, :],
                         xf5[:, :], sem="VE")
                    E.wait("sync", "VE")
                    E.dma("sync", outfm[ot * 128:(ot + 1) * 128,
                                        jc * 512:(jc + 1) * 512], of5[:, :],
                          sem="ST")
                    E.wait("vector", "ST")
                    E.wait("tensor", "VE")
            E.wait("sync", "ST")
            E.barrier()
            E.flush()
        es2.close()

    return nc


_L3_CACHE = {}


def run_l3(inputs, x3):
    w = inputs
    if "nca" not in _L3_CACHE:
        _L3_CACHE["nca"] = build_l3a()
        _L3_CACHE["ncb"] = build_l3b()
    nca, ncb = _L3_CACHE["nca"], _L3_CACHE["ncb"]

    w_in = np.asarray(w["L3_attn_W_in"], np.float32)
    conv_w = np.asarray(w["L3_attn_conv_w"], np.float32).reshape(2048, 4)
    conv_b = np.asarray(w["L3_attn_conv_b"], np.float32)
    alog_full = np.asarray(w["L3_attn_A_log"], np.float32)
    b_dt = np.asarray(w["L3_attn_b_dt"], np.float32)
    dvec_full = np.asarray(w["L3_attn_D"], np.float32)
    in_maps = []
    for c in range(N_CORES):
        b, nh = c // 2, c % 2
        nsl = slice(1024 * nh, 1024 * (nh + 1))
        perm = np.concatenate([np.arange(1024 * nh, 1024 * (nh + 1)),
                               np.arange(1024 * (1 - nh), 1024 * (2 - nh))])
        m = dict(
            xfm=np.ascontiguousarray(x3[b].T),
            w_xc=np.ascontiguousarray(w_in[:, 0:2048][:, perm]),
            w_z=np.ascontiguousarray(w_in[:, 2048 + 1024 * nh:
                                          2048 + 1024 * (nh + 1)]),
            convw=np.ascontiguousarray(conv_w[perm].reshape(16, 128, 4)),
            convb=np.ascontiguousarray(conv_b[perm].reshape(16, 128, 1)),
            w_x=np.ascontiguousarray(np.asarray(w["L3_attn_W_x"],
                                                np.float32)[perm]),
            w_dt=np.ascontiguousarray(w["L3_attn_W_dt"][:, nsl]),
            bdt=tile_bias(b_dt[nsl]),
            alog=np.ascontiguousarray(alog_full[nsl].reshape(8, 128, 16)),
            dvec=tile_bias(dvec_full[nsl]),
        )
        in_maps.append(m)
    res = run_bass_kernel_spmd(nca, in_maps, core_ids=list(range(N_CORES)))
    _L3_CACHE["resa"] = res.results
    ygz_full = np.empty((4, 2048, T), dtype=BF)
    for c in range(N_CORES):
        b, nh = c // 2, c % 2
        ygz_full[b, 1024 * nh:1024 * (nh + 1)] = res.results[c]["ygz"]

    shared_b = dict(
        wout=np.ascontiguousarray(w["L3_attn_W_out"]),
        wT3=np.ascontiguousarray(np.asarray(w["L3_ffn_W"], np.float32).T),
        bT3=tile_bias(w["L3_ffn_b"]),
        ldT=np.ascontiguousarray(np.asarray(w["L3_ffn_lora_down"],
                                            np.float32).T),
        luT=np.ascontiguousarray(np.asarray(w["L3_ffn_lora_up"],
                                            np.float32).T),
        daux=np.ascontiguousarray(w["L3_ffn_down_aux"]),
        uaT=np.ascontiguousarray(np.asarray(w["L3_ffn_up_aux"],
                                            np.float32).T),
    )
    in_maps_b = []
    for c in range(N_CORES):
        b, h = c // 2, c % 2
        rows = slice(h * 2048, (h + 1) * 2048)
        m = dict(shared_b)
        m["xq3"] = np.ascontiguousarray(x3[b][rows].T)
        m["ygzo"] = np.ascontiguousarray(ygz_full[b][:, rows.start:rows.stop])
        in_maps_b.append(m)
    res_b = run_bass_kernel_spmd(ncb, in_maps_b, core_ids=list(range(N_CORES)))
    _L3_CACHE["resb"] = res_b.results
    out = np.empty_like(x3)
    for c in range(N_CORES):
        b, h = c // 2, c % 2
        out[b, h * 2048:(h + 1) * 2048] = res_b.results[c]["outfm"].T
    return out



_L0_CACHE = {}


def run_l0(inputs, x):
    if "nc" not in _L0_CACHE:
        _L0_CACHE["nc"] = build_l0()
    nc = _L0_CACHE["nc"]

    w = inputs
    shared = dict(
        w_in=np.ascontiguousarray(w["L0_attn_W_in"]),
        b_in_row=np.ascontiguousarray(w["L0_attn_b_in"][None, :]),
        b_in_t=tile_bias(w["L0_attn_b_in"]),
        ln_g_t=tile_bias(w["L0_attn_ln_g"]),
        ln_b_t=tile_bias(w["L0_attn_ln_b"]),
        qkvec=np.stack([tile_bias(w["L0_attn_gamma_q"])[0],
                        tile_bias(w["L0_attn_beta_q"])[0],
                        tile_bias(w["L0_attn_gamma_k"])[0],
                        tile_bias(w["L0_attn_beta_k"])[0]]),
        w_out=np.ascontiguousarray(w["L0_attn_W_out"]),
        b_out_t=tile_bias(w["L0_attn_b_out"]),
        router_wT=np.ascontiguousarray(w["L0_ffn_router_w"].T),
        router_b_t=tile_bias(w["L0_ffn_router_bias"]),
        gate_p=np.ascontiguousarray(w["L0_ffn_gate_proj"]),
        up_p=np.ascontiguousarray(w["L0_ffn_up_proj"]),
        down_p=np.ascontiguousarray(w["L0_ffn_down_proj"]),
        shT_gate=np.ascontiguousarray(w["L0_ffn_sh_gate"].T),
        shT_up=np.ascontiguousarray(w["L0_ffn_sh_up"].T),
        shT_down=np.ascontiguousarray(w["L0_ffn_sh_down"].T),
    )
    in_maps = []
    for c in range(N_CORES):
        b, h = c // 2, c % 2
        xb = x[b]
        own = xb[h * 2048:(h + 1) * 2048]
        oth = xb[(1 - h) * 2048:(2 - h) * 2048]
        xperm = np.concatenate([own, oth], axis=0)
        m = dict(shared)
        m["xfm"] = np.ascontiguousarray(xperm.T)
        in_maps.append(m)
    res = run_bass_kernel_spmd(nc, in_maps, core_ids=list(range(N_CORES)))
    _L0_CACHE["res"] = res.results
    x1 = np.empty_like(x)
    for c in range(N_CORES):
        b, h = c // 2, c % 2
        x1[b, h * 2048:(h + 1) * 2048] = res.results[c]["x1fm"].T
    return x1


def tile_bias(arr):
    arr = np.asarray(arr, np.float32)
    n = arr.shape[0]
    nt = (n + 127) // 128
    out = np.zeros((nt, 128, 1), np.float32)
    out[:, :, 0].reshape(-1)[:n] = arr
    return out


def kernel(**inputs):
    x = np.asarray(inputs["L0_attn_x"], np.float32)
    x1 = run_l0(inputs, x)
    x2 = run_l1(inputs, x1)
    x3 = run_l2(inputs, x2)
    out = run_l3(inputs, x3)
    return np.ascontiguousarray(out.astype(np.float32))
